# version 05
# baseline (speedup 1.0000x reference)
import math
import numpy as np
import ml_dtypes
from contextlib import ExitStack

import concourse.bass as bass
import concourse.mybir as mybir
from concourse.bass_utils import run_bass_kernel_spmd

F32 = mybir.dt.float32
BF16 = mybir.dt.bfloat16
I32 = mybir.dt.int32
U16 = mybir.dt.uint16
AF = mybir.ActivationFunctionType
ALU = mybir.AluOpType

BF16NP = ml_dtypes.bfloat16


ENGINES = ("tensor", "vector", "scalar", "gpsimd", "sync")
DMA_QUEUES = ("sync", "gpsimd", "scalar")


class Op:
    __slots__ = ("engine", "build", "reads", "writes", "seq", "is_dma",
                 "needs_inc", "inc_index", "waits", "tag", "dep_ops")

    def __init__(self, engine, build, reads, writes, seq, is_dma, tag=""):
        self.engine = engine
        self.build = build
        self.reads = reads
        self.writes = writes
        self.seq = seq
        self.is_dma = is_dma
        self.needs_inc = False
        self.inc_index = None
        self.waits = []
        self.tag = tag


class Prog:

    def __init__(self, name="prog"):
        self.nc = bass.Bass("TRN2", target_bir_lowering=False)
        self.stack = ExitStack()
        self.ops = []
        self.last_writer = {}
        self.readers = {}
        self.seq = 0
        self.sb_tensors = {}
        self.ps_tensors = {}
        self.dram = {}
        self._sem_handles = {}
        self.final_dma_ops = []
        self.pfx = ""
        self._alloc_stack = self.stack
        self._last_op_per_engine = {}
        self._barrier_pending = None
        self._barrier_crossed = set()

    def barrier(self):
        self._barrier_pending = dict(self._last_op_per_engine)
        self._barrier_crossed = set()

    def scope(self, pfx):
        from contextlib import contextmanager

        @contextmanager
        def _cm():
            old_stack, old_pfx = self._alloc_stack, self.pfx
            self._alloc_stack = ExitStack()
            self.pfx = pfx
            try:
                yield
            finally:
                self._alloc_stack.close()
                self._alloc_stack, self.pfx = old_stack, old_pfx
                self.barrier()
        return _cm()

    _io = None

    def set_io(self, io):
        self._io = io

    def dram_in(self, name, shape, dtype):
        if self._io is not None:
            assert name in self._io, f"fused io missing input {name}"
            return self._io[name]
        t = self.nc.dram_tensor(name, list(shape), dtype, kind="ExternalInput")
        self.dram[name] = t
        return t.ap()

    def dram_out(self, name, shape, dtype):
        if self._io is not None:
            assert name in self._io, f"fused io missing output {name}"
            return self._io[name]
        t = self.nc.dram_tensor(name, list(shape), dtype, kind="ExternalOutput")
        self.dram[name] = t
        return t.ap()

    def dram_tmp(self, name, shape, dtype):
        t = self.nc.dram_tensor(name, list(shape), dtype, kind="Internal")
        self.dram[name] = t
        return t.ap()

    def sb(self, name, shape, dtype):
        t = self._alloc_stack.enter_context(
            self.nc.sbuf_tensor("sb_" + self.pfx + name, list(shape), dtype))
        self.sb_tensors[self.pfx + name] = t
        return t

    def ps(self, name, shape, dtype=F32):
        t = self._alloc_stack.enter_context(
            self.nc.psum_tensor("ps_" + self.pfx + name, list(shape), dtype))
        self.ps_tensors[self.pfx + name] = t
        return t

    def op(self, engine, build, reads=(), writes=(), tag=""):
        o = Op(engine, build, tuple(reads), tuple(writes), self.seq, is_dma=False, tag=tag)
        self.seq += 1
        self._record(o)
        return o

    def dma(self, queue, build, reads=(), writes=(), tag="", final=False):
        assert queue in DMA_QUEUES
        o = Op(queue, build, tuple(reads), tuple(writes), self.seq, is_dma=True, tag=tag)
        self.seq += 1
        self._record(o)
        if final:
            self.final_dma_ops.append(o)
        return o

    def _record(self, o):
        deps = []
        for r in o.reads:
            w = self.last_writer.get(r)
            if w is not None:
                deps.append(w)
        for w_key in o.writes:
            w = self.last_writer.get(w_key)
            if w is not None:
                deps.append(w)
            deps.extend(self.readers.get(w_key, ()))
        if self._barrier_pending is not None and o.engine not in self._barrier_crossed:
            deps.extend(self._barrier_pending.values())
            self._barrier_crossed.add(o.engine)
        o.dep_ops = [d for d in set(deps) if d is not o]
        for r in o.reads:
            self.readers.setdefault(r, []).append(o)
        for w_key in o.writes:
            self.last_writer[w_key] = o
            self.readers[w_key] = []
        self._last_op_per_engine[o.engine] = o
        self.ops.append(o)

    NDMASEM = 8

    def finalize(self):
        nc = self.nc
        for o in self.ops:
            for d in o.dep_ops:
                if d.engine == "tensor" and o.engine == "tensor":
                    continue
                d.needs_inc = True
        for o in self.final_dma_ops:
            o.needs_inc = True
        counts = {e: 0 for e in ENGINES}
        dma_ord = {e: 0 for e in ENGINES}
        for o in self.ops:
            if o.is_dma:
                k = dma_ord[o.engine]
                dma_ord[o.engine] += 1
                o.inc_index = (k % Prog.NDMASEM, 16 * (k // Prog.NDMASEM + 1))
                o.needs_inc = True
            elif o.needs_inc:
                counts[o.engine] += 1
                o.inc_index = (0, counts[o.engine])
        for o in self.ops:
            wmax = {}
            for d in o.dep_ops:
                if d.engine == "tensor" and o.engine == "tensor":
                    continue
                if d.inc_index is None:
                    continue
                key = (d.engine, d.is_dma, d.inc_index[0])
                wmax[key] = max(wmax.get(key, 0), d.inc_index[1])
            o.waits = sorted(wmax.items())
        fin_waits = {}
        for o in self.final_dma_ops:
            key = (o.engine, True, o.inc_index[0])
            fin_waits[key] = max(fin_waits.get(key, 0), o.inc_index[1])

        sems = {}
        with self.stack:
            for e in ENGINES:
                sems[(e, False, 0)] = self.stack.enter_context(nc.semaphore("sem_" + e))
                if e in DMA_QUEUES:
                    for i in range(Prog.NDMASEM):
                        sems[(e, True, i)] = self.stack.enter_context(
                            nc.semaphore(f"dsem_{e}_{i}"))
            block = self.stack.enter_context(nc.Block())
            per_engine = {e: [o for o in self.ops if o.engine == e] for e in ENGINES}

            def emit_stream(eng_handle, eng_name):
                waited = {}
                for o in per_engine[eng_name]:
                    for (key, v) in o.waits:
                        if waited.get(key, 0) >= v:
                            continue
                        eng_handle.wait_ge(sems[key], v)
                        waited[key] = v
                    ins = o.build(eng_handle)
                    if o.is_dma:
                        ins.then_inc(sems[(o.engine, True, o.inc_index[0])], 16)
                    elif o.needs_inc:
                        ins.then_inc(sems[(o.engine, False, 0)], 1)
                if eng_name == "sync":
                    for (key, v) in sorted(fin_waits.items()):
                        if waited.get(key, 0) < v:
                            eng_handle.wait_ge(sems[key], v)

            for e in ENGINES:
                sec = getattr(block, e)

                def make(e=e):
                    @sec
                    def _(h):
                        emit_stream(h, e)
                make()
        return nc


B, T, D = 4, 2048, 2048
NCORES = 8
TLOC = 1024
LN_EPS = 1e-5


def _bf16(x):
    return np.ascontiguousarray(x.astype(BF16NP))


def emit_ln(p, n_tok=TLOC, d=D, in_dtype=F32):
    x = p.dram_in("x", [n_tok, d], in_dtype)
    xn = p.dram_out("xn", [n_tok, d], BF16)
    nchunk = n_tok // 128
    xt = p.sb("xt", [128, d], F32)
    sq = p.sb("sq", [128, d], F32)
    xnt = p.sb("xnt", [128, d], BF16)
    mean = p.sb("mean", [128, 1], F32)
    var = p.sb("var", [128, 1], F32)
    rstd = p.sb("rstd", [128, 1], F32)
    nmean = p.sb("nmean", [128, 1], F32)
    for c in range(nchunk):
        xk = f"x{c}"
        p.dma("sync", lambda h, c=c: h.dma_start(xt[:], x[c * 128:(c + 1) * 128, :]),
              reads=(), writes=(xk,), tag=f"ld{c}")
        p.op("vector", lambda h: h.reduce_sum(mean[:], xt[:], axis=mybir.AxisListType.X),
             reads=(xk,), writes=("mean",))
        p.op("vector", lambda h: h.tensor_scalar(nmean[:], mean[:], -1.0 / d, None, ALU.mult),
             reads=("mean",), writes=("nmean",))
        p.op("vector", lambda h: h.tensor_scalar(xt[:], xt[:], nmean[:], None, ALU.add),
             reads=(xk, "nmean"), writes=(xk,))
        p.op("scalar", lambda h: h.activation(sq[:], xt[:], AF.Square),
             reads=(xk,), writes=("sq",))
        p.op("vector", lambda h: h.reduce_sum(var[:], sq[:], axis=mybir.AxisListType.X),
             reads=("sq",), writes=("var",))
        p.op("vector", lambda h: h.tensor_scalar(var[:], var[:], 1.0 / d, LN_EPS, ALU.mult, ALU.add),
             reads=("var",), writes=("var",))
        p.op("scalar", lambda h: h.activation(rstd[:], var[:], AF.Ln),
             reads=("var",), writes=("rstd",))
        p.op("scalar", lambda h: h.activation(rstd[:], rstd[:], AF.Exp, scale=-0.5),
             reads=("rstd",), writes=("rstd",))
        p.op("vector", lambda h: h.tensor_scalar(xnt[:], xt[:], rstd[:], None, ALU.mult),
             reads=(xk, "rstd"), writes=("xn",))
        p.dma("sync", lambda h, c=c: h.dma_start(xn[c * 128:(c + 1) * 128, :], xnt[:]),
              reads=("xn",), writes=(), tag=f"st{c}", final=True)
    return None


def build_ln(n_tok=TLOC, d=D, in_dtype=F32):
    p = Prog("ln")
    with p.scope('ln_'):
        emit_ln(p, n_tok, d, in_dtype)
    return p.finalize()


def emit_proj0(p):
    X = mybir.AxisListType.X
    xnf = p.dram_in("xnf", [128, 16 * 2048], BF16)
    wq_d = p.dram_in("wq", [2048, 1024], F32)
    wk_d = p.dram_in("wk", [2048, 256], F32)
    wv_d = p.dram_in("wv", [2048, 256], F32)
    wb_d = p.dram_in("wb", [2048, 16], F32)
    bb_d = p.dram_in("bb", [16, 1], F32)
    qnw_d = p.dram_in("qnw", [1, 64], F32)
    knw_d = p.dram_in("knw", [1, 64], F32)
    cos_d = p.dram_in("cosr", [2048, 256], F32)
    sin_d = p.dram_in("sinr", [2048, 256], F32)
    id_d = p.dram_in("ident", [128, 128], BF16)
    qa_d = p.dram_out("qa", [16, 65, 2048], BF16)
    ka_d = p.dram_out("ka", [4, 66, 2048], BF16)
    v_d = p.dram_out("vv", [2048, 256], BF16)
    beta_d = p.dram_out("beta", [16, 2048], F32)

    xn = p.sb("xn", [128, 16 * 2048], BF16)
    wq = p.sb("wq", [128, 16 * 1024], BF16)
    wk = p.sb("wk", [128, 16 * 256], BF16)
    wv = p.sb("wv", [128, 16 * 256], BF16)
    wb = p.sb("wb", [128, 16 * 16], BF16)
    bbt = p.sb("bbt", [16, 1], F32)
    ident = p.sb("ident", [128, 128], BF16)
    cosr = p.sb("cosr", [128, 16 * 256], F32)
    sinr = p.sb("sinr", [128, 16 * 256], F32)
    qwrow = p.sb("qwrow", [1, 512], F32)
    kwrow = p.sb("kwrow", [1, 256], F32)
    onesc = p.sb("onesc", [1, 128], BF16)
    qwB = p.sb("qwB", [128, 512], F32)
    kwB = p.sb("kwB", [128, 256], F32)

    p.dma("sync", lambda h: h.dma_start(xn[:], xnf), writes=("xn",))
    p.dma("gpsimd", lambda h: h.dma_start(
        wq[:].rearrange("p (c n) -> p c n", c=16), wq_d.rearrange("(c p) n -> p c n", p=128)), writes=("wq",))
    p.dma("gpsimd", lambda h: h.dma_start(
        wk[:].rearrange("p (c n) -> p c n", c=16), wk_d.rearrange("(c p) n -> p c n", p=128)), writes=("wk",))
    p.dma("gpsimd", lambda h: h.dma_start(
        wv[:].rearrange("p (c n) -> p c n", c=16), wv_d.rearrange("(c p) n -> p c n", p=128)), writes=("wv",))
    p.dma("gpsimd", lambda h: h.dma_start(
        wb[:].rearrange("p (c n) -> p c n", c=16), wb_d.rearrange("(c p) n -> p c n", p=128)), writes=("wb",))
    p.dma("gpsimd", lambda h: h.dma_start(bbt[:], bb_d), writes=("bbt",))
    p.dma("sync", lambda h: h.dma_start(ident[:], id_d), writes=("ident",))
    p.dma("sync", lambda h: h.dma_start(
        cosr[:].rearrange("p (c n) -> p c n", c=16), cos_d.rearrange("(c p) n -> p c n", p=128)), writes=("cosr",))
    p.dma("sync", lambda h: h.dma_start(
        sinr[:].rearrange("p (c n) -> p c n", c=16), sin_d.rearrange("(c p) n -> p c n", p=128)), writes=("sinr",))
    for r in range(8):
        p.dma("sync", lambda h, r=r: h.dma_start(qwrow[:, 64 * r:64 * r + 64], qnw_d),
              writes=("qwrow",))
    for r in range(4):
        p.dma("sync", lambda h, r=r: h.dma_start(kwrow[:, 64 * r:64 * r + 64], knw_d),
              writes=("kwrow",))
    p.op("vector", lambda h: h.memset(onesc[:], 1.0), writes=("onesc",))
    p.op("vector", lambda h: h.tensor_scalar(qwrow[:], qwrow[:], 0.125, None, ALU.mult),
         reads=("qwrow",), writes=("qwrow",))
    qwrb = p.sb("qwrb", [1, 512], BF16)
    kwrb = p.sb("kwrb", [1, 256], BF16)
    p.op("vector", lambda h: h.tensor_copy(qwrb[:], qwrow[:]), reads=("qwrow",), writes=("qwrb",))
    p.op("vector", lambda h: h.tensor_copy(kwrb[:], kwrow[:]), reads=("kwrow",), writes=("kwrb",))
    psB = p.ps("psB", [128, 512], F32)
    p.op("tensor", lambda h: h.matmul(psB[:], onesc[:], qwrb[:], start=True, stop=True),
         reads=("onesc", "qwrb"), writes=("psB",))
    p.op("vector", lambda h: h.tensor_copy(qwB[:], psB[:]), reads=("psB",), writes=("qwB",))
    p.op("tensor", lambda h: h.matmul(psB[:, 0:256], onesc[:], kwrb[:], start=True, stop=True),
         reads=("onesc", "kwrb"), writes=("psB",))
    p.op("vector", lambda h: h.tensor_copy(kwB[:], psB[:, 0:256]), reads=("psB",), writes=("kwB",))

    ps_a = p.ps("ps_a", [128, 512], F32)
    ps_t = p.ps("ps_t", [128, 128], BF16)
    sq = p.sb("sqt", [128, 512], F32)
    cur = p.sb("cur", [128, 512], F32)
    ta = p.sb("ta", [128, 256], F32)
    tb = p.sb("tb", [128, 256], F32)
    tc2t = p.sb("tc2", [128, 256], F32)
    td = p.sb("td", [128, 256], F32)
    rst = p.sb("rst", [128, 8], F32)
    nrm = p.sb("nrm", [128, 8], F32)
    mlt = p.sb("mlt", [128, 8], F32)
    aug = p.sb("aug", [128, 8 * 66], BF16)
    tpo = p.sb("tpo", [128, 8 * 128], BF16)
    ktn = p.sb("ktn", [128, 64], BF16)
    tr1 = p.sb("tr1", [64, 128], BF16)
    colm = p.sb("colm", [64, 1], BF16)
    rowm = p.sb("rowm", [1, 64], F32)
    kbar2 = p.sb("kbar2", [1, 4], F32)
    kbarb = p.sb("kbarb", [1, 16], BF16)
    kbarB = p.sb("kbarB", [128, 16], F32)
    vsb = p.sb("vsb", [128, 256], BF16)
    beta_s = p.sb("beta_s", [16, 2048], F32)
    ps_beta = p.ps("ps_beta", [16, 512], F32)

    def rope_views(t, nh, width):
        x1 = bass.AP(t, 0, [[width, 128], [64, nh], [1, 32]])
        x2 = bass.AP(t, 32, [[width, 128], [64, nh], [1, 32]])
        return x1, x2

    def cs_views(nh, tcc):
        c = bass.AP(cosr, tcc * 256, [[16 * 256, 128], [32, nh], [1, 32]])
        s = bass.AP(sinr, tcc * 256, [[16 * 256, 128], [32, nh], [1, 32]])
        return c, s

    def tmp_views(nh):
        return (bass.AP(ta, 0, [[256, 128], [32, nh], [1, 32]]),
                bass.AP(tb, 0, [[256, 128], [32, nh], [1, 32]]),
                bass.AP(tc2t, 0, [[256, 128], [32, nh], [1, 32]]),
                bass.AP(td, 0, [[256, 128], [32, nh], [1, 32]]))

    def rms_rope(nh, tcc, tag, wB, wBkey):
        fw = nh * 64
        p.op("scalar", lambda h: h.activation(sq[:, 0:fw], ps_a[:, 0:fw], AF.Square),
             reads=(tag + "_ps",), writes=(tag + "_sq",))
        for hh in range(nh):
            p.op("vector", lambda h, hh=hh: h.reduce_sum(
                nrm[:, hh:hh + 1], sq[:, 64 * hh:64 * hh + 64], axis=X),
                 reads=(tag + "_sq",), writes=(tag + "_nrm",))
        p.op("vector", lambda h: h.tensor_scalar(
            rst[:, 0:nh], nrm[:, 0:nh], 1.0 / 64, 1e-6, ALU.mult, ALU.add),
             reads=(tag + "_nrm",), writes=(tag + "_rst",))
        p.op("scalar", lambda h: h.activation(rst[:, 0:nh], rst[:, 0:nh], AF.Ln),
             reads=(tag + "_rst",), writes=(tag + "_rst",))
        p.op("scalar", lambda h: h.activation(rst[:, 0:nh], rst[:, 0:nh], AF.Exp, scale=-0.5),
             reads=(tag + "_rst",), writes=(tag + "_rst",))
        for hh in range(nh):
            p.op("vector", lambda h, hh=hh: h.tensor_scalar(
                cur[:, 64 * hh:64 * hh + 64], ps_a[:, 64 * hh:64 * hh + 64],
                rst[:, hh:hh + 1], None, ALU.mult),
                 reads=(tag + "_ps", tag + "_rst"), writes=(tag + "_cur",))
        p.op("vector", lambda h: h.tensor_tensor(cur[:, 0:fw], cur[:, 0:fw], wB[:, 0:fw], ALU.mult),
             reads=(tag + "_cur", wBkey), writes=(tag + "_cur",))
        x1, x2 = rope_views(cur, nh, 512)
        cv, sv = cs_views(nh, tcc)
        tav, tbv, tcv, tdv = tmp_views(nh)
        p.op("vector", lambda h: h.tensor_tensor(tav, x1, cv, ALU.mult),
             reads=(tag + "_cur", "cosr"), writes=(tag + "_ta",))
        p.op("vector", lambda h: h.tensor_tensor(tbv, x1, sv, ALU.mult),
             reads=(tag + "_cur", "sinr"), writes=(tag + "_tb",))
        p.op("vector", lambda h: h.tensor_tensor(tcv, x2, sv, ALU.mult),
             reads=(tag + "_cur", "sinr"), writes=(tag + "_tc",))
        p.op("vector", lambda h: h.tensor_tensor(tdv, x2, cv, ALU.mult),
             reads=(tag + "_cur", "cosr"), writes=(tag + "_td",))
        p.op("vector", lambda h: h.tensor_tensor(x1, tav, tcv, ALU.subtract),
             reads=(tag + "_ta", tag + "_tc"), writes=(tag + "_cur",))
        p.op("vector", lambda h: h.tensor_tensor(x2, tbv, tdv, ALU.add),
             reads=(tag + "_tb", tag + "_td"), writes=(tag + "_cur",))
        p.op("scalar", lambda h: h.activation(sq[:, 0:fw], cur[:, 0:fw], AF.Square),
             reads=(tag + "_cur",), writes=(tag + "_sq",))
        for hh in range(nh):
            p.op("vector", lambda h, hh=hh: h.reduce_sum(
                nrm[:, hh:hh + 1], sq[:, 64 * hh:64 * hh + 64], axis=X),
                 reads=(tag + "_sq",), writes=(tag + "_nrm",))

    p.op("gpsimd", lambda h: h.memset(aug[:], 1.0), writes=("aug_ones",))
    for tcc in range(16):
        for kc in range(16):
            p.op("tensor", lambda h, kc=kc, tcc=tcc: h.matmul(
                ps_a[:, 0:256],
                xn[:, kc * 2048 + tcc * 128: kc * 2048 + tcc * 128 + 128],
                wk[:, kc * 256: kc * 256 + 256],
                start=(kc == 0), stop=(kc == 15)),
                 reads=("xn", "wk"), writes=("k_ps",))
        rms_rope(4, tcc, "k", kwB, "kwB")
        for hh in range(4):
            p.op("vector", lambda h, hh=hh, tcc=tcc: h.tensor_copy(
                ktn[:, hh * 16 + tcc: hh * 16 + tcc + 1], nrm[:, hh:hh + 1]),
                 reads=("k_nrm",), writes=("ktn",))
        for hh in range(4):
            p.op("vector", lambda h, hh=hh: h.tensor_copy(
                aug[:, 66 * hh:66 * hh + 64], cur[:, 64 * hh:64 * hh + 64]),
                 reads=("k_cur", "aug_ones"), writes=("k_aug",))
            p.op("vector", lambda h, hh=hh: h.tensor_copy(
                aug[:, 66 * hh + 65:66 * hh + 66], nrm[:, hh:hh + 1]),
                 reads=("k_nrm",), writes=("k_aug",))
        for hh in range(4):
            p.op("tensor", lambda h, hh=hh: h.transpose(
                ps_t[0:66, :], aug[:, 66 * hh:66 * hh + 66], ident[:]),
                 reads=("k_aug", "ident", "ps_t_free"), writes=("ps_t",))
            p.op("vector", lambda h, hh=hh: h.tensor_copy(
                tpo[0:66, 128 * hh:128 * hh + 128], ps_t[0:66, :]),
                 reads=("ps_t",), writes=("k_tpo", "ps_t_free"))
        p.dma("sync", lambda h, tcc=tcc: h.dma_start(
            ka_d[:, :, tcc * 128:tcc * 128 + 128].transpose([1, 0, 2]),
            bass.AP(tpo, 0, [[8 * 128, 66], [128, 4], [1, 128]])),
              reads=("k_tpo",), tag=f"ka{tcc}", final=True)
    p.op("tensor", lambda h: h.transpose(ps_t[0:64, :], ktn[:], ident[:]),
         reads=("ktn", "ident", "ps_t_free"), writes=("ps_t",))
    p.op("vector", lambda h: h.tensor_copy(tr1[:], ps_t[0:64, :]),
         reads=("ps_t",), writes=("tr1", "ps_t_free"))
    p.op("vector", lambda h: h.reduce_max(colm[:], tr1[:], axis=X),
         reads=("tr1",), writes=("colm",))
    p.op("tensor", lambda h: h.transpose(ps_t[0:1, 0:64], colm[:], ident[0:64, 0:64]),
         reads=("colm", "ident", "ps_t_free"), writes=("ps_t",))
    p.op("vector", lambda h: h.tensor_copy(rowm[:], ps_t[0:1, 0:64]),
         reads=("ps_t",), writes=("rowm", "ps_t_free"))
    for hh in range(4):
        p.op("vector", lambda h, hh=hh: h.reduce_max(
            kbar2[:, hh:hh + 1], rowm[:, 16 * hh:16 * hh + 16], axis=X),
             reads=("rowm",), writes=("kbar2",))
    p.op("scalar", lambda h: h.activation(kbar2[:], kbar2[:], AF.Ln),
         reads=("kbar2",), writes=("kbar2",))
    p.op("scalar", lambda h: h.activation(kbar2[:], kbar2[:], AF.Exp, scale=0.5),
         reads=("kbar2",), writes=("kbar2",))
    for hh in range(16):
        p.op("vector", lambda h, hh=hh: h.tensor_copy(
            kbarb[:, hh:hh + 1], kbar2[:, hh // 4:hh // 4 + 1]),
             reads=("kbar2",), writes=("kbarb",))
    p.op("tensor", lambda h: h.matmul(psB[:, 0:16], onesc[:], kbarb[:], start=True, stop=True),
         reads=("onesc", "kbarb"), writes=("psB",))
    p.op("vector", lambda h: h.tensor_copy(kbarB[:], psB[:, 0:16]), reads=("psB",), writes=("kbarB",))

    for tcc in range(16):
        for kc in range(16):
            p.op("tensor", lambda h, kc=kc, tcc=tcc: h.matmul(
                ps_a[:, 0:256],
                xn[:, kc * 2048 + tcc * 128: kc * 2048 + tcc * 128 + 128],
                wv[:, kc * 256: kc * 256 + 256],
                start=(kc == 0), stop=(kc == 15)),
                 reads=("xn", "wv"), writes=("k_ps",))
        p.op("vector", lambda h: h.tensor_copy(vsb[:], ps_a[:, 0:256]),
             reads=("k_ps",), writes=("vsb",))
        p.dma("sync", lambda h, tcc=tcc: h.dma_start(
            v_d[tcc * 128:tcc * 128 + 128, :], vsb[:]),
              reads=("vsb",), tag=f"v{tcc}", final=True)

    for ft in range(4):
        for kc in range(16):
            p.op("tensor", lambda h, kc=kc, ft=ft: h.matmul(
                ps_beta[:],
                wb[:, kc * 16: kc * 16 + 16],
                xn[:, kc * 2048 + ft * 512: kc * 2048 + ft * 512 + 512],
                start=(kc == 0), stop=(kc == 15)),
                 reads=("xn", "wb"), writes=("b_ps",))
        p.op("vector", lambda h, ft=ft: h.tensor_scalar(
            beta_s[:, ft * 512:ft * 512 + 512], ps_beta[:], bbt[:], None, ALU.add),
             reads=("b_ps", "bbt"), writes=("beta_s",))
    p.dma("sync", lambda h: h.dma_start(beta_d, beta_s[:]),
          reads=("beta_s",), tag="beta", final=True)

    for tcc in range(16):
        for g in range(2):
            for kc in range(16):
                p.op("tensor", lambda h, kc=kc, tcc=tcc, g=g: h.matmul(
                    ps_a[:],
                    xn[:, kc * 2048 + tcc * 128: kc * 2048 + tcc * 128 + 128],
                    wq[:, kc * 1024 + g * 512: kc * 1024 + g * 512 + 512],
                    start=(kc == 0), stop=(kc == 15)),
                     reads=("xn", "wq"), writes=("k_ps",))
            rms_rope(8, tcc, "k", qwB, "qwB")
            p.op("scalar", lambda h: h.activation(mlt[:, 0:8], nrm[:, 0:8], AF.Ln),
                 reads=("k_nrm",), writes=("mlt",))
            p.op("scalar", lambda h: h.activation(mlt[:, 0:8], mlt[:, 0:8], AF.Exp, scale=0.5),
                 reads=("mlt",), writes=("mlt",))
            p.op("vector", lambda h, g=g: h.scalar_tensor_tensor(
                mlt[:, 0:8], mlt[:, 0:8], -1.0, kbarB[:, 8 * g:8 * g + 8], ALU.mult, ALU.mult),
                 reads=("mlt", "kbarB"), writes=("mlt",))
            for hh in range(8):
                p.op("vector", lambda h, hh=hh: h.tensor_copy(
                    aug[:, 66 * hh:66 * hh + 64], cur[:, 64 * hh:64 * hh + 64]),
                     reads=("k_cur",), writes=("k_aug",))
                p.op("vector", lambda h, hh=hh: h.tensor_copy(
                    aug[:, 66 * hh + 64:66 * hh + 65], mlt[:, hh:hh + 1]),
                     reads=("mlt",), writes=("k_aug",))
            for hh in range(8):
                p.op("tensor", lambda h, hh=hh: h.transpose(
                    ps_t[0:65, :], aug[:, 66 * hh:66 * hh + 65], ident[:]),
                     reads=("k_aug", "ident", "ps_t_free"), writes=("ps_t",))
                p.op("vector", lambda h, hh=hh: h.tensor_copy(
                    tpo[0:65, 128 * hh:128 * hh + 128], ps_t[0:65, :]),
                     reads=("ps_t",), writes=("k_tpo", "ps_t_free"))
            p.dma("sync", lambda h, tcc=tcc, g=g: h.dma_start(
                qa_d[8 * g:8 * g + 8, :, tcc * 128:tcc * 128 + 128].transpose([1, 0, 2]),
                bass.AP(tpo, 0, [[8 * 128, 65], [128, 8], [1, 128]])),
                  reads=("k_tpo",), tag=f"qa{tcc}_{g}", final=True)
    return None


def build_proj0():
    p = Prog("proj0")
    with p.scope('pro_'):
        emit_proj0(p)
    return p.finalize()


def rope_tables():
    pos = np.arange(T, dtype=np.float64)
    dim_t = 10000.0 ** (2.0 * np.arange(32, dtype=np.float64) / 64)
    ang = pos[:, None] / dim_t[None, :]
    cos = np.cos(ang).astype(np.float32)
    sin = np.sin(ang).astype(np.float32)
    cosr = np.tile(cos, (1, 8))
    sinr = np.tile(sin, (1, 8))
    return np.ascontiguousarray(cosr), np.ascontiguousarray(sinr)


def emit_attn0(p):
    qa_d = p.dram_in("qa", [16, 65, 2048], BF16)
    ka_d = p.dram_in("ka", [4, 66, 2048], BF16)
    v_d = p.dram_in("vv", [2048, 256], BF16)
    beta_d = p.dram_in("beta", [16, 2048], F32)
    id_d = p.dram_in("ident", [128, 128], BF16)
    idf_d = p.dram_in("identf", [128, 128], F32)
    sm_d = p.dram_in("smask", [128, 128], BF16)
    smt_d = p.dram_in("smaskt", [128, 128], BF16)
    cm_d = p.dram_in("cmask", [128, 128], BF16)
    oo_d = p.dram_out("oo", [16, 64, 2048], BF16)

    identf = p.sb("identf", [128, 128], F32)
    smask = p.sb("smask", [128, 128], BF16)
    smaskt = p.sb("smaskt", [128, 128], BF16)
    cmask = p.sb("cmask", [128, 128], BF16)
    ka = p.sb("ka", [66, 4 * 2048], BF16)
    vsb = p.sb("vsb", [128, 16 * 256], BF16)
    onesr = p.sb("onesr", [1, 128], BF16)
    onescol = p.sb("onescol", [128, 1], BF16)
    ones1 = p.sb("ones1", [1, 1], BF16)
    p.dma("sync", lambda h: h.dma_start(identf[:], idf_d), writes=("identf",))
    p.dma("sync", lambda h: h.dma_start(smask[:], sm_d), writes=("smask",))
    p.dma("sync", lambda h: h.dma_start(smaskt[:], smt_d), writes=("smaskt",))
    p.dma("sync", lambda h: h.dma_start(cmask[:], cm_d), writes=("cmask",))
    p.dma("sync", lambda h: h.dma_start(
        ka[:].rearrange("p (c n) -> p c n", c=4), ka_d.transpose([1, 0, 2])), writes=("ka",))
    p.dma("sync", lambda h: h.dma_start(
        vsb[:].rearrange("p (c n) -> p c n", c=16),
        v_d.rearrange("(c p) n -> p c n", p=128)), writes=("vsb",))
    p.op("vector", lambda h: h.memset(onesr[:], 1.0), writes=("onesr",))
    p.op("vector", lambda h: h.memset(onescol[:], 1.0), writes=("onescol",))
    p.op("vector", lambda h: h.memset(ones1[:], 1.0), writes=("ones1",))

    S = []
    for s in range(2):
        d = {}
        d["qs"] = p.sb(f"qs{s}", [65, 2048], BF16)
        d["brow"] = p.sb(f"brow{s}", [1, 2048], F32)
        d["ubuf"] = p.sb(f"ubuf{s}", [128, 16 * 65], BF16)
        d["o_sb"] = p.sb(f"o_sb{s}", [64, 2048], BF16)
        d["e_sb"] = [p.sb(f"e_sb{s}_{k}", [128, 128], BF16) for k in range(2)]
        d["e_st"] = p.sb(f"e_st{s}", [128, 128], BF16)
        d["e_ca"] = p.sb(f"e_ca{s}", [128, 128], BF16)
        d["e2t"] = p.sb(f"e2t{s}", [128, 128], BF16)
        d["zb"] = [p.sb(f"zb{s}_{k}", [128, 128], BF16) for k in range(2)]
        d["yb"] = [p.sb(f"yb{s}_{k}", [128, 128], BF16) for k in range(2)]
        d["psb"] = p.sb(f"psb{s}", [65, 256], F32)
        d["row"] = p.sb(f"row{s}", [1, 128], F32)
        d["mbsb"] = p.sb(f"mbsb{s}", [1, 128], BF16)
        d["mbscol"] = p.sb(f"mbscol{s}", [128, 1], F32)
        d["wtile"] = p.sb(f"wtile{s}", [128, 64], F32)
        d["tp"] = [p.sb(f"tp{s}_{k}", [128, 64], BF16) for k in range(2)]
        d["screcb"] = p.sb(f"screcb{s}", [1, 128], BF16)
        d["rb"] = p.sb(f"rb{s}", [64, 128], BF16)
        d["ps_P"] = p.ps(f"psP{s}", [65, 256], F32)
        d["ps_e"] = p.ps(f"pse{s}", [128, 128], F32)
        d["ps_u"] = p.ps(f"psu{s}", [128, 64], F32)
        d["ps_m"] = p.ps(f"psm{s}", [128, 128], F32)
        S.append(d)

    class Rec:
        def __init__(self):
            self.q = []

        def op(self, *a, **k):
            self.q.append(lambda: p.op(*a, **k))

        def dma(self, *a, **k):
            self.q.append(lambda: p.dma(*a, **k))

    def emit_block(r, s, hh, i):
        d = S[s]
        kv = hh // 4
        K = lambda nm: f"{nm}_{s}"
        qsl = d["qs"][:, i * 128:i * 128 + 128]
        for j in range(i):
            eb = d["e_sb"][j % 2]
            ek = K(f"esb{j % 2}")
            r.op("tensor", lambda h, j=j, kv=kv, d=d, qsl=qsl: h.matmul(
                d["ps_e"][:], ka[0:65, kv * 2048 + j * 128: kv * 2048 + j * 128 + 128],
                qsl, start=True, stop=True),
                 reads=("ka", K("qs")), writes=(K("pse"),))
            r.op("scalar", lambda h, eb=eb, d=d: h.activation(eb[:], d["ps_e"][:], AF.Exp),
                 reads=(K("pse"),), writes=(ek,))
            r.op("tensor", lambda h, j=j, eb=eb, d=d: h.matmul(
                d["ps_P"][0:65, 0:128], d["ubuf"][:, 65 * j: 65 * j + 65], eb[:],
                start=(j == 0), stop=False),
                 reads=(K("ubuf"), ek), writes=(K("psP"),))
        eb = d["e_sb"][i % 2]
        ek = K(f"esb{i % 2}")
        r.op("tensor", lambda h, i=i, kv=kv, d=d, qsl=qsl: h.matmul(
            d["ps_e"][:], ka[0:65, kv * 2048 + i * 128: kv * 2048 + i * 128 + 128],
            qsl, start=True, stop=True),
             reads=("ka", K("qs")), writes=(K("pse"),))
        r.op("scalar", lambda h, eb=eb, d=d: h.activation(eb[:], d["ps_e"][:], AF.Exp),
             reads=(K("pse"),), writes=(ek,))
        r.op("vector", lambda h, eb=eb, d=d: h.tensor_tensor(d["e_st"][:], eb[:], smask[:], ALU.mult),
             reads=(ek, "smask"), writes=(K("est"),))
        r.op("vector", lambda h, eb=eb, d=d: h.tensor_tensor(d["e_ca"][:], eb[:], cmask[:], ALU.mult),
             reads=(ek, "cmask"), writes=(K("eca"),))
        r.op("tensor", lambda h, i=i, d=d: h.matmul(
            d["ps_P"][64:65, 128:256], onescol[:], d["e_st"][:], start=(i == 0), stop=False),
             reads=("onescol", K("est")), writes=(K("psP"),))
        r.op("vector", lambda h, d=d: h.tensor_copy(d["psb"][:], d["ps_P"][:]),
             reads=(K("psP"),), writes=(K("psb"),))
        if i > 0:
            r.op("vector", lambda h, d=d: h.tensor_tensor(d["row"][:], d["psb"][64:65, 0:128], d["psb"][64:65, 128:256], ALU.add),
                 reads=(K("psb"),), writes=(K("row"),))
        else:
            r.op("vector", lambda h, d=d: h.tensor_copy(d["row"][:], d["psb"][64:65, 128:256]),
                 reads=(K("psb"),), writes=(K("row"),))
        r.op("vector", lambda h, d=d: h.tensor_scalar(d["row"][:], d["row"][:], 1e-20, None, ALU.max),
             reads=(K("row"),), writes=(K("row"),))
        r.op("vector", lambda h, d=d: h.reciprocal(d["row"][:], d["row"][:]),
             reads=(K("row"),), writes=(K("row"),))
        r.op("vector", lambda h, d=d, i=i: h.scalar_tensor_tensor(
            d["row"][:], d["row"][:], -1.0, d["brow"][:, i * 128:i * 128 + 128], ALU.mult, ALU.mult),
             reads=(K("row"), K("brow")), writes=(K("row"),))
        r.op("vector", lambda h, d=d: h.tensor_copy(d["mbsb"][:], d["row"][:]),
             reads=(K("row"),), writes=(K("mbsb"),))
        r.op("tensor", lambda h, d=d: h.matmul(
            d["ps_u"][:, 0:1], d["mbsb"][:], ones1[:], start=True, stop=True),
             reads=(K("mbsb"), "ones1"), writes=(K("psu"),))
        r.op("vector", lambda h, d=d: h.tensor_copy(d["mbscol"][:], d["ps_u"][:, 0:1]),
             reads=(K("psu"),), writes=(K("mbscol"),))
        if i > 0:
            r.op("tensor", lambda h, d=d: h.transpose(
                d["ps_m"][:, 0:64], d["psb"][0:64, 0:128], identf[0:64, 0:64]),
                 reads=(K("psb"), "identf"), writes=(K("psm"),))
            r.op("vector", lambda h, d=d, i=i, kv=kv: h.scalar_tensor_tensor(
                d["wtile"][:], d["ps_m"][:, 0:64], d["mbscol"][:],
                vsb[:, i * 256 + kv * 64: i * 256 + kv * 64 + 64], ALU.mult, ALU.add),
                 reads=(K("psm"), K("mbscol"), "vsb"), writes=(K("wtile"),))
        else:
            r.op("vector", lambda h, d=d, i=i, kv=kv: h.tensor_copy(
                d["wtile"][:], vsb[:, i * 256 + kv * 64: i * 256 + kv * 64 + 64]),
                 reads=("vsb",), writes=(K("wtile"),))
        r.op("vector", lambda h, d=d: h.tensor_copy(d["tp"][0][:], d["wtile"][:]),
             reads=(K("wtile"),), writes=(K("tp0"),))
        r.op("tensor", lambda h, d=d: h.matmul(d["ps_m"][:], onesr[:], d["mbsb"][:],
                                               start=True, stop=True),
             reads=("onesr", K("mbsb"), K("psm")), writes=(K("psm"),))
        r.op("vector", lambda h, d=d: h.tensor_tensor(d["zb"][0][:], d["e_st"][:], d["ps_m"][:], ALU.mult),
             reads=(K("est"), K("psm")), writes=(K("zb0"),))
        r.op("tensor", lambda h, i=i, kv=kv, d=d, qsl=qsl: h.matmul(
            d["ps_e"][:], qsl, ka[0:65, kv * 2048 + i * 128: kv * 2048 + i * 128 + 128],
            start=True, stop=True),
             reads=("ka", K("qs")), writes=(K("pse"),))
        r.op("scalar", lambda h, d=d: h.activation(d["e2t"][:], d["ps_e"][:], AF.Exp),
             reads=(K("pse"),), writes=(K("e2t"),))
        r.op("vector", lambda h, d=d: h.scalar_tensor_tensor(
            d["yb"][0][:], d["e2t"][:], d["mbscol"][:], smaskt[:], ALU.mult, ALU.mult),
             reads=(K("e2t"), K("mbscol"), "smaskt"), writes=(K("yb0"),))
        for lev in range(7):
            zc, zn = d["zb"][lev % 2], d["zb"][(lev + 1) % 2]
            yc, yn = d["yb"][lev % 2], d["yb"][(lev + 1) % 2]
            tc_, tn = d["tp"][lev % 2], d["tp"][(lev + 1) % 2]
            kzc, kzn = K(f"zb{lev % 2}"), K(f"zb{(lev + 1) % 2}")
            kyc, kyn = K(f"yb{lev % 2}"), K(f"yb{(lev + 1) % 2}")
            ktc, ktn = K(f"tp{lev % 2}"), K(f"tp{(lev + 1) % 2}")
            r.op("tensor", lambda h, d=d, zc=zc, tc_=tc_: h.matmul(
                d["ps_u"][:], zc[:], tc_[:], start=True, stop=True),
                 reads=(kzc, ktc), writes=(K("psu"),))
            r.op("vector", lambda h, d=d, tn=tn, tc_=tc_: h.tensor_tensor(
                tn[:], d["ps_u"][:], tc_[:], ALU.add),
                 reads=(K("psu"), ktc), writes=(ktn,))
            if lev < 6:
                r.op("tensor", lambda h, d=d, yc=yc, zc=zc: h.matmul(
                    d["ps_m"][:], yc[:], zc[:], start=True, stop=True),
                     reads=(kyc, kzc, K("psm")), writes=(K("psm"),))
                r.op("scalar", lambda h, d=d, zn=zn: h.copy(zn[:], d["ps_m"][:]),
                     reads=(K("psm"),), writes=(kzn,))
                if lev < 5:
                    r.op("tensor", lambda h, d=d, zc=zc, yc=yc: h.matmul(
                        d["ps_m"][:], zc[:], yc[:], start=True, stop=True),
                         reads=(kzc, kyc, K("psm")), writes=(K("psm"),))
                    r.op("scalar", lambda h, d=d, yn=yn: h.copy(yn[:], d["ps_m"][:]),
                         reads=(K("psm"),), writes=(kyn,))
        r.op("vector", lambda h, d=d, i=i: h.tensor_copy(
            d["ubuf"][:, 65 * i:65 * i + 64], d["tp"][1][:]),
             reads=(K("tp1"),), writes=(K("ubuf"),))
        r.op("tensor", lambda h, d=d, i=i: h.matmul(
            d["ps_P"][0:65, 0:128], d["ubuf"][:, 65 * i:65 * i + 65], d["e_ca"][:],
            start=(i == 0), stop=True),
             reads=(K("ubuf"), K("eca")), writes=(K("psP"),))
        r.op("vector", lambda h, d=d: h.tensor_scalar(d["row"][:], d["ps_P"][64:65, 0:128], 1e-20, None, ALU.max),
             reads=(K("psP"),), writes=(K("row"),))
        r.op("vector", lambda h, d=d: h.reciprocal(d["row"][:], d["row"][:]),
             reads=(K("row"),), writes=(K("row"),))
        r.op("vector", lambda h, d=d: h.tensor_copy(d["screcb"][:], d["row"][:]),
             reads=(K("row"),), writes=(K("screcb"),))
        r.op("tensor", lambda h, d=d: h.matmul(
            d["ps_m"][0:64, :], onesr[:, 0:64], d["screcb"][:], start=True, stop=True),
             reads=("onesr", K("screcb"), K("psm")), writes=(K("psm"),))
        r.op("vector", lambda h, d=d: h.tensor_copy(d["rb"][:], d["ps_m"][0:64, :]),
             reads=(K("psm"),), writes=(K("rb"),))
        r.op("vector", lambda h, d=d, i=i: h.tensor_tensor(
            d["o_sb"][:, i * 128:i * 128 + 128], d["ps_P"][0:64, 0:128], d["rb"][:], ALU.mult),
             reads=(K("psP"), K("rb")), writes=(K("osb"),))

    from itertools import zip_longest
    for hp in range(8):
        for s in range(2):
            hh = 2 * hp + s
            d = S[s]
            p.dma("sync", lambda h, hh=hh, d=d: h.dma_start(d["qs"][:], qa_d[hh]),
                  writes=(f"qs_{s}",), tag=f"qld{hh}")
            p.dma("sync", lambda h, hh=hh, d=d: h.dma_start(d["brow"][:], beta_d[hh:hh + 1, :]),
                  writes=(f"brow_{s}",), tag=f"bld{hh}")
            p.op("gpsimd", lambda h, d=d: h.memset(d["ubuf"][:], 1.0), writes=(f"ubuf_{s}",))
        for i in range(16):
            ra, rb_ = Rec(), Rec()
            emit_block(ra, 0, 2 * hp, i)
            emit_block(rb_, 1, 2 * hp + 1, i)
            for a, b in zip_longest(ra.q, rb_.q):
                if a:
                    a()
                if b:
                    b()
        for s in range(2):
            hh = 2 * hp + s
            d = S[s]

            def oo_store(h, hh=hh, d=d):
                dst = oo_d[hh]
                srcap = d["o_sb"][:]
                if len(dst.shape) == 3:
                    srcap = srcap.rearrange("p (j t) -> p j t", j=dst.shape[1])
                return h.dma_start(dst, srcap)
            p.dma("sync", oo_store, reads=(f"osb_{s}",), writes=("oo_out",),
                  tag=f"oo{hh}", final=True)
    return None


def build_attn0():
    p = Prog("attn0")
    with p.scope('att_'):
        emit_attn0(p)
    return p.finalize()


def emit_wo_res(p, nt=TLOC):
    o_d = p.dram_in("ofm", [128, 16 * nt], BF16)
    wo_d = p.dram_in("wo", [2048, 2048], F32)
    bo_d = p.dram_in("bo", [1, 2048], F32)
    x_d = p.dram_in("xin", [nt, 2048], F32)
    xo_d = p.dram_out("xout", [nt, 2048], F32)
    ofm = p.sb("ofm", [128, 16 * nt], BF16)
    wo = p.sb("wo", [128, 16 * 2048], BF16)
    bo = p.sb("bo", [1, 2048], F32)
    bob = p.sb("bob", [1, 2048], BF16)
    onesr = p.sb("onesr", [1, 128], BF16)
    xt = p.sb("xt", [128, 2048], F32)
    ot = p.sb("ot", [128, 2048], F32)
    boB = p.sb("boB", [128, 2048], F32)
    ps = p.ps("ps", [128, 512], F32)
    p.dma("sync", lambda h: h.dma_start(ofm[:], o_d), writes=("ofm",))
    p.dma("gpsimd", lambda h: h.dma_start(
        wo[:].rearrange("p (c n) -> p c n", c=16), wo_d.rearrange("(c p) n -> p c n", p=128)),
        writes=("wo",))
    p.dma("sync", lambda h: h.dma_start(bo[:], bo_d), writes=("bo",))
    p.op("vector", lambda h: h.memset(onesr[:], 1.0), writes=("onesr",))
    p.op("vector", lambda h: h.tensor_copy(bob[:], bo[:]), reads=("bo",), writes=("bob",))
    for f in range(4):
        p.op("tensor", lambda h, f=f: h.matmul(
            ps[:], onesr[:], bob[:, f * 512:f * 512 + 512], start=True, stop=True),
             reads=("onesr", "bob"), writes=("ps",))
        p.op("vector", lambda h, f=f: h.tensor_copy(boB[:, f * 512:f * 512 + 512], ps[:]),
             reads=("ps",), writes=("boB",))
    ntc = nt // 128
    for tc in range(ntc):
        p.dma("sync", lambda h, tc=tc: h.dma_start(xt[:], x_d[tc * 128:tc * 128 + 128, :]),
              writes=("xt",))
        for f in range(4):
            for kc in range(16):
                p.op("tensor", lambda h, tc=tc, f=f, kc=kc: h.matmul(
                    ps[:], ofm[:, kc * nt + tc * 128: kc * nt + tc * 128 + 128],
                    wo[:, kc * 2048 + f * 512: kc * 2048 + f * 512 + 512],
                    start=(kc == 0), stop=(kc == 15)),
                     reads=("ofm", "wo"), writes=("ps",))
            p.op("vector", lambda h, f=f: h.tensor_tensor(
                ot[:, f * 512:f * 512 + 512], ps[:], boB[:, f * 512:f * 512 + 512], ALU.add),
                 reads=("ps", "boB"), writes=("ot",))
        p.op("vector", lambda h: h.tensor_tensor(ot[:], ot[:], xt[:], ALU.add),
             reads=("ot", "xt"), writes=("ot",))
        p.dma("sync", lambda h, tc=tc: h.dma_start(xo_d[tc * 128:tc * 128 + 128, :], ot[:]),
              reads=("ot",), tag=f"st{tc}", final=True)
    return None


def build_wo_res(nt=TLOC):
    p = Prog("wo_res")
    with p.scope('wo__'):
        emit_wo_res(p, nt)
    return p.finalize()


def emit_moe0(p, ntok=2048, nexp=8, nsh=1):
    X = mybir.AxisListType.X
    ntc = ntok // 128
    nft = ntok // 512
    xnf_d = p.dram_in("xnf", [128, 16 * ntok], BF16)
    gwt_d = p.dram_in("gwt", [2048, 16], F32)
    gb_d = p.dram_in("gb", [1, 16], F32)
    w1t_d = p.dram_in("w1t", [nexp, 2048, 1408], F32)
    w3t_d = p.dram_in("w3t", [nexp, 2048, 1408], F32)
    w2t_d = p.dram_in("w2t", [nexp, 1408, 2048], F32)
    b1_d = p.dram_in("b1", [128, nexp * 11], F32)
    b3_d = p.dram_in("b3", [128, nexp * 11], F32)
    b2_d = p.dram_in("b2", [nexp, 2048], F32)
    ws1t_d = p.dram_in("ws1t", [2048, nsh * 1408], F32)
    ws3t_d = p.dram_in("ws3t", [2048, nsh * 1408], F32)
    ws2t_d = p.dram_in("ws2t", [nsh * 1408, 2048], F32)
    bs1_d = p.dram_in("bs1", [128, nsh * 11], F32)
    bs3_d = p.dram_in("bs3", [128, nsh * 11], F32)
    id_d = p.dram_in("ident", [128, 128], BF16)
    y_d = p.dram_out("yy", [ntok, 2048], F32)

    xn = p.sb("xn", [128, 16 * ntok], BF16)
    gwt = p.sb("gwt", [128, 16 * 16], BF16)
    gbr = p.sb("gbr", [1, 16], F32)
    gbrb = p.sb("gbrb", [1, 16], BF16)
    ident = p.sb("ident", [128, 128], BF16)
    onesr = p.sb("onesr", [1, 128], BF16)
    probs = p.sb("probs", [128, 16], F32)
    biased = p.sb("biased", [128, 16], F32)
    scr = p.sb("scr", [128, 16], F32)
    mcol = p.sb("mcol", [128, 1], F32)
    t6 = p.sb("t6", [128, 1], F32)
    ind = p.sb("ind", [128, 16], F32)
    cwtm = p.sb("cwtm", [128, 16], F32)
    cw8 = p.sb("cw8", [128, nexp], BF16)
    cwfm = p.sb("cwfm", [nexp, ntok], F32)
    cwfmb = p.sb("cwfmb", [nexp, ntok], BF16)
    sel = p.sb("sel", [nexp, 128], BF16)
    b2sb = p.sb("b2sb", [nexp, 2048], BF16)
    b1sb = p.sb("b1sb", [128, nexp * 11], F32)
    b3sb = p.sb("b3sb", [128, nexp * 11], F32)
    bs1sb = p.sb("bs1sb", [128, nsh * 11], F32)
    bs3sb = p.sb("bs3sb", [128, nsh * 11], F32)
    w13 = [p.sb(f"w13_{i}", [128, 16 * 128], BF16) for i in range(4)]
    w2t = [p.sb("w2t_0", [128, 11 * 2048], BF16)]
    gt = [p.sb(f"gt{i}", [128, 512], F32) for i in range(2)]
    ut = [p.sb(f"ut{i}", [128, 512], F32) for i in range(2)]
    hbuf = p.sb("hbuf", [128, 11 * ntok], BF16)
    cwB = p.sb("cwB", [128, ntok], BF16)
    yst = [p.sb(f"yst{i}", [128, 512], F32) for i in range(3)]
    ps_g = [p.ps(f"ps_g{i}", [128, 512], F32) for i in range(2)]
    ps_u = [p.ps(f"ps_u{i}", [128, 512], F32) for i in range(2)]
    ps_y = [p.ps(f"ps_y{i}", [128, 512], F32) for i in range(2)]
    ps_r = p.ps("ps_r", [128, 16], F32)
    ps_t = p.ps("ps_t", [128, 128], BF16)

    p.dma("sync", lambda h: h.dma_start(xn[:], xnf_d), writes=("xn",))
    p.dma("gpsimd", lambda h: h.dma_start(
        gwt[:].rearrange("p (c n) -> p c n", c=16), gwt_d.rearrange("(c p) n -> p c n", p=128)),
        writes=("gwt",))
    p.dma("sync", lambda h: h.dma_start(gbr[:], gb_d), writes=("gbr",))
    p.dma("sync", lambda h: h.dma_start(ident[:], id_d), writes=("ident",))
    p.dma("sync", lambda h: h.dma_start(b1sb[:], b1_d), writes=("b1sb",))
    p.dma("sync", lambda h: h.dma_start(b3sb[:], b3_d), writes=("b3sb",))
    p.dma("sync", lambda h: h.dma_start(bs1sb[:], bs1_d), writes=("bs1sb",))
    p.dma("sync", lambda h: h.dma_start(bs3sb[:], bs3_d), writes=("bs3sb",))
    p.dma("gpsimd", lambda h: h.dma_start(b2sb[:], b2_d), writes=("b2sb",))
    p.op("vector", lambda h: h.memset(onesr[:], 1.0), writes=("onesr",))
    p.op("vector", lambda h: h.tensor_copy(gbrb[:], gbr[:]), reads=("gbr",), writes=("gbrb",))

    gbB = p.sb("gbB", [128, 16], F32)
    p.op("tensor", lambda h: h.matmul(ps_r[:], onesr[:], gbrb[:], start=True, stop=True),
         reads=("onesr", "gbrb"), writes=("ps_r",))
    p.op("vector", lambda h: h.tensor_copy(gbB[:], ps_r[:]), reads=("ps_r",), writes=("gbB",))
    for tc in range(ntc):
        for kc in range(16):
            p.op("tensor", lambda h, tc=tc, kc=kc: h.matmul(
                ps_r[:], xn[:, kc * ntok + tc * 128: kc * ntok + tc * 128 + 128],
                gwt[:, kc * 16: kc * 16 + 16], start=(kc == 0), stop=(kc == 15)),
                 reads=("xn", "gwt"), writes=("ps_r",))
        p.op("vector", lambda h: h.reduce_max(mcol[:], ps_r[:], axis=X, negate=True),
             reads=("ps_r",), writes=("mcol",))
        p.op("scalar", lambda h: h.activation(probs[:], ps_r[:], AF.Exp, bias=mcol[:]),
             reads=("ps_r", "mcol"), writes=("probs",))
        p.op("vector", lambda h: h.reduce_sum(mcol[:], probs[:], axis=X),
             reads=("probs",), writes=("mcol",))
        p.op("vector", lambda h: h.reciprocal(mcol[:], mcol[:]), reads=("mcol",), writes=("mcol",))
        p.op("vector", lambda h: h.tensor_scalar(probs[:], probs[:], mcol[:], None, ALU.mult),
             reads=("probs", "mcol"), writes=("probs",))
        p.op("vector", lambda h: h.tensor_tensor(biased[:], probs[:], gbB[:], ALU.add),
             reads=("probs", "gbB"), writes=("biased",))
        p.op("vector", lambda h: h.tensor_copy(scr[:], biased[:]), reads=("biased",), writes=("scr",))
        for it in range(5):
            p.op("vector", lambda h: h.reduce_max(t6[:], scr[:], axis=X),
                 reads=("scr",), writes=("t6",))
            p.op("vector", lambda h: h.tensor_scalar(ind[:], scr[:], t6[:], None, ALU.is_ge),
                 reads=("scr", "t6"), writes=("ind",))
            p.op("vector", lambda h: h.scalar_tensor_tensor(scr[:], ind[:], -1e30, scr[:], ALU.mult, ALU.add),
                 reads=("ind", "scr"), writes=("scr",))
        p.op("vector", lambda h: h.reduce_max(t6[:], scr[:], axis=X),
             reads=("scr",), writes=("t6",))
        p.op("vector", lambda h: h.tensor_scalar(ind[:], biased[:], t6[:], None, ALU.is_ge),
             reads=("biased", "t6"), writes=("ind",))
        p.op("vector", lambda h: h.tensor_tensor(cwtm[:], probs[:], ind[:], ALU.mult),
             reads=("probs", "ind"), writes=("cwtm",))
        p.op("vector", lambda h: h.tensor_copy(cw8[:], cwtm[:, 0:nexp]),
             reads=("cwtm",), writes=("cw8",))
        p.op("tensor", lambda h: h.transpose(ps_t[0:nexp, :], cw8[:], ident[:]),
             reads=("cw8", "ident"), writes=("ps_t",))
        p.op("vector", lambda h, tc=tc: h.tensor_copy(cwfm[:, tc * 128:tc * 128 + 128], ps_t[0:nexp, 0:128]),
             reads=("ps_t",), writes=("cwfm",))
    p.op("vector", lambda h: h.tensor_copy(cwfmb[:], cwfm[:]), reads=("cwfm",), writes=("cwfmb",))

    def expert_pass(e, w1slc, w3slc, w2slc, b1col, b3col, scale_cw, first):
        wkey = "w2buf"
        p.dma("gpsimd", lambda h: h.dma_start(
            w2t[0][:].rearrange("p (c n) -> p c n", c=11), w2slc), writes=(wkey,),
              tag=f"w2ld{e}")
        for hc in range(11):
            bkey = f"w13_{(hc % 2) * 2}", f"w13_{(hc % 2) * 2 + 1}"
            p.dma("gpsimd", lambda h, hc=hc: h.dma_start(
                w13[(hc % 2) * 2][:].rearrange("p (c n) -> p c n", c=16), w1slc(hc)),
                  writes=(bkey[0],), tag=f"w1ld{e}_{hc}")
            p.dma("gpsimd", lambda h, hc=hc: h.dma_start(
                w13[(hc % 2) * 2 + 1][:].rearrange("p (c n) -> p c n", c=16), w3slc(hc)),
                  writes=(bkey[1],), tag=f"w3ld{e}_{hc}")
            for ft in range(nft):
                rr = (hc * nft + ft) % 2
                psg, psu2 = ps_g[rr], ps_u[rr]
                gtt, utt = gt[rr], ut[rr]
                kg, ku, kgt, kut = f"ps_g{rr}", f"ps_u{rr}", f"gt{rr}", f"ut{rr}"
                for kc in range(16):
                    p.op("tensor", lambda h, hc=hc, ft=ft, kc=kc, psg=psg: h.matmul(
                        psg[:], w13[(hc % 2) * 2][:, kc * 128: kc * 128 + 128],
                        xn[:, kc * ntok + ft * 512: kc * ntok + ft * 512 + 512],
                        start=(kc == 0), stop=(kc == 15)),
                         reads=(bkey[0], "xn"), writes=(kg,))
                for kc in range(16):
                    p.op("tensor", lambda h, hc=hc, ft=ft, kc=kc, psu2=psu2: h.matmul(
                        psu2[:], w13[(hc % 2) * 2 + 1][:, kc * 128: kc * 128 + 128],
                        xn[:, kc * ntok + ft * 512: kc * ntok + ft * 512 + 512],
                        start=(kc == 0), stop=(kc == 15)),
                         reads=(bkey[1], "xn"), writes=(ku,))
                p.op("scalar", lambda h, b1col=b1col, hc=hc, psg=psg, gtt=gtt: h.activation(
                    gtt[:], psg[:], AF.Silu, bias=b1col(hc)),
                     reads=(kg, "b1sb"), writes=(kgt,))
                if scale_cw:
                    p.op("vector", lambda h, b3col=b3col, hc=hc, ft=ft, psu2=psu2, utt=utt: h.scalar_tensor_tensor(
                        utt[:], psu2[:], b3col(hc), cwB[:, ft * 512:ft * 512 + 512],
                        ALU.add, ALU.mult),
                         reads=(ku, "b3sb", "cwB"), writes=(kut,))
                else:
                    p.op("vector", lambda h, b3col=b3col, hc=hc, psu2=psu2, utt=utt: h.tensor_scalar(
                        utt[:], psu2[:], b3col(hc), None, ALU.add),
                         reads=(ku, "b3sb"), writes=(kut,))
                p.op("vector", lambda h, hc=hc, ft=ft, gtt=gtt, utt=utt: h.tensor_tensor(
                    hbuf[:, hc * ntok + ft * 512: hc * ntok + ft * 512 + 512],
                    gtt[:], utt[:], ALU.mult),
                     reads=(kgt, kut), writes=("hbuf",))
        for tc in range(ntc):
            for mf in range(4):
                ri = (tc * 4 + mf)
                psy, ky = ps_y[ri % 2], f"ps_y{ri % 2}"
                ys, kys = yst[ri % 3], f"yst{ri % 3}"
                for hc in range(11):
                    p.op("tensor", lambda h, tc=tc, mf=mf, hc=hc, psy=psy: h.matmul(
                        psy[:], hbuf[:, hc * ntok + tc * 128: hc * ntok + tc * 128 + 128],
                        w2t[0][:, hc * 2048 + mf * 512: hc * 2048 + mf * 512 + 512],
                        start=(hc == 0), stop=(hc == 10 and not first)),
                         reads=("hbuf", wkey), writes=(ky,))
                if first:
                    p.op("tensor", lambda h, tc=tc, mf=mf, psy=psy: h.matmul(
                        psy[:], cwfmb[:, tc * 128:tc * 128 + 128],
                        b2sb[:, mf * 512:mf * 512 + 512], start=False, stop=True),
                         reads=("cwfmb", "b2sb"), writes=(ky,))
                p.op("vector", lambda h, psy=psy, ys=ys: h.tensor_copy(ys[:], psy[:]),
                     reads=(ky,), writes=(kys,))
                p.dma("gpsimd", lambda h, tc=tc, mf=mf, first=first, ys=ys: h.dma_start(
                    y_d[tc * 128:tc * 128 + 128, mf * 512:mf * 512 + 512], ys[:],
                    accum_op=(ALU.bypass if first else ALU.add)),
                      reads=(kys,), writes=(f"ydr_{tc}_{mf}",),
                      tag=f"yacc{e}_{tc}_{mf}", final=True)

    for e in range(nexp):
        p.op("vector", lambda h: h.memset(sel[:], 0.0), writes=("sel",))
        p.dma("sync", lambda h, e=e: h.dma_start(sel[e:e + 1, :], onesr[:]),
              reads=("onesr",), writes=("sel",))
        for ft in range(nft):
            p.op("tensor", lambda h, e=e, ft=ft: h.matmul(
                ps_y[0][:], sel[:], cwfmb[:, ft * 512:ft * 512 + 512],
                start=True, stop=True),
                 reads=("sel", "cwfmb", "ps_y0"), writes=("ps_y0",))
            p.op("vector", lambda h, ft=ft: h.tensor_copy(cwB[:, ft * 512:ft * 512 + 512], ps_y[0][:]),
                 reads=("ps_y0",), writes=("cwB",))
        expert_pass(
            e,
            w1slc=lambda hc, e=e: w1t_d[e].rearrange("(c p) n -> p c n", p=128)[:, :, hc * 128:hc * 128 + 128],
            w3slc=lambda hc, e=e: w3t_d[e].rearrange("(c p) n -> p c n", p=128)[:, :, hc * 128:hc * 128 + 128],
            w2slc=w2t_d[e].rearrange("(c p) n -> p c n", p=128),
            b1col=lambda hc, e=e: b1sb[:, e * 11 + hc: e * 11 + hc + 1],
            b3col=lambda hc, e=e: b3sb[:, e * 11 + hc: e * 11 + hc + 1],
            scale_cw=True, first=(e == 0))
    for s in range(nsh):
        expert_pass(
            nexp + s,
            w1slc=lambda hc, s=s: ws1t_d.rearrange("(c p) n -> p c n", p=128)[:, :, s * 1408 + hc * 128:s * 1408 + hc * 128 + 128],
            w3slc=lambda hc, s=s: ws3t_d.rearrange("(c p) n -> p c n", p=128)[:, :, s * 1408 + hc * 128:s * 1408 + hc * 128 + 128],
            w2slc=ws2t_d[s * 1408:(s + 1) * 1408, :].rearrange("(c p) n -> p c n", p=128),
            b1col=lambda hc, s=s: bs1sb[:, s * 11 + hc:s * 11 + hc + 1],
            b3col=lambda hc, s=s: bs3sb[:, s * 11 + hc:s * 11 + hc + 1],
            scale_cw=False, first=False)
    return None


def build_moe0():
    p = Prog("moe0")
    with p.scope('moe_'):
        emit_moe0(p)
    return p.finalize()


def emit_combine_ln(p, nt=TLOC, d=D, two_y=True):
    X = mybir.AxisListType.X
    xa_d = p.dram_in("xattn", [nt, d], F32)
    y0_d = p.dram_in("y0", [nt, d], F32)
    y1_d = p.dram_in("y1", [nt, d], F32) if two_y else None
    bs2_d = p.dram_in("bs2", [1, d], F32)
    x1_d = p.dram_out("x1", [nt, d], F32)
    xn_d = p.dram_out("xn", [nt, d], BF16)
    xt = p.sb("xt", [128, d], F32)
    y0t = p.sb("y0t", [128, d], F32)
    y1t = p.sb("y1t", [128, d], F32)
    sq = p.sb("sq", [128, d], F32)
    xnt = p.sb("xnt", [128, d], BF16)
    bs2 = p.sb("bs2", [1, d], F32)
    bs2b = p.sb("bs2b", [1, d], BF16)
    bs2B = p.sb("bs2B", [128, d], F32)
    onesr = p.sb("onesr", [1, 128], BF16)
    mean = p.sb("mean", [128, 1], F32)
    var = p.sb("var", [128, 1], F32)
    rstd = p.sb("rstd", [128, 1], F32)
    nmean = p.sb("nmean", [128, 1], F32)
    ps = p.ps("ps", [128, 512], F32)
    p.dma("sync", lambda h: h.dma_start(bs2[:], bs2_d), writes=("bs2",))
    p.op("vector", lambda h: h.memset(onesr[:], 1.0), writes=("onesr",))
    p.op("vector", lambda h: h.tensor_copy(bs2b[:], bs2[:]), reads=("bs2",), writes=("bs2b",))
    for f in range(4):
        p.op("tensor", lambda h, f=f: h.matmul(
            ps[:], onesr[:], bs2b[:, f * 512:f * 512 + 512], start=True, stop=True),
             reads=("onesr", "bs2b"), writes=("ps",))
        p.op("vector", lambda h, f=f: h.tensor_copy(bs2B[:, f * 512:f * 512 + 512], ps[:]),
             reads=("ps",), writes=("bs2B",))
    for c in range(nt // 128):
        sl = slice(c * 128, c * 128 + 128)
        p.dma("sync", lambda h, sl=sl: h.dma_start(xt[:], xa_d[sl, :]), writes=("xt",))
        p.dma("sync", lambda h, sl=sl: h.dma_start(y0t[:], y0_d[sl, :]), writes=("y0t",))
        if two_y:
            p.dma("sync", lambda h, sl=sl: h.dma_start(y1t[:], y1_d[sl, :]), writes=("y1t",))
            p.op("vector", lambda h: h.tensor_tensor(y0t[:], y0t[:], y1t[:], ALU.add),
                 reads=("y0t", "y1t"), writes=("y0t",))
        p.op("vector", lambda h: h.tensor_tensor(y0t[:], y0t[:], bs2B[:], ALU.add),
             reads=("y0t", "bs2B"), writes=("y0t",))
        p.op("vector", lambda h: h.tensor_tensor(xt[:], xt[:], y0t[:], ALU.add),
             reads=("xt", "y0t"), writes=("xt",))
        p.dma("sync", lambda h, sl=sl: h.dma_start(x1_d[sl, :], xt[:]),
              reads=("xt",), tag=f"x1st{c}", final=True)
        p.op("vector", lambda h: h.reduce_sum(mean[:], xt[:], axis=X),
             reads=("xt",), writes=("mean",))
        p.op("vector", lambda h: h.tensor_scalar(nmean[:], mean[:], -1.0 / d, None, ALU.mult),
             reads=("mean",), writes=("nmean",))
        p.op("vector", lambda h: h.tensor_scalar(xt[:], xt[:], nmean[:], None, ALU.add),
             reads=("xt", "nmean"), writes=("xt",))
        p.op("scalar", lambda h: h.activation(sq[:], xt[:], AF.Square),
             reads=("xt",), writes=("sq",))
        p.op("vector", lambda h: h.reduce_sum(var[:], sq[:], axis=X),
             reads=("sq",), writes=("var",))
        p.op("vector", lambda h: h.tensor_scalar(var[:], var[:], 1.0 / d, LN_EPS, ALU.mult, ALU.add),
             reads=("var",), writes=("var",))
        p.op("scalar", lambda h: h.activation(rstd[:], var[:], AF.Ln),
             reads=("var",), writes=("rstd",))
        p.op("scalar", lambda h: h.activation(rstd[:], rstd[:], AF.Exp, scale=-0.5),
             reads=("rstd",), writes=("rstd",))
        p.op("vector", lambda h: h.tensor_scalar(xnt[:], xt[:], rstd[:], None, ALU.mult),
             reads=("xt", "rstd"), writes=("xnt",))
        p.dma("sync", lambda h, sl=sl: h.dma_start(xn_d[sl, :], xnt[:]),
              reads=("xnt",), tag=f"xnst{c}", final=True)
    return None


def build_combine_ln(nt=TLOC, d=D):
    p = Prog("combine_ln")
    with p.scope('com_'):
        emit_combine_ln(p, nt, d)
    return p.finalize()


def emit_proj1(p):
    xnf_d = p.dram_in("xnf", [128, 16 * 2048], BF16)
    wq_d = p.dram_in("wq", [2048, 1024], F32)
    wk_d = p.dram_in("wk", [2048, 1024], F32)
    wv_d = p.dram_in("wv", [2048, 1024], F32)
    bq_d = p.dram_in("bq", [128, 8], F32)
    bk_d = p.dram_in("bk", [128, 8], F32)
    bv_d = p.dram_in("bv", [1, 1024], F32)
    id_d = p.dram_in("ident", [128, 128], BF16)
    qf_d = p.dram_out("qf", [8, 128, 2048], BF16)
    kff_d = p.dram_out("kff", [8, 128, 2048], BF16)
    ktm_d = p.dram_out("ktm", [2048, 1024], BF16)
    v_d = p.dram_out("vtm", [2048, 1024], BF16)

    xn = p.sb("xn", [128, 16 * 2048], BF16)
    wq = p.sb("wq", [128, 16 * 1024], BF16)
    wk = p.sb("wk", [128, 16 * 1024], BF16)
    wv = p.sb("wv", [128, 16 * 1024], BF16)
    bqc = p.sb("bqc", [128, 8], F32)
    bkc = p.sb("bkc", [128, 8], F32)
    bvr = p.sb("bvr", [1, 1024], F32)
    bvrb = p.sb("bvrb", [1, 1024], BF16)
    ident = p.sb("ident", [128, 128], BF16)
    onesr = p.sb("onesr", [1, 128], BF16)
    mn = p.sb("mn", [128, 512], F32)
    mx = p.sb("mx", [128, 512], F32)
    ex = p.sb("ex", [128, 512], F32)
    ob = p.sb("ob", [128, 512], BF16)
    vtile = p.sb("vtile", [128, 1024], BF16)
    ps_a = p.ps("ps_a", [128, 512], F32)
    ps_t = p.ps("ps_t", [128, 128], BF16)

    p.dma("sync", lambda h: h.dma_start(xn[:], xnf_d), writes=("xn",))
    for nm, buf, dr in (("qw", wq, wq_d), ("kw", wk, wk_d), ("vw", wv, wv_d)):
        p.dma("gpsimd", lambda h, buf=buf, dr=dr: h.dma_start(
            buf[:].rearrange("p (c n) -> p c n", c=16), dr.rearrange("(c p) n -> p c n", p=128)),
            writes=(nm,))
    p.dma("sync", lambda h: h.dma_start(bqc[:], bq_d), writes=("qb",))
    p.dma("sync", lambda h: h.dma_start(bkc[:], bk_d), writes=("kb",))
    p.dma("sync", lambda h: h.dma_start(bvr[:], bv_d), writes=("bvr",))
    p.dma("sync", lambda h: h.dma_start(ident[:], id_d), writes=("ident",))
    p.op("vector", lambda h: h.memset(onesr[:], 1.0), writes=("onesr",))
    p.op("vector", lambda h: h.tensor_copy(bvrb[:], bvr[:]), reads=("bvr",), writes=("bvrb",))

    for nm, wbuf, bcol, out_d in (("q", wq, bqc, qf_d), ("k", wk, bkc, kff_d)):
        for hh in range(8):
            for ft in range(4):
                for kc in range(16):
                    p.op("tensor", lambda h, hh=hh, ft=ft, kc=kc, wbuf=wbuf: h.matmul(
                        ps_a[:], wbuf[:, kc * 1024 + hh * 128: kc * 1024 + hh * 128 + 128],
                        xn[:, kc * 2048 + ft * 512: kc * 2048 + ft * 512 + 512],
                        start=(kc == 0), stop=(kc == 15)),
                         reads=(nm + "w", "xn"), writes=("ps_a",))
                p.op("vector", lambda h, hh=hh: h.tensor_scalar(
                    mn[:], ps_a[:], bcol[:, hh:hh + 1], 0.0, ALU.add, ALU.min),
                     reads=("ps_a", nm + "b"), writes=("mn",))
                p.op("vector", lambda h, hh=hh: h.tensor_scalar(
                    mx[:], ps_a[:], bcol[:, hh:hh + 1], 0.0, ALU.add, ALU.max),
                     reads=("ps_a", nm + "b"), writes=("mx",))
                p.op("scalar", lambda h: h.activation(ex[:], mn[:], AF.Exp),
                     reads=("mn",), writes=("ex",))
                p.op("vector", lambda h: h.tensor_tensor(ob[:], ex[:], mx[:], ALU.add),
                     reads=("ex", "mx"), writes=("ob",))
                p.dma("sync", lambda h, hh=hh, ft=ft, out_d=out_d: h.dma_start(
                    out_d[hh][:, ft * 512:ft * 512 + 512], ob[:]),
                      reads=("ob",), writes=(f"{nm}outd{hh}",),
                      tag=f"{nm}st{hh}_{ft}", final=True)
    kfl = p.sb("kfl", [128, 2048], BF16)
    ttile = p.sb("ttile", [128, 128], BF16)
    for hh in range(8):
        p.dma("sync", lambda h, hh=hh: h.dma_start(kfl[:], kff_d[hh]),
              reads=(f"koutd{hh}",), writes=("kfl",), tag=f"krl{hh}")
        for tc in range(16):
            p.op("tensor", lambda h, tc=tc: h.transpose(
                ps_t[:], kfl[:, tc * 128:tc * 128 + 128], ident[:]),
                 reads=("kfl", "ident"), writes=("ps_t",))
            p.op("vector", lambda h, tc=tc: h.tensor_copy(ttile[:], ps_t[:]),
                 reads=("ps_t",), writes=("ttile",))
            p.dma("sync", lambda h, hh=hh, tc=tc: h.dma_start(
                ktm_d[tc * 128:tc * 128 + 128, hh * 128:hh * 128 + 128], ttile[:]),
                  reads=("ttile",), tag=f"ktmst{hh}_{tc}", final=True)
    for tc in range(16):
        for ft in range(2):
            for kc in range(16):
                p.op("tensor", lambda h, tc=tc, ft=ft, kc=kc: h.matmul(
                    ps_a[:], xn[:, kc * 2048 + tc * 128: kc * 2048 + tc * 128 + 128],
                    wv[:, kc * 1024 + ft * 512: kc * 1024 + ft * 512 + 512],
                    start=(kc == 0), stop=False),
                     reads=("vw", "xn"), writes=("ps_a",))
            p.op("tensor", lambda h, tc=tc, ft=ft: h.matmul(
                ps_a[:], onesr[:], bvrb[:, ft * 512:ft * 512 + 512], start=False, stop=True),
                 reads=("onesr", "bvrb"), writes=("ps_a",))
            p.op("vector", lambda h, ft=ft: h.tensor_copy(vtile[:, ft * 512:ft * 512 + 512], ps_a[:]),
                 reads=("ps_a",), writes=("vtile",))
        p.dma("sync", lambda h, tc=tc: h.dma_start(v_d[tc * 128:tc * 128 + 128, :], vtile[:]),
              reads=("vtile",), tag=f"vst{tc}", final=True)
    return None


def build_proj1():
    p = Prog("proj1")
    with p.scope('pro_'):
        emit_proj1(p)
    return p.finalize()


def emit_attn1(p):
    qf_d = p.dram_in("qf", [8, 128, 2048], BF16)
    kff_d = p.dram_in("kff", [8, 128, 2048], BF16)
    ktm_d = p.dram_in("ktm", [2048, 1024], BF16)
    v_d = p.dram_in("vtm", [2048, 1024], BF16)
    cm_d = p.dram_in("cmask", [128, 128], BF16)
    oo_d = p.dram_out("oo", [8, 128, 2048], BF16)
    qf = p.sb("qf", [128, 2048], BF16)
    kff = p.sb("kff", [128, 2048], BF16)
    ktm = p.sb("ktm", [128, 16 * 1024], BF16)
    vtm = p.sb("vtm", [128, 16 * 1024], BF16)
    cmask = p.sb("cmask", [128, 128], BF16)
    kvsb = p.sb("kvsb", [128, 128], BF16)
    kssb = p.sb("kssb", [128, 1], BF16)
    e_am = p.sb("e_am", [128, 128], BF16)
    zrow = p.sb("zrow", [1, 128], F32)
    zrb = p.sb("zrb", [1, 128], BF16)
    rbt = p.sb("rbt", [128, 128], F32)
    onesr = p.sb("onesr", [1, 128], BF16)
    onescol = p.sb("onescol", [128, 1], BF16)
    o1 = p.sb("o1", [128, 2048], BF16)
    ps_kv = p.ps("ps_kv", [128, 128], F32)
    ps_ks = p.ps("ps_ks", [128, 1], F32)
    ps_a = p.ps("ps_a", [128, 128], F32)
    ps_o = p.ps("ps_o", [128, 128], F32)
    ps_z = p.ps("ps_z", [1, 128], F32)
    ps_b = p.ps("ps_b", [128, 128], F32)
    p.dma("sync", lambda h: h.dma_start(cmask[:], cm_d), writes=("cmask",))
    p.dma("sync", lambda h: h.dma_start(
        ktm[:].rearrange("p (c n) -> p c n", c=16), ktm_d.rearrange("(c p) n -> p c n", p=128)),
        writes=("ktm",))
    p.dma("sync", lambda h: h.dma_start(
        vtm[:].rearrange("p (c n) -> p c n", c=16), v_d.rearrange("(c p) n -> p c n", p=128)),
        writes=("vtm",))
    p.op("vector", lambda h: h.memset(onesr[:], 1.0), writes=("onesr",))
    p.op("vector", lambda h: h.memset(onescol[:], 1.0), writes=("onescol",))
    for hh in range(8):
        p.dma("sync", lambda h, hh=hh: h.dma_start(qf[:], qf_d[hh]), writes=("qf",), tag=f"q{hh}")
        p.dma("sync", lambda h, hh=hh: h.dma_start(kff[:], kff_d[hh]), writes=("kff",), tag=f"k{hh}")
        for n in range(16):
            cs = slice(n * 128, n * 128 + 128)
            hs = slice(n * 1024 + hh * 128, n * 1024 + hh * 128 + 128)
            if n == 0:
                p.op("vector", lambda h: h.memset(kvsb[:], 0.0), writes=("kvsb",))
                p.op("vector", lambda h: h.memset(kssb[:], 0.0), writes=("kssb",))
            else:
                p.op("vector", lambda h: h.tensor_copy(kvsb[:], ps_kv[:]),
                     reads=("ps_kv",), writes=("kvsb",))
                p.op("vector", lambda h: h.tensor_copy(kssb[:], ps_ks[:]),
                     reads=("ps_ks",), writes=("kssb",))
            p.op("tensor", lambda h, cs=cs: h.matmul(
                ps_a[:], kff[:, cs], qf[:, cs], start=True, stop=True),
                 reads=("kff", "qf"), writes=("ps_a",))
            p.op("vector", lambda h: h.tensor_tensor(e_am[:], ps_a[:], cmask[:], ALU.mult),
                 reads=("ps_a", "cmask"), writes=("e_am",))
            p.op("tensor", lambda h, hs=hs, n=n: h.matmul(
                ps_kv[:], ktm[:, hs], vtm[:, hs], start=(n == 0), stop=(n == 15)),
                 reads=("ktm", "vtm"), writes=("ps_kv",))
            p.op("tensor", lambda h, hs=hs, n=n: h.matmul(
                ps_ks[:], ktm[:, hs], onescol[:], start=(n == 0), stop=(n == 15)),
                 reads=("ktm", "onescol"), writes=("ps_ks",))
            p.op("tensor", lambda h, cs=cs: h.matmul(
                ps_o[:], kvsb[:], qf[:, cs], start=True, stop=False),
                 reads=("kvsb", "qf"), writes=("ps_o",))
            p.op("tensor", lambda h, hs=hs: h.matmul(
                ps_o[:], vtm[:, hs], e_am[:], start=False, stop=True),
                 reads=("vtm", "e_am"), writes=("ps_o",))
            p.op("tensor", lambda h, cs=cs: h.matmul(
                ps_z[:], kssb[:], qf[:, cs], start=True, stop=False),
                 reads=("kssb", "qf"), writes=("ps_z",))
            p.op("tensor", lambda h: h.matmul(
                ps_z[:], onescol[:], e_am[:], start=False, stop=True),
                 reads=("onescol", "e_am"), writes=("ps_z",))
            p.op("vector", lambda h: h.tensor_scalar(zrow[:], ps_z[:], 1e-6, None, ALU.add),
                 reads=("ps_z",), writes=("zrow",))
            p.op("vector", lambda h: h.reciprocal(zrow[:], zrow[:]), reads=("zrow",), writes=("zrow",))
            p.op("vector", lambda h: h.tensor_copy(zrb[:], zrow[:]), reads=("zrow",), writes=("zrb",))
            p.op("tensor", lambda h: h.matmul(ps_b[:], onesr[:], zrb[:], start=True, stop=True),
                 reads=("onesr", "zrb"), writes=("ps_b",))
            p.op("vector", lambda h: h.tensor_copy(rbt[:], ps_b[:]), reads=("ps_b",), writes=("rbt",))
            p.op("vector", lambda h, cs=cs: h.tensor_tensor(o1[:, cs], ps_o[:], rbt[:], ALU.mult),
                 reads=("ps_o", "rbt"), writes=("o1",))
        def oo_store(h, hh=hh):
            dst = oo_d[hh]
            srcap = o1[:]
            if len(dst.shape) == 3:
                srcap = srcap.rearrange("p (j t) -> p j t", j=dst.shape[1])
            return h.dma_start(dst, srcap)
        p.dma("sync", oo_store, reads=("o1",), writes=("oo_out",), tag=f"oo{hh}", final=True)
    return None


def build_attn1():
    p = Prog("attn1")
    with p.scope('att_'):
        emit_attn1(p)
    return p.finalize()


def emit_dequant(p):
    gup_d = p.dram_in("gup", [1408, 256], I32)
    gus_d = p.dram_in("gus", [16, 1408], mybir.dt.float16)
    dnp_d = p.dram_in("dnp", [256, 704], I32)
    dns_d = p.dram_in("dns", [44, 256], mybir.dt.float16)
    idf_d = p.dram_in("identf", [128, 128], F32)
    id_d = p.dram_in("ident", [128, 128], BF16)
    gut_d = p.dram_out("gut", [2048, 1408], BF16)
    dnt_d = p.dram_out("dnt", [5632, 256], BF16)
    plan = []
    for rc in range(11):
        plan.append((gup_d[rc * 128:(rc + 1) * 128, :], 256,
                     gus_d[:, rc * 128:(rc + 1) * 128], 16,
                     gut_d[:, rc * 128:(rc + 1) * 128], f"gu{rc}"))
    for rc in range(2):
        plan.append((dnp_d[rc * 128:(rc + 1) * 128, :], 704,
                     dns_d[:, rc * 128:(rc + 1) * 128], 44,
                     dnt_d[:, rc * 128:(rc + 1) * 128], f"dn{rc}"))
    _emit_dequant_core(p, plan, idf_d, id_d)


def _emit_dequant_core(p, plan, idf_d, id_d):

    identf = p.sb("identf", [128, 128], F32)
    ident = p.sb("ident", [128, 128], BF16)
    pkt = p.sb("pkt", [128, 704], I32)
    n_t = p.sb("n_t", [128, 704], I32)
    nf = p.sb("nf", [128, 5632], F32)
    sgf = p.sb("sgf", [128, 5632], F32)
    low3 = p.sb("low3", [128, 5632], F32)
    aaf = p.sb("aaf", [128, 5632], F32)
    bbf = p.sb("bbf", [128, 5632], F32)
    u16t = p.sb("u16t", [128, 5632], U16)
    vals = p.sb("vals", [128, 5632], BF16)
    scf16 = p.sb("scf16", [44, 128], mybir.dt.float16)
    scf = p.sb("scf", [44, 128], F32)
    scT = p.sb("scT", [128, 44], F32)
    tst = p.sb("tst", [128, 44 * 128], BF16)
    ps_t = p.ps("ps_t", [128, 128], BF16)
    ps_tf = p.ps("ps_tf", [128, 128], F32)

    p.dma("sync", lambda h: h.dma_start(identf[:], idf_d), writes=("identf",))
    p.dma("sync", lambda h: h.dma_start(ident[:], id_d), writes=("ident",))

    def decode_chunk(pk_ap, nwords, scales_ap, ngroups, out_ap, rc_tag):
        nin = nwords * 8
        p.dma("sync", lambda h: h.dma_start(pkt[:, 0:nwords], pk_ap), writes=("pkt",), tag=rc_tag + "pk")
        for j in range(8):
            dst = slice(j * nwords, (j + 1) * nwords)
            p.op("vector", lambda h, j=j, dst=dst: h.tensor_scalar(
                n_t[:, 0:nwords], pkt[:, 0:nwords], 4 * j, 15,
                ALU.logical_shift_right, ALU.bitwise_and),
                 reads=("pkt",), writes=("n_t",))
            p.op("vector", lambda h, dst=dst: h.tensor_copy(nf[:, dst], n_t[:, 0:nwords]),
                 reads=("n_t",), writes=("nf",))
        p.op("vector", lambda h: h.tensor_scalar(sgf[:, 0:nin], nf[:, 0:nin], 7.0, 0.0, ALU.subtract, ALU.max),
             reads=("nf",), writes=("sgf",))
        p.op("vector", lambda h: h.tensor_scalar(sgf[:, 0:nin], sgf[:, 0:nin], 1.0, 32768.0, ALU.min, ALU.mult),
             reads=("sgf",), writes=("sgf",))
        p.op("vector", lambda h: h.scalar_tensor_tensor(
            low3[:, 0:nin], sgf[:, 0:nin], -1.0 / 4096.0, nf[:, 0:nin], ALU.mult, ALU.add),
             reads=("sgf", "nf"), writes=("low3",))
        p.op("vector", lambda h: h.tensor_scalar(aaf[:, 0:nin], low3[:, 0:nin], 1.0, 16128.0, ALU.min, ALU.mult),
             reads=("low3",), writes=("aaf",))
        p.op("vector", lambda h: h.tensor_scalar(bbf[:, 0:nin], low3[:, 0:nin], 1.0, 0.0, ALU.subtract, ALU.max),
             reads=("low3",), writes=("bbf",))
        p.op("vector", lambda h: h.tensor_scalar(bbf[:, 0:nin], bbf[:, 0:nin], 1.0, 64.0, ALU.min, ALU.mult),
             reads=("bbf",), writes=("bbf",))
        p.op("vector", lambda h: h.tensor_tensor(bbf[:, 0:nin], bbf[:, 0:nin], low3[:, 0:nin], ALU.mult),
             reads=("bbf", "low3"), writes=("bbf",))
        p.op("vector", lambda h: h.tensor_tensor(aaf[:, 0:nin], aaf[:, 0:nin], bbf[:, 0:nin], ALU.add),
             reads=("aaf", "bbf"), writes=("aaf",))
        p.op("vector", lambda h: h.tensor_tensor(aaf[:, 0:nin], aaf[:, 0:nin], sgf[:, 0:nin], ALU.add),
             reads=("aaf", "sgf"), writes=("aaf",))
        p.op("vector", lambda h: h.tensor_copy(u16t[:, 0:nin], aaf[:, 0:nin]),
             reads=("aaf",), writes=("u16t",))
        p.dma("sync", lambda h: h.dma_start(scf16[0:ngroups, :], scales_ap), writes=("scf16",), tag=rc_tag + "sc")
        p.op("vector", lambda h, ngroups=ngroups: h.tensor_copy(scf[0:ngroups, :], scf16[0:ngroups, :]),
             reads=("scf16",), writes=("scf",))
        p.op("tensor", lambda h, ngroups=ngroups: h.transpose(
            ps_tf[:, 0:ngroups], scf[0:ngroups, :], identf[0:ngroups, 0:ngroups]),
             reads=("scf", "identf"), writes=("ps_tf",))
        p.op("vector", lambda h, ngroups=ngroups: h.tensor_copy(scT[:, 0:ngroups], ps_tf[:, 0:ngroups]),
             reads=("ps_tf",), writes=("scT",))
        for g in range(ngroups):
            gview_v = bass.AP(vals, 16 * g, [[5632, 128], [nwords, 8], [1, 16]])
            gview_a = bass.AP(aaf, 16 * g, [[5632, 128], [nwords, 8], [1, 16]])
            gview_u = bass.AP(u16t, 16 * g, [[5632, 128], [nwords, 8], [1, 16]])
            p.op("vector", lambda h, g=g, gv=gview_v, gu=gview_u: h.tensor_scalar(
                gv, gu.bitcast(BF16), scT[:, g:g + 1], None, ALU.mult),
                 reads=("u16t", "scT"), writes=("vals",))
        nt_ = nin // 128
        for ic in range(nt_):
            p.op("tensor", lambda h, ic=ic: h.transpose(
                ps_t[:], vals[:, ic * 128:ic * 128 + 128], ident[:]),
                 reads=("vals", "ident"), writes=("ps_t",))
            p.op("vector", lambda h, ic=ic: h.tensor_copy(tst[:, ic * 128:ic * 128 + 128], ps_t[:]),
                 reads=("ps_t",), writes=("tst",))
        p.dma("sync", lambda h, nt_=nt_: h.dma_start(
            out_ap.rearrange("(c p) n -> p c n", p=128),
            tst[:].rearrange("p (c n) -> p c n", c=44)[:, 0:nt_, :] if nt_ != 44 else
            tst[:].rearrange("p (c n) -> p c n", c=44)),
              reads=("tst",), tag=rc_tag + "out", final=True)

    for args in plan:
        decode_chunk(*args)
    return None


def build_dequant():
    p = Prog("dequant")
    with p.scope('deq_'):
        emit_dequant(p)
    return p.finalize()


def emit_ffn1(p, ntok=2048, nsub=2):
    ntc = ntok // 128
    nft = ntok // 512
    xnf_d = p.dram_in("xnf", [128, 16 * ntok], BF16)
    wg_d = p.dram_in("wgT", [2048, nsub * 1408], BF16)
    wu_d = p.dram_in("wuT", [2048, nsub * 1408], BF16)
    wd_d = p.dram_in("wdT", [nsub * 1408, 2048], BF16)
    y_d = p.dram_out("yy", [ntok, 2048], F32)
    xn = p.sb("xn", [128, 16 * ntok], BF16)
    w13 = [p.sb(f"w13_{i}", [128, 16 * 128], BF16) for i in range(4)]
    w2t = p.sb("w2t", [128, 11 * 2048], BF16)
    gt = [p.sb(f"gt{i}", [128, 512], F32) for i in range(2)]
    ut = [p.sb(f"ut{i}", [128, 512], F32) for i in range(2)]
    hbuf = p.sb("hbuf", [128, 11 * ntok], BF16)
    yst = [p.sb(f"yst{i}", [128, 512], F32) for i in range(3)]
    ps_g = [p.ps(f"ps_g{i}", [128, 512], F32) for i in range(2)]
    ps_u = [p.ps(f"ps_u{i}", [128, 512], F32) for i in range(2)]
    ps_y = [p.ps(f"ps_y{i}", [128, 512], F32) for i in range(2)]
    p.dma("sync", lambda h: h.dma_start(xn[:], xnf_d), writes=("xn",))
    for sub in range(nsub):
        ss = slice(sub * 1408, (sub + 1) * 1408)
        p.dma("gpsimd", lambda h, sub=sub: h.dma_start(
            w2t[:].rearrange("p (c n) -> p c n", c=11),
            wd_d[sub * 1408:(sub + 1) * 1408, :].rearrange("(c p) n -> p c n", p=128)),
              writes=("w2buf",), tag=f"w2ld{sub}")
        for hc in range(11):
            hco = sub * 1408 + hc * 128
            p.dma("gpsimd", lambda h, hco=hco, hc=hc: h.dma_start(
                w13[(hc % 2) * 2][:].rearrange("p (c n) -> p c n", c=16),
                wg_d.rearrange("(c p) n -> p c n", p=128)[:, :, hco:hco + 128]),
                  writes=(f"w13_{(hc % 2) * 2}",), tag=f"wg{sub}_{hc}")
            p.dma("gpsimd", lambda h, hco=hco, hc=hc: h.dma_start(
                w13[(hc % 2) * 2 + 1][:].rearrange("p (c n) -> p c n", c=16),
                wu_d.rearrange("(c p) n -> p c n", p=128)[:, :, hco:hco + 128]),
                  writes=(f"w13_{(hc % 2) * 2 + 1}",), tag=f"wu{sub}_{hc}")
            for ft in range(nft):
                rr = (hc * nft + ft) % 2
                psg, psu2, gtt, utt = ps_g[rr], ps_u[rr], gt[rr], ut[rr]
                kg, ku, kgt, kut = f"ps_g{rr}", f"ps_u{rr}", f"gt{rr}", f"ut{rr}"
                for kc in range(16):
                    p.op("tensor", lambda h, hc=hc, ft=ft, kc=kc, psg=psg: h.matmul(
                        psg[:], w13[(hc % 2) * 2][:, kc * 128: kc * 128 + 128],
                        xn[:, kc * ntok + ft * 512: kc * ntok + ft * 512 + 512],
                        start=(kc == 0), stop=(kc == 15)),
                         reads=(f"w13_{(hc % 2) * 2}", "xn"), writes=(kg,))
                for kc in range(16):
                    p.op("tensor", lambda h, hc=hc, ft=ft, kc=kc, psu2=psu2: h.matmul(
                        psu2[:], w13[(hc % 2) * 2 + 1][:, kc * 128: kc * 128 + 128],
                        xn[:, kc * ntok + ft * 512: kc * ntok + ft * 512 + 512],
                        start=(kc == 0), stop=(kc == 15)),
                         reads=(f"w13_{(hc % 2) * 2 + 1}", "xn"), writes=(ku,))
                p.op("scalar", lambda h, psg=psg, gtt=gtt: h.activation(gtt[:], psg[:], AF.Silu),
                     reads=(kg,), writes=(kgt,))
                p.op("vector", lambda h, psu2=psu2, utt=utt: h.tensor_copy(utt[:], psu2[:]),
                     reads=(ku,), writes=(kut,))
                p.op("vector", lambda h, hc=hc, ft=ft, gtt=gtt, utt=utt: h.tensor_tensor(
                    hbuf[:, hc * ntok + ft * 512: hc * ntok + ft * 512 + 512],
                    gtt[:], utt[:], ALU.mult),
                     reads=(kgt, kut), writes=("hbuf",))
        for tc in range(ntc):
            for mf in range(4):
                ri = tc * 4 + mf
                psy, ky = ps_y[ri % 2], f"ps_y{ri % 2}"
                ys, kys = yst[ri % 3], f"yst{ri % 3}"
                for hc in range(11):
                    p.op("tensor", lambda h, tc=tc, mf=mf, hc=hc, psy=psy: h.matmul(
                        psy[:], hbuf[:, hc * ntok + tc * 128: hc * ntok + tc * 128 + 128],
                        w2t[:, hc * 2048 + mf * 512: hc * 2048 + mf * 512 + 512],
                        start=(hc == 0), stop=(hc == 10)),
                         reads=("hbuf", "w2buf"), writes=(ky,))
                p.op("vector", lambda h, psy=psy, ys=ys: h.tensor_copy(ys[:], psy[:]),
                     reads=(ky,), writes=(kys,))
                p.dma("gpsimd", lambda h, tc=tc, mf=mf, sub=sub, ys=ys: h.dma_start(
                    y_d[tc * 128:tc * 128 + 128, mf * 512:mf * 512 + 512], ys[:],
                    accum_op=(ALU.bypass if sub == 0 else ALU.add)),
                      reads=(kys,), writes=(f"ydr_{tc}_{mf}",),
                      tag=f"yacc{sub}_{tc}_{mf}", final=True)
    return None


def build_ffn1():
    p = Prog("ffn1")
    with p.scope('ffn_'):
        emit_ffn1(p)
    return p.finalize()


def emit_final(p, nt=TLOC, d=D, two_y=True):
    X = mybir.AxisListType.X
    xa_d = p.dram_in("x1a", [nt, d], F32)
    y0_d = p.dram_in("y0", [nt, d], F32)
    y1_d = p.dram_in("y1", [nt, d], F32) if two_y else None
    x2_d = p.dram_out("x2", [nt, d], F32)
    xt = p.sb("xt", [128, d], F32)
    y0t = p.sb("y0t", [128, d], F32)
    y1t = p.sb("y1t", [128, d], F32)
    sq = p.sb("sq", [128, d], F32)
    ms = p.sb("ms", [128, 1], F32)
    rstd = p.sb("rstd", [128, 1], F32)
    for c in range(nt // 128):
        sl = slice(c * 128, c * 128 + 128)
        p.dma("sync", lambda h, sl=sl: h.dma_start(xt[:], xa_d[sl, :]), writes=("xt",))
        p.dma("sync", lambda h, sl=sl: h.dma_start(y0t[:], y0_d[sl, :]), writes=("y0t",))
        if two_y:
            p.dma("sync", lambda h, sl=sl: h.dma_start(y1t[:], y1_d[sl, :]), writes=("y1t",))
            p.op("vector", lambda h: h.tensor_tensor(y0t[:], y0t[:], y1t[:], ALU.add),
                 reads=("y0t", "y1t"), writes=("y0t",))
        p.op("scalar", lambda h: h.activation(sq[:], y0t[:], AF.Square),
             reads=("y0t",), writes=("sq",))
        p.op("vector", lambda h: h.reduce_sum(ms[:], sq[:], axis=X),
             reads=("sq",), writes=("ms",))
        p.op("vector", lambda h: h.tensor_scalar(ms[:], ms[:], 1.0 / d, 1e-6, ALU.mult, ALU.add),
             reads=("ms",), writes=("ms",))
        p.op("scalar", lambda h: h.activation(rstd[:], ms[:], AF.Ln),
             reads=("ms",), writes=("rstd",))
        p.op("scalar", lambda h: h.activation(rstd[:], rstd[:], AF.Exp, scale=-0.5),
             reads=("rstd",), writes=("rstd",))
        p.op("vector", lambda h: h.tensor_scalar(y0t[:], y0t[:], rstd[:], None, ALU.mult),
             reads=("y0t", "rstd"), writes=("y0t",))
        p.op("vector", lambda h: h.tensor_tensor(xt[:], xt[:], y0t[:], ALU.add),
             reads=("xt", "y0t"), writes=("xt",))
        p.dma("sync", lambda h, sl=sl: h.dma_start(x2_d[sl, :], xt[:]),
              reads=("xt",), tag=f"st{c}", final=True)
    return None


def build_final(nt=TLOC, d=D):
    p = Prog("final")
    with p.scope('fin_'):
        emit_final(p, nt, d)
    return p.finalize()


def emit_tm2fm(p):
    tm = p.dram_in("tm", [TLOC, 2048], BF16)
    fm = p.dram_in("fm", [2048, TLOC], BF16)
    idd = p.dram_in("ident", [128, 128], BF16)
    ident = p.sb("ident", [128, 128], BF16)
    tmt = p.sb("tmt", [128, 2048], BF16)
    stg = p.sb("stg", [128, 16 * 128], BF16)
    ps_t = p.ps("ps_t", [128, 128], BF16)
    p.dma("sync", lambda h: h.dma_start(ident[:], idd), writes=("ident",))
    for tc in range(TLOC // 128):
        p.dma("sync", lambda h, tc=tc: h.dma_start(tmt[:], tm[tc * 128:tc * 128 + 128, :]),
              writes=("tmt",), tag=f"ld{tc}")
        for dc in range(16):
            p.op("tensor", lambda h, dc=dc: h.transpose(
                ps_t[:], tmt[:, dc * 128:dc * 128 + 128], ident[:]),
                 reads=("tmt", "ident"), writes=("ps_t",))
            p.op("vector", lambda h, dc=dc: h.tensor_copy(stg[:, dc * 128:dc * 128 + 128], ps_t[:]),
                 reads=("ps_t",), writes=("stg",))
        p.dma("sync", lambda h, tc=tc: h.dma_start(
            fm[:, tc * 128:tc * 128 + 128].rearrange("(c p) t -> p c t", p=128),
            stg[:].rearrange("p (c t) -> p c t", c=16)),
              reads=("stg",), writes=("fm_out",), tag=f"st{tc}", final=True)


def emit_fm2own(p):
    fm = p.dram_in("fm", [2048, TLOC], BF16)
    xno = p.dram_in("xnfown", [128, 16 * TLOC], BF16)
    p.dma("sync", lambda h: h.dma_start(
        xno.rearrange("p (c t) -> p c t", c=16),
        fm.rearrange("(c p) t -> p c t", p=128)),
          reads=("fm_in",), writes=("xno_out",), tag="fmo", final=True)


def emit_gather(p, ag_flat_ap, nch, width, dst3, idxname, tag):
    from concourse.bass import IndirectOffsetOnAxis
    idx_d = p.dram_in(idxname, [128, nch], I32)
    idxt = p.sb("idxt", [128, nch], I32)
    ga = p.sb("ga", [128, width], BF16)
    gb2 = p.sb("gb2", [128, width], BF16)
    tiles = [ga, gb2]
    p.dma("sync", lambda h: h.dma_start(idxt[:], idx_d), writes=("idxt",))
    for c in range(nch):
        t = tiles[c % 2]
        tk = f"gt{c % 2}"
        p.dma("gpsimd", lambda h, c=c, t=t: h.indirect_dma_start(
            t[:], None, ag_flat_ap, IndirectOffsetOnAxis(idxt[:, c:c + 1], 0)),
              reads=("idxt", "ag_in"), writes=(tk,), tag=f"{tag}g{c}")
        p.dma("sync", lambda h, c=c, t=t: h.dma_start(dst3[:, c, :], t[:]),
              reads=(tk,), writes=("gout",), tag=f"{tag}s{c}", final=True)


def emit_wasm(p):
    gu_ag = p.dram_in("gu_ag", [8 * 2048, 1408], BF16)
    dn_ag = p.dram_in("dn_ag", [8 * 5632, 256], BF16)
    wg = p.dram_in("wg_asm", [2048, 5632], BF16)
    wu = p.dram_in("wu_asm", [2048, 5632], BF16)
    wd = p.dram_in("wd_asm", [5632, 2048], BF16)

    def unperm(src2d, dst2d, tag):
        nk = dst2d.shape[0] // 128
        for a in range(8):
            sr = src2d.rearrange("(a k b) n -> a k b n", a=8, b=16)[a]
            dr = bass.AP(dst2d.tensor, dst2d.offset + a * _row_stride(dst2d),
                         [[128 * _row_stride(dst2d), nk], [8 * _row_stride(dst2d), 16],
                          [1, dst2d.shape[1]]])
            p.dma("sync", lambda h, sr=sr, dr=dr: h.dma_start(dr, sr),
                  reads=("ag_in",), writes=("w_out",), tag=f"{tag}a{a}", final=True)

    for r in range(4):
        unperm(gu_ag[r * 2048:(r + 1) * 2048, :], wg[:, r * 1408:(r + 1) * 1408], f"wg{r}")
        unperm(gu_ag[(4 + r) * 2048:(5 + r) * 2048, :], wu[:, r * 1408:(r + 1) * 1408], f"wu{r}")
    for r in range(8):
        unperm(dn_ag[r * 5632:(r + 1) * 5632, :], wd[:, r * 256:(r + 1) * 256], f"wd{r}")


def _row_stride(ap2d):
    return ap2d.ap[0][0]


PAIRS = [[0, 1], [2, 3], [4, 5], [6, 7]]
ALL8 = [[0, 1, 2, 3, 4, 5, 6, 7]]


def build_fused():
    p = Prog("fused")
    din, dtmp = p.dram_in, p.dram_tmp
    ext = {}
    ext["x"] = din("x", [TLOC, 2048], F32)
    for nm, sh in [("ident", [128, 128]), ("smask", [128, 128]), ("smaskt", [128, 128]),
                   ("cmask", [128, 128])]:
        ext[nm] = din(nm, sh, BF16)
    ext["identf"] = din("identf", [128, 128], F32)
    ext["cosr"] = din("cosr", [2048, 256], F32)
    ext["sinr"] = din("sinr", [2048, 256], F32)
    for nm, sh, dt in [
        ("a0_wq", [2048, 1024], F32), ("a0_wk", [2048, 256], F32), ("a0_wv", [2048, 256], F32),
        ("a0_wb", [2048, 16], F32), ("a0_bb", [16, 1], F32),
        ("a0_qnw", [1, 64], F32), ("a0_knw", [1, 64], F32),
        ("wo0", [2048, 2048], F32), ("bo0", [1, 2048], F32),
        ("m_gwt", [2048, 16], F32), ("m_gb", [1, 16], F32),
        ("m_w1t", [16, 2048, 1408], F32), ("m_w3t", [16, 2048, 1408], F32),
        ("m_w2t", [16, 1408, 2048], F32),
        ("m_b1", [128, 176], F32), ("m_b3", [128, 176], F32), ("m_b2", [16, 2048], F32),
        ("m_ws1t", [2048, 2816], F32), ("m_ws3t", [2048, 2816], F32),
        ("m_ws2t", [2816, 2048], F32), ("m_bs1", [128, 22], F32), ("m_bs3", [128, 22], F32),
        ("m_bs2", [1, 2048], F32),
        ("a1_wq", [2048, 1024], F32), ("a1_wk", [2048, 1024], F32), ("a1_wv", [2048, 1024], F32),
        ("a1_bq", [128, 8], F32), ("a1_bk", [128, 8], F32), ("a1_bv", [1, 1024], F32),
        ("wo1", [2048, 2048], F32), ("bo1", [1, 2048], F32),
        ("dq_gup", [1408, 256], I32), ("dq_gus", [16, 1408], mybir.dt.float16),
        ("dq_dnp", [256, 704], I32), ("dq_dns", [44, 256], mybir.dt.float16),
    ]:
        ext[nm] = din(nm, sh, dt)
    x2_out = p.dram_out("x2", [TLOC, 2048], F32)

    T = dtmp
    xn_tm = T("i_xntm", [TLOC, 2048], BF16)
    fm_own = T("i_fmown", [2048, TLOC], BF16)
    xnf_own = T("i_xnfown", [128, 16 * TLOC], BF16)
    fmag_t = p.nc.dram_tensor("i_fmag", [8 * 2048, TLOC], BF16, kind="Internal")
    xnf_full = T("i_xnffull", [128, 16 * 2048], BF16)
    qa_t = T("i_qa", [16, 65, 2048], BF16)
    ka_t = T("i_ka", [4, 66, 2048], BF16)
    vv_t = T("i_vv", [2048, 256], BF16)
    beta_t = T("i_beta", [16, 2048], F32)
    oo_loc = T("i_ooloc", [TLOC, 2048], BF16)
    oag_t = p.nc.dram_tensor("i_oag", [8 * TLOC, 2048], BF16, kind="Internal")
    ofm_asm = T("i_ofmasm", [128, 16 * TLOC], BF16)
    x_attn = T("i_xattn", [TLOC, 2048], F32)
    yy0 = T("i_yy0", [TLOC, 2048], F32)
    x1_t = T("i_x1", [TLOC, 2048], F32)
    xn1_tm = T("i_xn1tm", [TLOC, 2048], BF16)
    qf_t = T("i_qf", [8, 128, 2048], BF16)
    kff_t = T("i_kff", [8, 128, 2048], BF16)
    ktm_t = T("i_ktm", [2048, 1024], BF16)
    vtm_t = T("i_vtm", [2048, 1024], BF16)
    x1a_t = T("i_x1a", [TLOC, 2048], F32)
    gut_t = T("i_gut", [2048, 1408], BF16)
    dnt_t = T("i_dnt", [5632, 256], BF16)
    gu_ag = T("i_guag", [8 * 2048, 1408], BF16)
    dn_ag = T("i_dnag", [8 * 5632, 256], BF16)
    wg_asm = T("i_wgasm", [2048, 5632], BF16)
    wu_asm = T("i_wuasm", [2048, 5632], BF16)
    wd_asm = T("i_wdasm", [5632, 2048], BF16)
    yy1 = T("i_yy1", [TLOC, 2048], F32)

    fmag_flat = bass.AP(fmag_t, 0, [[TLOC, 8 * 2048], [1, TLOC]])
    oag_flat = bass.AP(oag_t, 0, [[1024, 8 * TLOC * 2], [1, 1024]])
    oo0_view = oo_loc.rearrange("(h r) t -> h r t", h=16)
    oo1_view = oo_loc.rearrange("(h r) t -> h r t", h=8)
    xnf_full3 = xnf_full.rearrange("p (c j t) -> p (c j) t", j=2, t=1024)
    ofm3 = ofm_asm.rearrange("p (c t) -> p c t", c=16)

    def phase(pfx, fn, io, **kw):
        p.set_io(io)
        with p.scope(pfx):
            fn(p, **kw)
        p.set_io(None)

    import os
    maxcoll = int(os.environ.get("FUSED_MAXCOLL", "99"))
    ncoll = [0]

    def coll(kind, op, groups, in_ap, out_ap, tag):
        p.barrier()
        if ncoll[0] < maxcoll:
            p.op("gpsimd", lambda h: h.collective_compute(
                kind, op, replica_groups=groups, ins=[in_ap.opt()], outs=[out_ap.opt()]),
                 reads=(), writes=(), tag=tag)
        ncoll[0] += 1
        p.barrier()

    def gphase(pfx, ag_flat, nch, width, dst3, idxname):
        p.set_io({idxname: ext[idxname]} if idxname in ext else None)
        ext.setdefault(idxname, din(idxname, [128, nch], I32))
        p.set_io({idxname: ext[idxname]})
        with p.scope(pfx):
            emit_gather(p, ag_flat, nch, width, dst3, idxname, pfx)
        p.set_io(None)

    phase("ln0_", emit_ln, {"x": ext["x"], "xn": xn_tm})
    phase("tf0_", emit_tm2fm, {"tm": xn_tm, "fm": fm_own, "ident": ext["ident"]})
    phase("fo0_", emit_fm2own, {"fm": fm_own, "xnfown": xnf_own})
    coll("AllGather", ALU.bypass, ALL8, fm_own, fmag_t.ap(), "agxn0")
    gphase("gx0_", fmag_flat, 32, 1024, xnf_full3, "gxn_idx")
    phase("p0_", emit_proj0, {
        "xnf": xnf_full, "wq": ext["a0_wq"], "wk": ext["a0_wk"], "wv": ext["a0_wv"],
        "wb": ext["a0_wb"], "bb": ext["a0_bb"], "qnw": ext["a0_qnw"], "knw": ext["a0_knw"],
        "cosr": ext["cosr"], "sinr": ext["sinr"], "ident": ext["ident"],
        "qa": qa_t, "ka": ka_t, "vv": vv_t, "beta": beta_t})
    phase("at0_", emit_attn0, {
        "qa": qa_t, "ka": ka_t, "vv": vv_t, "beta": beta_t, "ident": ext["ident"],
        "identf": ext["identf"], "smask": ext["smask"], "smaskt": ext["smaskt"],
        "cmask": ext["cmask"], "oo": oo0_view})
    coll("AllGather", ALU.bypass, ALL8, oo_loc, oag_t.ap(), "ago0")
    gphase("go0_", oag_flat, 16, 1024, ofm3, "go_idx")
    phase("wo0_", emit_wo_res, {
        "ofm": ofm_asm, "wo": ext["wo0"], "bo": ext["bo0"], "xin": ext["x"],
        "xout": x_attn})
    phase("mo0_", emit_moe0, {
        "xnf": xnf_own, "gwt": ext["m_gwt"], "gb": ext["m_gb"],
        "w1t": ext["m_w1t"], "w3t": ext["m_w3t"], "w2t": ext["m_w2t"],
        "b1": ext["m_b1"], "b3": ext["m_b3"], "b2": ext["m_b2"],
        "ws1t": ext["m_ws1t"], "ws3t": ext["m_ws3t"], "ws2t": ext["m_ws2t"],
        "bs1": ext["m_bs1"], "bs3": ext["m_bs3"], "ident": ext["ident"], "yy": yy0},
        ntok=TLOC, nexp=16, nsh=2)
    phase("cb1_", emit_combine_ln, {
        "xattn": x_attn, "y0": yy0, "bs2": ext["m_bs2"], "x1": x1_t, "xn": xn1_tm},
        two_y=False)
    phase("tf1_", emit_tm2fm, {"tm": xn1_tm, "fm": fm_own, "ident": ext["ident"]})
    phase("fo1_", emit_fm2own, {"fm": fm_own, "xnfown": xnf_own})
    coll("AllGather", ALU.bypass, ALL8, fm_own, fmag_t.ap(), "agxn1")
    gphase("gx1_", fmag_flat, 32, 1024, xnf_full3, "gxn_idx")
    phase("p1_", emit_proj1, {
        "xnf": xnf_full, "wq": ext["a1_wq"], "wk": ext["a1_wk"], "wv": ext["a1_wv"],
        "bq": ext["a1_bq"], "bk": ext["a1_bk"], "bv": ext["a1_bv"], "ident": ext["ident"],
        "qf": qf_t, "kff": kff_t, "ktm": ktm_t, "vtm": vtm_t})
    phase("at1_", emit_attn1, {
        "qf": qf_t, "kff": kff_t, "ktm": ktm_t, "vtm": vtm_t, "cmask": ext["cmask"],
        "oo": oo1_view})
    coll("AllGather", ALU.bypass, ALL8, oo_loc, oag_t.ap(), "ago1")
    gphase("go1_", oag_flat, 16, 1024, ofm3, "go_idx")
    phase("wo1_", emit_wo_res, {
        "ofm": ofm_asm, "wo": ext["wo1"], "bo": ext["bo1"], "xin": x1_t,
        "xout": x1a_t})
    phase("dq_", emit_dequant, {
        "gup": ext["dq_gup"], "gus": ext["dq_gus"], "dnp": ext["dq_dnp"],
        "dns": ext["dq_dns"], "identf": ext["identf"], "ident": ext["ident"],
        "gut": gut_t, "dnt": dnt_t})
    coll("AllGather", ALU.bypass, ALL8, gut_t, gu_ag, "agdq1")
    coll("AllGather", ALU.bypass, ALL8, dnt_t, dn_ag, "agdq2")
    phase("wa_", emit_wasm, {"gu_ag": gu_ag, "dn_ag": dn_ag,
                             "wg_asm": wg_asm, "wu_asm": wu_asm, "wd_asm": wd_asm})
    phase("f1_", emit_ffn1, {
        "xnf": xnf_own, "wgT": wg_asm, "wuT": wu_asm, "wdT": wd_asm, "yy": yy1},
        ntok=TLOC, nsub=4)
    phase("fin_", emit_final, {"x1a": x1a_t, "y0": yy1, "x2": x2_out}, two_y=False)
    return p.finalize()


def _fused_in_map(ins, c, consts, shared):
    b, pp = c // 2, c % 2
    identb, identf, smask, smaskt, cmask, cosr, sinr = consts
    ds1 = slice(pp * 1024, (pp + 1) * 1024)
    pidx = np.arange(128, dtype=np.int32)
    gxn = np.empty((128, 32), np.int32)
    for ch in range(16):
        for j in range(2):
            gxn[:, ch * 2 + j] = (2 * b + j) * 2048 + ch * 128 + pidx
    go = np.empty((128, 16), np.int32)
    for ch in range(16):
        go[:, ch] = ((2 * b + ch // 8) * 1024 + (ch % 8) * 128 + pidx) * 2 + pp
    m = {
        "x": np.ascontiguousarray(ins["L0_attn_x"].reshape(4, 2, TLOC, D)[b, pp]),
        "ident": identb, "identf": identf, "smask": smask, "smaskt": smaskt,
        "cmask": cmask, "cosr": cosr, "sinr": sinr,
        "a0_wq": np.ascontiguousarray(ins["L0_attn_Wq"][:, pp * 1024:(pp + 1) * 1024]),
        "a0_wk": np.ascontiguousarray(ins["L0_attn_Wk"][:, pp * 256:(pp + 1) * 256]),
        "a0_wv": np.ascontiguousarray(ins["L0_attn_Wv"][:, pp * 256:(pp + 1) * 256]),
        "a0_wb": np.ascontiguousarray(ins["L0_attn_Wb"][:, pp * 16:(pp + 1) * 16]),
        "a0_bb": np.ascontiguousarray(ins["L0_attn_bb"][pp * 16:(pp + 1) * 16].reshape(16, 1)),
        "a0_qnw": np.ascontiguousarray(ins["L0_attn_q_norm_w"].reshape(1, 64)),
        "a0_knw": np.ascontiguousarray(ins["L0_attn_k_norm_w"].reshape(1, 64)),
        "wo0": ins["L0_attn_Wo"], "bo0": shared["zeros2048"],
        "m_gwt": shared["gwt"], "m_gb": shared["gb"],
        "m_w1t": shared["w1t"], "m_w3t": shared["w3t"], "m_w2t": shared["w2t"],
        "m_b1": shared["b1"], "m_b3": shared["b3"], "m_b2": ins["L0_ffn_b2"],
        "m_ws1t": shared["ws1t"], "m_ws3t": shared["ws3t"], "m_ws2t": shared["ws2t"],
        "m_bs1": shared["bs1"], "m_bs3": shared["bs3"],
        "m_bs2": np.ascontiguousarray(ins["L0_ffn_bs2"].reshape(1, 2048)),
        "a1_wq": np.ascontiguousarray(ins["L1_attn_Wq"][:, ds1]),
        "a1_wk": np.ascontiguousarray(ins["L1_attn_Wk"][:, ds1]),
        "a1_wv": np.ascontiguousarray(ins["L1_attn_Wv"][:, ds1]),
        "a1_bq": np.ascontiguousarray(ins["L1_attn_bq"][ds1].reshape(8, 128).T),
        "a1_bk": np.ascontiguousarray(ins["L1_attn_bk"][ds1].reshape(8, 128).T),
        "a1_bv": np.ascontiguousarray(ins["L1_attn_bv"][ds1].reshape(1, 1024)),
        "wo1": ins["L1_attn_Wo"],
        "bo1": np.ascontiguousarray(ins["L1_attn_bo"].reshape(1, 2048)),
        "dq_gup": np.ascontiguousarray(ins["L1_ffn_gate_up_packed"][c * 1408:(c + 1) * 1408]),
        "dq_gus": np.ascontiguousarray(ins["L1_ffn_gate_up_scales"][:, c * 1408:(c + 1) * 1408]),
        "dq_dnp": np.ascontiguousarray(ins["L1_ffn_down_packed"][c * 256:(c + 1) * 256]),
        "dq_dns": np.ascontiguousarray(ins["L1_ffn_down_scales"][:, c * 256:(c + 1) * 256]),
        "gxn_idx": gxn, "go_idx": go,
    }
    return m


def kernel_fused(**ins):
    identb = np.eye(128, dtype=BF16NP)
    identf = np.eye(128, dtype=np.float32)
    smask, smaskt, cmask = _masks()
    cosr, sinr = rope_tables()
    consts = (identb, identf, smask, smaskt, cmask, cosr, sinr)
    w1, w2, w3 = ins["L0_ffn_w1"], ins["L0_ffn_w2"], ins["L0_ffn_w3"]
    b1, b3 = ins["L0_ffn_b1"], ins["L0_ffn_b3"]
    shared = {
        "zeros2048": np.zeros((1, 2048), np.float32),
        "gwt": np.ascontiguousarray(ins["L0_ffn_gate_w"].T),
        "gb": np.ascontiguousarray(ins["L0_ffn_gate_b"].reshape(1, 16)),
        "w1t": np.ascontiguousarray(w1.transpose(0, 2, 1)),
        "w3t": np.ascontiguousarray(w3.transpose(0, 2, 1)),
        "w2t": np.ascontiguousarray(w2.transpose(0, 2, 1)),
        "b1": np.ascontiguousarray(b1.reshape(16, 11, 128).transpose(2, 0, 1).reshape(128, 176)),
        "b3": np.ascontiguousarray(b3.reshape(16, 11, 128).transpose(2, 0, 1).reshape(128, 176)),
        "ws1t": np.ascontiguousarray(ins["L0_ffn_ws1"].T),
        "ws3t": np.ascontiguousarray(ins["L0_ffn_ws3"].T),
        "ws2t": np.ascontiguousarray(ins["L0_ffn_ws2"].T),
        "bs1": np.ascontiguousarray(ins["L0_ffn_bs1"].reshape(22, 128).T),
        "bs3": np.ascontiguousarray(ins["L0_ffn_bs3"].reshape(22, 128).T),
    }
    C = list(range(8))
    nc = _prog("fused", build_fused)
    in_maps = [_fused_in_map(ins, c, consts, shared) for c in C]
    res = run_bass_kernel_spmd(nc, in_maps, core_ids=C)
    out = np.empty((4, 2048, 2048), np.float32)
    for c in C:
        b, th = c // 2, c % 2
        out[b, th * TLOC:(th + 1) * TLOC] = res.results[c]["x2"]
    return out


_progcache = {}


def _prog(name, fn):
    if name not in _progcache:
        _progcache[name] = fn()
    return _progcache[name]


def _fm_chunks(a2d):
    ntok = a2d.shape[1]
    return np.ascontiguousarray(a2d.reshape(16, 128, ntok).transpose(1, 0, 2).reshape(128, 16 * ntok))


def _masks():
    ii, jj = np.arange(128)[:, None], np.arange(128)[None, :]
    smask = (ii < jj).astype(BF16NP)
    smaskt = (jj < ii).astype(BF16NP)
    cmask = (ii <= jj).astype(BF16NP)
    return smask, smaskt, cmask


def kernel_unfused(**ins):
    identb = np.eye(128, dtype=BF16NP)
    identf = np.eye(128, dtype=np.float32)
    smask, smaskt, cmask = _masks()
    cosr, sinr = rope_tables()
    C = list(range(8))

    x0 = ins["L0_attn_x"]
    xr = x0.reshape(4, 2, TLOC, D)

    nc = _prog("ln", build_ln)
    r_ln = run_bass_kernel_spmd(nc, [
        {"x": np.ascontiguousarray(xr[c // 2, c % 2])} for c in C], core_ids=C)
    xn0 = [r_ln.results[c]["xn"] for c in C]

    def xnf_maps(xn_list):
        out = []
        for b in range(4):
            full = np.concatenate([xn_list[2 * b], xn_list[2 * b + 1]], axis=0)
            out.append(_fm_chunks(full.T))
        return out

    xnf0 = xnf_maps(xn0)

    Wq, Wk, Wv, Wb = ins["L0_attn_Wq"], ins["L0_attn_Wk"], ins["L0_attn_Wv"], ins["L0_attn_Wb"]
    nc = _prog("proj0", build_proj0)
    im = []
    for c in C:
        b, hg = c // 2, c % 2
        im.append({
            "xnf": xnf0[b],
            "wq": np.ascontiguousarray(Wq[:, hg * 1024:(hg + 1) * 1024]),
            "wk": np.ascontiguousarray(Wk[:, hg * 256:(hg + 1) * 256]),
            "wv": np.ascontiguousarray(Wv[:, hg * 256:(hg + 1) * 256]),
            "wb": np.ascontiguousarray(Wb[:, hg * 16:(hg + 1) * 16]),
            "bb": np.ascontiguousarray(ins["L0_attn_bb"][hg * 16:(hg + 1) * 16].reshape(16, 1)),
            "qnw": np.ascontiguousarray(ins["L0_attn_q_norm_w"].reshape(1, 64)),
            "knw": np.ascontiguousarray(ins["L0_attn_k_norm_w"].reshape(1, 64)),
            "cosr": cosr, "sinr": sinr, "ident": identb,
        })
    r_p0 = run_bass_kernel_spmd(nc, im, core_ids=C)

    nc = _prog("attn0", build_attn0)
    im = []
    for c in C:
        r = r_p0.results[c]
        im.append({"qa": r["qa"], "ka": r["ka"], "vv": r["vv"], "beta": r["beta"],
                   "ident": identb, "identf": identf, "smask": smask, "cmask": cmask})
    r_a0 = run_bass_kernel_spmd(nc, im, core_ids=C)

    nc = _prog("wo_res", build_wo_res)
    zeros_bo = np.zeros((1, 2048), np.float32)
    im = []
    for c in C:
        b, th = c // 2, c % 2
        ofull = np.concatenate([
            r_a0.results[2 * b]["oo"].reshape(1024, 2048),
            r_a0.results[2 * b + 1]["oo"].reshape(1024, 2048)], axis=0)
        im.append({
            "ofm": _fm_chunks(ofull[:, th * TLOC:(th + 1) * TLOC]),
            "wo": ins["L0_attn_Wo"],
            "bo": zeros_bo,
            "xin": np.ascontiguousarray(xr[b, th]),
        })
    r_w0 = run_bass_kernel_spmd(nc, im, core_ids=C)

    w1, w2, w3 = ins["L0_ffn_w1"], ins["L0_ffn_w2"], ins["L0_ffn_w3"]
    b1, b2, b3 = ins["L0_ffn_b1"], ins["L0_ffn_b2"], ins["L0_ffn_b3"]
    ws1, ws2, ws3 = ins["L0_ffn_ws1"], ins["L0_ffn_ws2"], ins["L0_ffn_ws3"]
    bs1, bs3 = ins["L0_ffn_bs1"], ins["L0_ffn_bs3"]
    gw, gb = ins["L0_ffn_gate_w"], ins["L0_ffn_gate_b"]
    nc = _prog("moe0", build_moe0)
    im = []
    for c in C:
        b, eg = c // 2, c % 2
        es = slice(eg * 8, eg * 8 + 8)
        hs = slice(eg * 1408, (eg + 1) * 1408)
        order = list(range(eg * 8, eg * 8 + 8)) + list(range((1 - eg) * 8, (1 - eg) * 8 + 8))
        im.append({
            "xnf": xnf0[b],
            "gwt": np.ascontiguousarray(gw[order].T),
            "gb": np.ascontiguousarray(gb[order].reshape(1, 16)),
            "w1t": np.ascontiguousarray(w1[es].transpose(0, 2, 1)),
            "w3t": np.ascontiguousarray(w3[es].transpose(0, 2, 1)),
            "w2t": np.ascontiguousarray(w2[es].transpose(0, 2, 1)),
            "b1": np.ascontiguousarray(b1[es].reshape(8, 11, 128).transpose(2, 0, 1).reshape(128, 88)),
            "b3": np.ascontiguousarray(b3[es].reshape(8, 11, 128).transpose(2, 0, 1).reshape(128, 88)),
            "b2": np.ascontiguousarray(b2[es]),
            "ws1t": np.ascontiguousarray(ws1[hs].T),
            "ws3t": np.ascontiguousarray(ws3[hs].T),
            "ws2t": np.ascontiguousarray(ws2[:, hs].T),
            "bs1": np.ascontiguousarray(bs1[hs].reshape(11, 128).T),
            "bs3": np.ascontiguousarray(bs3[hs].reshape(11, 128).T),
            "ident": identb,
        })
    r_m0 = run_bass_kernel_spmd(nc, im, core_ids=C)

    nc = _prog("combine_ln", build_combine_ln)
    im = []
    for c in C:
        b, th = c // 2, c % 2
        sl = slice(th * TLOC, (th + 1) * TLOC)
        im.append({
            "xattn": r_w0.results[c]["xout"],
            "y0": np.ascontiguousarray(r_m0.results[2 * b]["yy"][sl]),
            "y1": np.ascontiguousarray(r_m0.results[2 * b + 1]["yy"][sl]),
            "bs2": np.ascontiguousarray(ins["L0_ffn_bs2"].reshape(1, 2048)),
        })
    r_c1 = run_bass_kernel_spmd(nc, im, core_ids=C)
    xn1 = [r_c1.results[c]["xn"] for c in C]
    x1 = [r_c1.results[c]["x1"] for c in C]
    xnf1 = xnf_maps(xn1)

    nc = _prog("proj1", build_proj1)
    im = []
    for c in C:
        b, hg = c // 2, c % 2
        ds = slice(hg * 1024, (hg + 1) * 1024)
        im.append({
            "xnf": xnf1[b],
            "wq": np.ascontiguousarray(ins["L1_attn_Wq"][:, ds]),
            "wk": np.ascontiguousarray(ins["L1_attn_Wk"][:, ds]),
            "wv": np.ascontiguousarray(ins["L1_attn_Wv"][:, ds]),
            "bq": np.ascontiguousarray(ins["L1_attn_bq"][ds].reshape(8, 128).T),
            "bk": np.ascontiguousarray(ins["L1_attn_bk"][ds].reshape(8, 128).T),
            "bv": np.ascontiguousarray(ins["L1_attn_bv"][ds].reshape(1, 1024)),
            "ident": identb,
        })
    r_p1 = run_bass_kernel_spmd(nc, im, core_ids=C)

    nc = _prog("attn1", build_attn1)
    im = []
    for c in C:
        r = r_p1.results[c]
        im.append({"qf": r["qf"], "kff": r["kff"], "ktm": r["ktm"], "vtm": r["vtm"],
                   "cmask": cmask})
    r_a1 = run_bass_kernel_spmd(nc, im, core_ids=C)

    nc = _prog("wo_res", build_wo_res)
    im = []
    for c in C:
        b, th = c // 2, c % 2
        ofull = np.concatenate([
            r_a1.results[2 * b]["oo"].reshape(1024, 2048),
            r_a1.results[2 * b + 1]["oo"].reshape(1024, 2048)], axis=0)
        im.append({
            "ofm": _fm_chunks(ofull[:, th * TLOC:(th + 1) * TLOC]),
            "wo": ins["L1_attn_Wo"],
            "bo": np.ascontiguousarray(ins["L1_attn_bo"].reshape(1, 2048)),
            "xin": x1[c],
        })
    r_w1 = run_bass_kernel_spmd(nc, im, core_ids=C)

    gup, gus = ins["L1_ffn_gate_up_packed"], ins["L1_ffn_gate_up_scales"]
    dnp, dns = ins["L1_ffn_down_packed"], ins["L1_ffn_down_scales"]
    nc = _prog("dequant", build_dequant)
    im = []
    for c in C:
        im.append({
            "gup": np.ascontiguousarray(gup[c * 1408:(c + 1) * 1408]),
            "gus": np.ascontiguousarray(gus[:, c * 1408:(c + 1) * 1408]),
            "dnp": np.ascontiguousarray(dnp[c * 256:(c + 1) * 256]),
            "dns": np.ascontiguousarray(dns[:, c * 256:(c + 1) * 256]),
            "identf": identf, "ident": identb,
        })
    r_dq = run_bass_kernel_spmd(nc, im, core_ids=C)
    perm_gu = np.array([8 * (i % 256) + i // 256 for i in range(2048)])
    perm_dn = np.array([8 * (i % 704) + i // 704 for i in range(5632)])
    inv_gu = np.argsort(perm_gu)
    inv_dn = np.argsort(perm_dn)
    w_guT = np.concatenate([r_dq.results[c]["gut"][inv_gu] for c in C], axis=1)
    w_dnT = np.concatenate([r_dq.results[c]["dnt"][inv_dn] for c in C], axis=1)

    nc = _prog("ffn1", build_ffn1)
    im = []
    for c in C:
        b, ih = c // 2, c % 2
        gsl = slice(ih * 2816, (ih + 1) * 2816)
        usl = slice(5632 + ih * 2816, 5632 + (ih + 1) * 2816)
        im.append({
            "xnf": xnf1[b],
            "wgT": np.ascontiguousarray(w_guT[:, gsl]),
            "wuT": np.ascontiguousarray(w_guT[:, usl]),
            "wdT": np.ascontiguousarray(w_dnT[gsl]),
        })
    r_f1 = run_bass_kernel_spmd(nc, im, core_ids=C)

    nc = _prog("final", build_final)
    im = []
    for c in C:
        b, th = c // 2, c % 2
        sl = slice(th * TLOC, (th + 1) * TLOC)
        im.append({
            "x1a": r_w1.results[c]["xout"],
            "y0": np.ascontiguousarray(r_f1.results[2 * b]["yy"][sl]),
            "y1": np.ascontiguousarray(r_f1.results[2 * b + 1]["yy"][sl]),
        })
    r_fin = run_bass_kernel_spmd(nc, im, core_ids=C)

    out = np.empty((4, 2048, 2048), np.float32)
    for c in C:
        b, th = c // 2, c % 2
        out[b, th * TLOC:(th + 1) * TLOC] = r_fin.results[c]["x2"]
    return out




kernel = kernel_fused


# version 06 (final)
# speedup vs baseline: 1.0993x; 1.0993x over previous
import math
import numpy as np
import ml_dtypes
from contextlib import ExitStack

import concourse.bass as bass
import concourse.mybir as mybir
from concourse.bass_utils import run_bass_kernel_spmd

F32 = mybir.dt.float32
BF16 = mybir.dt.bfloat16
I32 = mybir.dt.int32
U16 = mybir.dt.uint16
AF = mybir.ActivationFunctionType
ALU = mybir.AluOpType

BF16NP = ml_dtypes.bfloat16


ENGINES = ("tensor", "vector", "scalar", "gpsimd", "sync")
DMA_QUEUES = ("sync", "gpsimd", "scalar")


class Op:
    __slots__ = ("engine", "build", "reads", "writes", "seq", "is_dma",
                 "needs_inc", "inc_index", "waits", "tag", "dep_ops")

    def __init__(self, engine, build, reads, writes, seq, is_dma, tag=""):
        self.engine = engine
        self.build = build
        self.reads = reads
        self.writes = writes
        self.seq = seq
        self.is_dma = is_dma
        self.needs_inc = False
        self.inc_index = None
        self.waits = []
        self.tag = tag


class Prog:

    def __init__(self, name="prog"):
        self.nc = bass.Bass("TRN2", target_bir_lowering=False)
        self.stack = ExitStack()
        self.ops = []
        self.last_writer = {}
        self.readers = {}
        self.seq = 0
        self.sb_tensors = {}
        self.ps_tensors = {}
        self.dram = {}
        self._sem_handles = {}
        self.final_dma_ops = []
        self.pfx = ""
        self._alloc_stack = self.stack
        self._last_op_per_engine = {}
        self._barrier_pending = None
        self._barrier_crossed = set()

    def barrier(self):
        self._barrier_pending = dict(self._last_op_per_engine)
        self._barrier_crossed = set()

    def scope(self, pfx):
        from contextlib import contextmanager

        @contextmanager
        def _cm():
            old_stack, old_pfx = self._alloc_stack, self.pfx
            self._alloc_stack = ExitStack()
            self.pfx = pfx
            try:
                yield
            finally:
                self._alloc_stack.close()
                self._alloc_stack, self.pfx = old_stack, old_pfx
                self.barrier()
        return _cm()

    _io = None

    def set_io(self, io):
        self._io = io

    def dram_in(self, name, shape, dtype):
        if self._io is not None:
            assert name in self._io, f"fused io missing input {name}"
            return self._io[name]
        t = self.nc.dram_tensor(name, list(shape), dtype, kind="ExternalInput")
        self.dram[name] = t
        return t.ap()

    def dram_out(self, name, shape, dtype):
        if self._io is not None:
            assert name in self._io, f"fused io missing output {name}"
            return self._io[name]
        t = self.nc.dram_tensor(name, list(shape), dtype, kind="ExternalOutput")
        self.dram[name] = t
        return t.ap()

    def dram_tmp(self, name, shape, dtype):
        t = self.nc.dram_tensor(name, list(shape), dtype, kind="Internal")
        self.dram[name] = t
        return t.ap()

    def sb(self, name, shape, dtype):
        t = self._alloc_stack.enter_context(
            self.nc.sbuf_tensor("sb_" + self.pfx + name, list(shape), dtype))
        self.sb_tensors[self.pfx + name] = t
        return t

    def ps(self, name, shape, dtype=F32):
        t = self._alloc_stack.enter_context(
            self.nc.psum_tensor("ps_" + self.pfx + name, list(shape), dtype))
        self.ps_tensors[self.pfx + name] = t
        return t

    def op(self, engine, build, reads=(), writes=(), tag=""):
        o = Op(engine, build, tuple(reads), tuple(writes), self.seq, is_dma=False, tag=tag)
        self.seq += 1
        self._record(o)
        return o

    def dma(self, queue, build, reads=(), writes=(), tag="", final=False):
        assert queue in DMA_QUEUES
        o = Op(queue, build, tuple(reads), tuple(writes), self.seq, is_dma=True, tag=tag)
        self.seq += 1
        self._record(o)
        if final:
            self.final_dma_ops.append(o)
        return o

    def _record(self, o):
        deps = []
        for r in o.reads:
            w = self.last_writer.get(r)
            if w is not None:
                deps.append(w)
        for w_key in o.writes:
            w = self.last_writer.get(w_key)
            if w is not None:
                deps.append(w)
            deps.extend(self.readers.get(w_key, ()))
        if self._barrier_pending is not None and o.engine not in self._barrier_crossed:
            deps.extend(self._barrier_pending.values())
            self._barrier_crossed.add(o.engine)
        o.dep_ops = [d for d in set(deps) if d is not o]
        for r in o.reads:
            self.readers.setdefault(r, []).append(o)
        for w_key in o.writes:
            self.last_writer[w_key] = o
            self.readers[w_key] = []
        self._last_op_per_engine[o.engine] = o
        self.ops.append(o)

    NDMASEM = 8

    def finalize(self):
        nc = self.nc
        for o in self.ops:
            for d in o.dep_ops:
                if d.engine == "tensor" and o.engine == "tensor":
                    continue
                d.needs_inc = True
        for o in self.final_dma_ops:
            o.needs_inc = True
        counts = {e: 0 for e in ENGINES}
        dma_ord = {e: 0 for e in ENGINES}
        for o in self.ops:
            if o.is_dma:
                k = dma_ord[o.engine]
                dma_ord[o.engine] += 1
                o.inc_index = (k % Prog.NDMASEM, 16 * (k // Prog.NDMASEM + 1))
                o.needs_inc = True
            elif o.needs_inc:
                counts[o.engine] += 1
                o.inc_index = (0, counts[o.engine])
        for o in self.ops:
            wmax = {}
            for d in o.dep_ops:
                if d.engine == "tensor" and o.engine == "tensor":
                    continue
                if d.inc_index is None:
                    continue
                key = (d.engine, d.is_dma, d.inc_index[0])
                wmax[key] = max(wmax.get(key, 0), d.inc_index[1])
            o.waits = sorted(wmax.items())
        fin_waits = {}
        for o in self.final_dma_ops:
            key = (o.engine, True, o.inc_index[0])
            fin_waits[key] = max(fin_waits.get(key, 0), o.inc_index[1])

        sems = {}
        with self.stack:
            for e in ENGINES:
                sems[(e, False, 0)] = self.stack.enter_context(nc.semaphore("sem_" + e))
                if e in DMA_QUEUES:
                    for i in range(Prog.NDMASEM):
                        sems[(e, True, i)] = self.stack.enter_context(
                            nc.semaphore(f"dsem_{e}_{i}"))
            block = self.stack.enter_context(nc.Block())
            per_engine = {e: [o for o in self.ops if o.engine == e] for e in ENGINES}

            def emit_stream(eng_handle, eng_name):
                waited = {}
                for o in per_engine[eng_name]:
                    for (key, v) in o.waits:
                        if waited.get(key, 0) >= v:
                            continue
                        eng_handle.wait_ge(sems[key], v)
                        waited[key] = v
                    ins = o.build(eng_handle)
                    if o.is_dma:
                        ins.then_inc(sems[(o.engine, True, o.inc_index[0])], 16)
                    elif o.needs_inc:
                        ins.then_inc(sems[(o.engine, False, 0)], 1)
                if eng_name == "sync":
                    for (key, v) in sorted(fin_waits.items()):
                        if waited.get(key, 0) < v:
                            eng_handle.wait_ge(sems[key], v)

            for e in ENGINES:
                sec = getattr(block, e)

                def make(e=e):
                    @sec
                    def _(h):
                        emit_stream(h, e)
                make()
        return nc


B, T, D = 4, 2048, 2048
NCORES = 8
TLOC = 1024
LN_EPS = 1e-5


def _bf16(x):
    return np.ascontiguousarray(x.astype(BF16NP))


def emit_ln(p, n_tok=TLOC, d=D, in_dtype=F32):
    x = p.dram_in("x", [n_tok, d], in_dtype)
    xn = p.dram_out("xn", [n_tok, d], BF16)
    nchunk = n_tok // 128
    xt = p.sb("xt", [128, d], F32)
    sq = p.sb("sq", [128, d], F32)
    xnt = p.sb("xnt", [128, d], BF16)
    mean = p.sb("mean", [128, 1], F32)
    var = p.sb("var", [128, 1], F32)
    rstd = p.sb("rstd", [128, 1], F32)
    nmean = p.sb("nmean", [128, 1], F32)
    for c in range(nchunk):
        xk = f"x{c}"
        p.dma("sync", lambda h, c=c: h.dma_start(xt[:], x[c * 128:(c + 1) * 128, :]),
              reads=(), writes=(xk,), tag=f"ld{c}")
        p.op("vector", lambda h: h.reduce_sum(mean[:], xt[:], axis=mybir.AxisListType.X),
             reads=(xk,), writes=("mean",))
        p.op("vector", lambda h: h.tensor_scalar(nmean[:], mean[:], -1.0 / d, None, ALU.mult),
             reads=("mean",), writes=("nmean",))
        p.op("vector", lambda h: h.tensor_scalar(xt[:], xt[:], nmean[:], None, ALU.add),
             reads=(xk, "nmean"), writes=(xk,))
        p.op("scalar", lambda h: h.activation(sq[:], xt[:], AF.Square),
             reads=(xk,), writes=("sq",))
        p.op("vector", lambda h: h.reduce_sum(var[:], sq[:], axis=mybir.AxisListType.X),
             reads=("sq",), writes=("var",))
        p.op("vector", lambda h: h.tensor_scalar(var[:], var[:], 1.0 / d, LN_EPS, ALU.mult, ALU.add),
             reads=("var",), writes=("var",))
        p.op("scalar", lambda h: h.activation(rstd[:], var[:], AF.Ln),
             reads=("var",), writes=("rstd",))
        p.op("scalar", lambda h: h.activation(rstd[:], rstd[:], AF.Exp, scale=-0.5),
             reads=("rstd",), writes=("rstd",))
        p.op("vector", lambda h: h.tensor_scalar(xnt[:], xt[:], rstd[:], None, ALU.mult),
             reads=(xk, "rstd"), writes=("xn",))
        p.dma("sync", lambda h, c=c: h.dma_start(xn[c * 128:(c + 1) * 128, :], xnt[:]),
              reads=("xn",), writes=(), tag=f"st{c}", final=True)
    return None


def build_ln(n_tok=TLOC, d=D, in_dtype=F32):
    p = Prog("ln")
    with p.scope('ln_'):
        emit_ln(p, n_tok, d, in_dtype)
    return p.finalize()


def emit_proj0(p):
    X = mybir.AxisListType.X
    xnf = p.dram_in("xnf", [128, 16 * 2048], BF16)
    wq_d = p.dram_in("wq", [2048, 1024], F32)
    wk_d = p.dram_in("wk", [2048, 256], F32)
    wv_d = p.dram_in("wv", [2048, 256], F32)
    wb_d = p.dram_in("wb", [2048, 16], F32)
    bb_d = p.dram_in("bb", [16, 1], F32)
    qnw_d = p.dram_in("qnw", [1, 64], F32)
    knw_d = p.dram_in("knw", [1, 64], F32)
    cos_d = p.dram_in("cosr", [2048, 256], F32)
    sin_d = p.dram_in("sinr", [2048, 256], F32)
    id_d = p.dram_in("ident", [128, 128], BF16)
    qa_d = p.dram_out("qa", [16, 65, 2048], BF16)
    ka_d = p.dram_out("ka", [4, 66, 2048], BF16)
    v_d = p.dram_out("vv", [2048, 256], BF16)
    beta_d = p.dram_out("beta", [16, 2048], F32)

    xn = p.sb("xn", [128, 16 * 2048], BF16)
    wq = p.sb("wq", [128, 16 * 1024], BF16)
    wk = p.sb("wk", [128, 16 * 256], BF16)
    wv = p.sb("wv", [128, 16 * 256], BF16)
    wb = p.sb("wb", [128, 16 * 16], BF16)
    bbt = p.sb("bbt", [16, 1], F32)
    ident = p.sb("ident", [128, 128], BF16)
    cosr = p.sb("cosr", [128, 16 * 256], F32)
    sinr = p.sb("sinr", [128, 16 * 256], F32)
    qwrow = p.sb("qwrow", [1, 512], F32)
    kwrow = p.sb("kwrow", [1, 256], F32)
    onesc = p.sb("onesc", [1, 128], BF16)
    qwB = p.sb("qwB", [128, 512], F32)
    kwB = p.sb("kwB", [128, 256], F32)

    p.dma("sync", lambda h: h.dma_start(xn[:], xnf), writes=("xn",))
    p.dma("gpsimd", lambda h: h.dma_start(
        wq[:].rearrange("p (c n) -> p c n", c=16), wq_d.rearrange("(c p) n -> p c n", p=128)), writes=("wq",))
    p.dma("gpsimd", lambda h: h.dma_start(
        wk[:].rearrange("p (c n) -> p c n", c=16), wk_d.rearrange("(c p) n -> p c n", p=128)), writes=("wk",))
    p.dma("gpsimd", lambda h: h.dma_start(
        wv[:].rearrange("p (c n) -> p c n", c=16), wv_d.rearrange("(c p) n -> p c n", p=128)), writes=("wv",))
    p.dma("gpsimd", lambda h: h.dma_start(
        wb[:].rearrange("p (c n) -> p c n", c=16), wb_d.rearrange("(c p) n -> p c n", p=128)), writes=("wb",))
    p.dma("gpsimd", lambda h: h.dma_start(bbt[:], bb_d), writes=("bbt",))
    p.dma("sync", lambda h: h.dma_start(ident[:], id_d), writes=("ident",))
    p.dma("sync", lambda h: h.dma_start(
        cosr[:].rearrange("p (c n) -> p c n", c=16), cos_d.rearrange("(c p) n -> p c n", p=128)), writes=("cosr",))
    p.dma("sync", lambda h: h.dma_start(
        sinr[:].rearrange("p (c n) -> p c n", c=16), sin_d.rearrange("(c p) n -> p c n", p=128)), writes=("sinr",))
    for r in range(8):
        p.dma("sync", lambda h, r=r: h.dma_start(qwrow[:, 64 * r:64 * r + 64], qnw_d),
              writes=("qwrow",))
    for r in range(4):
        p.dma("sync", lambda h, r=r: h.dma_start(kwrow[:, 64 * r:64 * r + 64], knw_d),
              writes=("kwrow",))
    p.op("vector", lambda h: h.memset(onesc[:], 1.0), writes=("onesc",))
    p.op("vector", lambda h: h.tensor_scalar(qwrow[:], qwrow[:], 0.125, None, ALU.mult),
         reads=("qwrow",), writes=("qwrow",))
    qwrb = p.sb("qwrb", [1, 512], BF16)
    kwrb = p.sb("kwrb", [1, 256], BF16)
    p.op("vector", lambda h: h.tensor_copy(qwrb[:], qwrow[:]), reads=("qwrow",), writes=("qwrb",))
    p.op("vector", lambda h: h.tensor_copy(kwrb[:], kwrow[:]), reads=("kwrow",), writes=("kwrb",))
    psB = p.ps("psB", [128, 512], F32)
    p.op("tensor", lambda h: h.matmul(psB[:], onesc[:], qwrb[:], start=True, stop=True),
         reads=("onesc", "qwrb"), writes=("psB",))
    p.op("vector", lambda h: h.tensor_copy(qwB[:], psB[:]), reads=("psB",), writes=("qwB",))
    p.op("tensor", lambda h: h.matmul(psB[:, 0:256], onesc[:], kwrb[:], start=True, stop=True),
         reads=("onesc", "kwrb"), writes=("psB",))
    p.op("vector", lambda h: h.tensor_copy(kwB[:], psB[:, 0:256]), reads=("psB",), writes=("kwB",))

    ps_a = p.ps("ps_a", [128, 512], F32)
    ps_t = p.ps("ps_t", [128, 128], BF16)
    sq = p.sb("sqt", [128, 512], F32)
    cur = p.sb("cur", [128, 512], F32)
    ta = p.sb("ta", [128, 256], F32)
    tb = p.sb("tb", [128, 256], F32)
    tc2t = p.sb("tc2", [128, 256], F32)
    td = p.sb("td", [128, 256], F32)
    rst = p.sb("rst", [128, 8], F32)
    nrm = p.sb("nrm", [128, 8], F32)
    mlt = p.sb("mlt", [128, 8], F32)
    aug = p.sb("aug", [128, 8 * 66], BF16)
    tpo = p.sb("tpo", [128, 8 * 128], BF16)
    ktn = p.sb("ktn", [128, 64], BF16)
    tr1 = p.sb("tr1", [64, 128], BF16)
    colm = p.sb("colm", [64, 1], BF16)
    rowm = p.sb("rowm", [1, 64], F32)
    kbar2 = p.sb("kbar2", [1, 4], F32)
    kbarb = p.sb("kbarb", [1, 16], BF16)
    kbarB = p.sb("kbarB", [128, 16], F32)
    vsb = p.sb("vsb", [128, 256], BF16)
    beta_s = p.sb("beta_s", [16, 2048], F32)
    ps_beta = p.ps("ps_beta", [16, 512], F32)

    def rope_views(t, nh, width):
        x1 = bass.AP(t, 0, [[width, 128], [64, nh], [1, 32]])
        x2 = bass.AP(t, 32, [[width, 128], [64, nh], [1, 32]])
        return x1, x2

    def cs_views(nh, tcc):
        c = bass.AP(cosr, tcc * 256, [[16 * 256, 128], [32, nh], [1, 32]])
        s = bass.AP(sinr, tcc * 256, [[16 * 256, 128], [32, nh], [1, 32]])
        return c, s

    def tmp_views(nh):
        return (bass.AP(ta, 0, [[256, 128], [32, nh], [1, 32]]),
                bass.AP(tb, 0, [[256, 128], [32, nh], [1, 32]]),
                bass.AP(tc2t, 0, [[256, 128], [32, nh], [1, 32]]),
                bass.AP(td, 0, [[256, 128], [32, nh], [1, 32]]))

    def rms_rope(nh, tcc, tag, wB, wBkey):
        fw = nh * 64
        p.op("scalar", lambda h: h.activation(sq[:, 0:fw], ps_a[:, 0:fw], AF.Square),
             reads=(tag + "_ps",), writes=(tag + "_sq",))
        for hh in range(nh):
            p.op("vector", lambda h, hh=hh: h.reduce_sum(
                nrm[:, hh:hh + 1], sq[:, 64 * hh:64 * hh + 64], axis=X),
                 reads=(tag + "_sq",), writes=(tag + "_nrm",))
        p.op("vector", lambda h: h.tensor_scalar(
            rst[:, 0:nh], nrm[:, 0:nh], 1.0 / 64, 1e-6, ALU.mult, ALU.add),
             reads=(tag + "_nrm",), writes=(tag + "_rst",))
        p.op("scalar", lambda h: h.activation(rst[:, 0:nh], rst[:, 0:nh], AF.Ln),
             reads=(tag + "_rst",), writes=(tag + "_rst",))
        p.op("scalar", lambda h: h.activation(rst[:, 0:nh], rst[:, 0:nh], AF.Exp, scale=-0.5),
             reads=(tag + "_rst",), writes=(tag + "_rst",))
        for hh in range(nh):
            p.op("vector", lambda h, hh=hh: h.tensor_scalar(
                cur[:, 64 * hh:64 * hh + 64], ps_a[:, 64 * hh:64 * hh + 64],
                rst[:, hh:hh + 1], None, ALU.mult),
                 reads=(tag + "_ps", tag + "_rst"), writes=(tag + "_cur",))
        p.op("vector", lambda h: h.tensor_tensor(cur[:, 0:fw], cur[:, 0:fw], wB[:, 0:fw], ALU.mult),
             reads=(tag + "_cur", wBkey), writes=(tag + "_cur",))
        x1, x2 = rope_views(cur, nh, 512)
        cv, sv = cs_views(nh, tcc)
        tav, tbv, tcv, tdv = tmp_views(nh)
        p.op("vector", lambda h: h.tensor_tensor(tav, x1, cv, ALU.mult),
             reads=(tag + "_cur", "cosr"), writes=(tag + "_ta",))
        p.op("vector", lambda h: h.tensor_tensor(tbv, x1, sv, ALU.mult),
             reads=(tag + "_cur", "sinr"), writes=(tag + "_tb",))
        p.op("vector", lambda h: h.tensor_tensor(tcv, x2, sv, ALU.mult),
             reads=(tag + "_cur", "sinr"), writes=(tag + "_tc",))
        p.op("vector", lambda h: h.tensor_tensor(tdv, x2, cv, ALU.mult),
             reads=(tag + "_cur", "cosr"), writes=(tag + "_td",))
        p.op("vector", lambda h: h.tensor_tensor(x1, tav, tcv, ALU.subtract),
             reads=(tag + "_ta", tag + "_tc"), writes=(tag + "_cur",))
        p.op("vector", lambda h: h.tensor_tensor(x2, tbv, tdv, ALU.add),
             reads=(tag + "_tb", tag + "_td"), writes=(tag + "_cur",))
        p.op("scalar", lambda h: h.activation(sq[:, 0:fw], cur[:, 0:fw], AF.Square),
             reads=(tag + "_cur",), writes=(tag + "_sq",))
        for hh in range(nh):
            p.op("vector", lambda h, hh=hh: h.reduce_sum(
                nrm[:, hh:hh + 1], sq[:, 64 * hh:64 * hh + 64], axis=X),
                 reads=(tag + "_sq",), writes=(tag + "_nrm",))

    p.op("gpsimd", lambda h: h.memset(aug[:], 1.0), writes=("aug_ones",))
    for tcc in range(16):
        for kc in range(16):
            p.op("tensor", lambda h, kc=kc, tcc=tcc: h.matmul(
                ps_a[:, 0:256],
                xn[:, kc * 2048 + tcc * 128: kc * 2048 + tcc * 128 + 128],
                wk[:, kc * 256: kc * 256 + 256],
                start=(kc == 0), stop=(kc == 15)),
                 reads=("xn", "wk"), writes=("k_ps",))
        rms_rope(4, tcc, "k", kwB, "kwB")
        for hh in range(4):
            p.op("vector", lambda h, hh=hh, tcc=tcc: h.tensor_copy(
                ktn[:, hh * 16 + tcc: hh * 16 + tcc + 1], nrm[:, hh:hh + 1]),
                 reads=("k_nrm",), writes=("ktn",))
        for hh in range(4):
            p.op("vector", lambda h, hh=hh: h.tensor_copy(
                aug[:, 66 * hh:66 * hh + 64], cur[:, 64 * hh:64 * hh + 64]),
                 reads=("k_cur", "aug_ones"), writes=("k_aug",))
            p.op("vector", lambda h, hh=hh: h.tensor_copy(
                aug[:, 66 * hh + 65:66 * hh + 66], nrm[:, hh:hh + 1]),
                 reads=("k_nrm",), writes=("k_aug",))
        for hh in range(4):
            p.op("tensor", lambda h, hh=hh: h.transpose(
                ps_t[0:66, :], aug[:, 66 * hh:66 * hh + 66], ident[:]),
                 reads=("k_aug", "ident", "ps_t_free"), writes=("ps_t",))
            p.op("vector", lambda h, hh=hh: h.tensor_copy(
                tpo[0:66, 128 * hh:128 * hh + 128], ps_t[0:66, :]),
                 reads=("ps_t",), writes=("k_tpo", "ps_t_free"))
        p.dma("sync", lambda h, tcc=tcc: h.dma_start(
            ka_d[:, :, tcc * 128:tcc * 128 + 128].transpose([1, 0, 2]),
            bass.AP(tpo, 0, [[8 * 128, 66], [128, 4], [1, 128]])),
              reads=("k_tpo",), tag=f"ka{tcc}", final=True)
    p.op("tensor", lambda h: h.transpose(ps_t[0:64, :], ktn[:], ident[:]),
         reads=("ktn", "ident", "ps_t_free"), writes=("ps_t",))
    p.op("vector", lambda h: h.tensor_copy(tr1[:], ps_t[0:64, :]),
         reads=("ps_t",), writes=("tr1", "ps_t_free"))
    p.op("vector", lambda h: h.reduce_max(colm[:], tr1[:], axis=X),
         reads=("tr1",), writes=("colm",))
    p.op("tensor", lambda h: h.transpose(ps_t[0:1, 0:64], colm[:], ident[0:64, 0:64]),
         reads=("colm", "ident", "ps_t_free"), writes=("ps_t",))
    p.op("vector", lambda h: h.tensor_copy(rowm[:], ps_t[0:1, 0:64]),
         reads=("ps_t",), writes=("rowm", "ps_t_free"))
    for hh in range(4):
        p.op("vector", lambda h, hh=hh: h.reduce_max(
            kbar2[:, hh:hh + 1], rowm[:, 16 * hh:16 * hh + 16], axis=X),
             reads=("rowm",), writes=("kbar2",))
    p.op("scalar", lambda h: h.activation(kbar2[:], kbar2[:], AF.Ln),
         reads=("kbar2",), writes=("kbar2",))
    p.op("scalar", lambda h: h.activation(kbar2[:], kbar2[:], AF.Exp, scale=0.5),
         reads=("kbar2",), writes=("kbar2",))
    for hh in range(16):
        p.op("vector", lambda h, hh=hh: h.tensor_copy(
            kbarb[:, hh:hh + 1], kbar2[:, hh // 4:hh // 4 + 1]),
             reads=("kbar2",), writes=("kbarb",))
    p.op("tensor", lambda h: h.matmul(psB[:, 0:16], onesc[:], kbarb[:], start=True, stop=True),
         reads=("onesc", "kbarb"), writes=("psB",))
    p.op("vector", lambda h: h.tensor_copy(kbarB[:], psB[:, 0:16]), reads=("psB",), writes=("kbarB",))

    for tcc in range(16):
        for kc in range(16):
            p.op("tensor", lambda h, kc=kc, tcc=tcc: h.matmul(
                ps_a[:, 0:256],
                xn[:, kc * 2048 + tcc * 128: kc * 2048 + tcc * 128 + 128],
                wv[:, kc * 256: kc * 256 + 256],
                start=(kc == 0), stop=(kc == 15)),
                 reads=("xn", "wv"), writes=("k_ps",))
        p.op("vector", lambda h: h.tensor_copy(vsb[:], ps_a[:, 0:256]),
             reads=("k_ps",), writes=("vsb",))
        p.dma("sync", lambda h, tcc=tcc: h.dma_start(
            v_d[tcc * 128:tcc * 128 + 128, :], vsb[:]),
              reads=("vsb",), tag=f"v{tcc}", final=True)

    for ft in range(4):
        for kc in range(16):
            p.op("tensor", lambda h, kc=kc, ft=ft: h.matmul(
                ps_beta[:],
                wb[:, kc * 16: kc * 16 + 16],
                xn[:, kc * 2048 + ft * 512: kc * 2048 + ft * 512 + 512],
                start=(kc == 0), stop=(kc == 15)),
                 reads=("xn", "wb"), writes=("b_ps",))
        p.op("vector", lambda h, ft=ft: h.tensor_scalar(
            beta_s[:, ft * 512:ft * 512 + 512], ps_beta[:], bbt[:], None, ALU.add),
             reads=("b_ps", "bbt"), writes=("beta_s",))
    p.dma("sync", lambda h: h.dma_start(beta_d, beta_s[:]),
          reads=("beta_s",), tag="beta", final=True)

    for tcc in range(16):
        for g in range(2):
            for kc in range(16):
                p.op("tensor", lambda h, kc=kc, tcc=tcc, g=g: h.matmul(
                    ps_a[:],
                    xn[:, kc * 2048 + tcc * 128: kc * 2048 + tcc * 128 + 128],
                    wq[:, kc * 1024 + g * 512: kc * 1024 + g * 512 + 512],
                    start=(kc == 0), stop=(kc == 15)),
                     reads=("xn", "wq"), writes=("k_ps",))
            rms_rope(8, tcc, "k", qwB, "qwB")
            p.op("scalar", lambda h: h.activation(mlt[:, 0:8], nrm[:, 0:8], AF.Ln),
                 reads=("k_nrm",), writes=("mlt",))
            p.op("scalar", lambda h: h.activation(mlt[:, 0:8], mlt[:, 0:8], AF.Exp, scale=0.5),
                 reads=("mlt",), writes=("mlt",))
            p.op("vector", lambda h, g=g: h.scalar_tensor_tensor(
                mlt[:, 0:8], mlt[:, 0:8], -1.0, kbarB[:, 8 * g:8 * g + 8], ALU.mult, ALU.mult),
                 reads=("mlt", "kbarB"), writes=("mlt",))
            for hh in range(8):
                p.op("vector", lambda h, hh=hh: h.tensor_copy(
                    aug[:, 66 * hh:66 * hh + 64], cur[:, 64 * hh:64 * hh + 64]),
                     reads=("k_cur",), writes=("k_aug",))
                p.op("vector", lambda h, hh=hh: h.tensor_copy(
                    aug[:, 66 * hh + 64:66 * hh + 65], mlt[:, hh:hh + 1]),
                     reads=("mlt",), writes=("k_aug",))
            for hh in range(8):
                p.op("tensor", lambda h, hh=hh: h.transpose(
                    ps_t[0:65, :], aug[:, 66 * hh:66 * hh + 65], ident[:]),
                     reads=("k_aug", "ident", "ps_t_free"), writes=("ps_t",))
                p.op("vector", lambda h, hh=hh: h.tensor_copy(
                    tpo[0:65, 128 * hh:128 * hh + 128], ps_t[0:65, :]),
                     reads=("ps_t",), writes=("k_tpo", "ps_t_free"))
            p.dma("sync", lambda h, tcc=tcc, g=g: h.dma_start(
                qa_d[8 * g:8 * g + 8, :, tcc * 128:tcc * 128 + 128].transpose([1, 0, 2]),
                bass.AP(tpo, 0, [[8 * 128, 65], [128, 8], [1, 128]])),
                  reads=("k_tpo",), tag=f"qa{tcc}_{g}", final=True)
    return None


def build_proj0():
    p = Prog("proj0")
    with p.scope('pro_'):
        emit_proj0(p)
    return p.finalize()


def rope_tables():
    pos = np.arange(T, dtype=np.float64)
    dim_t = 10000.0 ** (2.0 * np.arange(32, dtype=np.float64) / 64)
    ang = pos[:, None] / dim_t[None, :]
    cos = np.cos(ang).astype(np.float32)
    sin = np.sin(ang).astype(np.float32)
    cosr = np.tile(cos, (1, 8))
    sinr = np.tile(sin, (1, 8))
    return np.ascontiguousarray(cosr), np.ascontiguousarray(sinr)


def emit_attn0(p):
    qa_d = p.dram_in("qa", [16, 65, 2048], BF16)
    ka_d = p.dram_in("ka", [4, 66, 2048], BF16)
    v_d = p.dram_in("vv", [2048, 256], BF16)
    beta_d = p.dram_in("beta", [16, 2048], F32)
    id_d = p.dram_in("ident", [128, 128], BF16)
    idf_d = p.dram_in("identf", [128, 128], F32)
    sm_d = p.dram_in("smask", [128, 128], BF16)
    smt_d = p.dram_in("smaskt", [128, 128], BF16)
    cm_d = p.dram_in("cmask", [128, 128], BF16)
    oo_d = p.dram_out("oo", [16, 64, 2048], BF16)

    identf = p.sb("identf", [128, 128], F32)
    smask = p.sb("smask", [128, 128], BF16)
    smaskt = p.sb("smaskt", [128, 128], BF16)
    cmask = p.sb("cmask", [128, 128], BF16)
    ka = p.sb("ka", [66, 4 * 2048], BF16)
    vsb = p.sb("vsb", [128, 16 * 256], BF16)
    onesr = p.sb("onesr", [1, 128], BF16)
    onescol = p.sb("onescol", [128, 1], BF16)
    ones1 = p.sb("ones1", [1, 1], BF16)
    p.dma("sync", lambda h: h.dma_start(identf[:], idf_d), writes=("identf",))
    p.dma("sync", lambda h: h.dma_start(smask[:], sm_d), writes=("smask",))
    p.dma("sync", lambda h: h.dma_start(smaskt[:], smt_d), writes=("smaskt",))
    p.dma("sync", lambda h: h.dma_start(cmask[:], cm_d), writes=("cmask",))
    p.dma("sync", lambda h: h.dma_start(
        ka[:].rearrange("p (c n) -> p c n", c=4), ka_d.transpose([1, 0, 2])), writes=("ka",))
    p.dma("sync", lambda h: h.dma_start(
        vsb[:].rearrange("p (c n) -> p c n", c=16),
        v_d.rearrange("(c p) n -> p c n", p=128)), writes=("vsb",))
    p.op("vector", lambda h: h.memset(onesr[:], 1.0), writes=("onesr",))
    p.op("vector", lambda h: h.memset(onescol[:], 1.0), writes=("onescol",))
    p.op("vector", lambda h: h.memset(ones1[:], 1.0), writes=("ones1",))

    S = []
    for s in range(2):
        d = {}
        d["qs"] = p.sb(f"qs{s}", [65, 2048], BF16)
        d["brow"] = p.sb(f"brow{s}", [1, 2048], F32)
        d["ubuf"] = p.sb(f"ubuf{s}", [128, 16 * 65], BF16)
        d["o_sb"] = p.sb(f"o_sb{s}", [64, 2048], BF16)
        d["e_sb"] = [p.sb(f"e_sb{s}_{k}", [128, 128], BF16) for k in range(2)]
        d["e_st"] = p.sb(f"e_st{s}", [128, 128], BF16)
        d["e_ca"] = p.sb(f"e_ca{s}", [128, 128], BF16)
        d["e2t"] = p.sb(f"e2t{s}", [128, 128], BF16)
        d["zb"] = [p.sb(f"zb{s}_{k}", [128, 128], BF16) for k in range(2)]
        d["yb"] = [p.sb(f"yb{s}_{k}", [128, 128], BF16) for k in range(2)]
        d["psb"] = p.sb(f"psb{s}", [65, 256], F32)
        d["row"] = p.sb(f"row{s}", [1, 128], F32)
        d["mbsb"] = p.sb(f"mbsb{s}", [1, 128], BF16)
        d["mbscol"] = p.sb(f"mbscol{s}", [128, 1], F32)
        d["wtile"] = p.sb(f"wtile{s}", [128, 64], F32)
        d["tp"] = [p.sb(f"tp{s}_{k}", [128, 64], BF16) for k in range(2)]
        d["screcb"] = p.sb(f"screcb{s}", [1, 128], BF16)
        d["rb"] = p.sb(f"rb{s}", [64, 128], BF16)
        d["ps_P"] = p.ps(f"psP{s}", [65, 256], F32)
        d["ps_e"] = p.ps(f"pse{s}", [128, 128], F32)
        d["ps_u"] = p.ps(f"psu{s}", [128, 64], F32)
        d["ps_m"] = p.ps(f"psm{s}", [128, 128], F32)
        S.append(d)

    class Rec:
        def __init__(self):
            self.q = []

        def op(self, *a, **k):
            self.q.append(lambda: p.op(*a, **k))

        def dma(self, *a, **k):
            self.q.append(lambda: p.dma(*a, **k))

    def emit_block(r, s, hh, i):
        d = S[s]
        kv = hh // 4
        K = lambda nm: f"{nm}_{s}"
        qsl = d["qs"][:, i * 128:i * 128 + 128]
        for j in range(i):
            eb = d["e_sb"][j % 2]
            ek = K(f"esb{j % 2}")
            r.op("tensor", lambda h, j=j, kv=kv, d=d, qsl=qsl: h.matmul(
                d["ps_e"][:], ka[0:65, kv * 2048 + j * 128: kv * 2048 + j * 128 + 128],
                qsl, start=True, stop=True),
                 reads=("ka", K("qs")), writes=(K("pse"),))
            r.op("scalar", lambda h, eb=eb, d=d: h.activation(eb[:], d["ps_e"][:], AF.Exp),
                 reads=(K("pse"),), writes=(ek,))
            r.op("tensor", lambda h, j=j, eb=eb, d=d: h.matmul(
                d["ps_P"][0:65, 0:128], d["ubuf"][:, 65 * j: 65 * j + 65], eb[:],
                start=(j == 0), stop=False),
                 reads=(K("ubuf"), ek), writes=(K("psP"),))
        eb = d["e_sb"][i % 2]
        ek = K(f"esb{i % 2}")
        r.op("tensor", lambda h, i=i, kv=kv, d=d, qsl=qsl: h.matmul(
            d["ps_e"][:], ka[0:65, kv * 2048 + i * 128: kv * 2048 + i * 128 + 128],
            qsl, start=True, stop=True),
             reads=("ka", K("qs")), writes=(K("pse"),))
        r.op("scalar", lambda h, eb=eb, d=d: h.activation(eb[:], d["ps_e"][:], AF.Exp),
             reads=(K("pse"),), writes=(ek,))
        r.op("vector", lambda h, eb=eb, d=d: h.tensor_tensor(d["e_st"][:], eb[:], smask[:], ALU.mult),
             reads=(ek, "smask"), writes=(K("est"),))
        r.op("vector", lambda h, eb=eb, d=d: h.tensor_tensor(d["e_ca"][:], eb[:], cmask[:], ALU.mult),
             reads=(ek, "cmask"), writes=(K("eca"),))
        r.op("tensor", lambda h, i=i, d=d: h.matmul(
            d["ps_P"][64:65, 128:256], onescol[:], d["e_st"][:], start=(i == 0), stop=False),
             reads=("onescol", K("est")), writes=(K("psP"),))
        r.op("vector", lambda h, d=d: h.tensor_copy(d["psb"][:], d["ps_P"][:]),
             reads=(K("psP"),), writes=(K("psb"),))
        if i > 0:
            r.op("vector", lambda h, d=d: h.tensor_tensor(d["row"][:], d["psb"][64:65, 0:128], d["psb"][64:65, 128:256], ALU.add),
                 reads=(K("psb"),), writes=(K("row"),))
        else:
            r.op("vector", lambda h, d=d: h.tensor_copy(d["row"][:], d["psb"][64:65, 128:256]),
                 reads=(K("psb"),), writes=(K("row"),))
        r.op("vector", lambda h, d=d: h.tensor_scalar(d["row"][:], d["row"][:], 1e-20, None, ALU.max),
             reads=(K("row"),), writes=(K("row"),))
        r.op("vector", lambda h, d=d: h.reciprocal(d["row"][:], d["row"][:]),
             reads=(K("row"),), writes=(K("row"),))
        r.op("vector", lambda h, d=d, i=i: h.scalar_tensor_tensor(
            d["row"][:], d["row"][:], -1.0, d["brow"][:, i * 128:i * 128 + 128], ALU.mult, ALU.mult),
             reads=(K("row"), K("brow")), writes=(K("row"),))
        r.op("vector", lambda h, d=d: h.tensor_copy(d["mbsb"][:], d["row"][:]),
             reads=(K("row"),), writes=(K("mbsb"),))
        r.op("tensor", lambda h, d=d: h.matmul(
            d["ps_u"][:, 0:1], d["mbsb"][:], ones1[:], start=True, stop=True),
             reads=(K("mbsb"), "ones1"), writes=(K("psu"),))
        r.op("vector", lambda h, d=d: h.tensor_copy(d["mbscol"][:], d["ps_u"][:, 0:1]),
             reads=(K("psu"),), writes=(K("mbscol"),))
        if i > 0:
            r.op("tensor", lambda h, d=d: h.transpose(
                d["ps_m"][:, 0:64], d["psb"][0:64, 0:128], identf[0:64, 0:64]),
                 reads=(K("psb"), "identf"), writes=(K("psm"),))
            r.op("vector", lambda h, d=d, i=i, kv=kv: h.scalar_tensor_tensor(
                d["wtile"][:], d["ps_m"][:, 0:64], d["mbscol"][:],
                vsb[:, i * 256 + kv * 64: i * 256 + kv * 64 + 64], ALU.mult, ALU.add),
                 reads=(K("psm"), K("mbscol"), "vsb"), writes=(K("wtile"),))
        else:
            r.op("vector", lambda h, d=d, i=i, kv=kv: h.tensor_copy(
                d["wtile"][:], vsb[:, i * 256 + kv * 64: i * 256 + kv * 64 + 64]),
                 reads=("vsb",), writes=(K("wtile"),))
        r.op("vector", lambda h, d=d: h.tensor_copy(d["tp"][0][:], d["wtile"][:]),
             reads=(K("wtile"),), writes=(K("tp0"),))
        r.op("tensor", lambda h, d=d: h.matmul(d["ps_m"][:], onesr[:], d["mbsb"][:],
                                               start=True, stop=True),
             reads=("onesr", K("mbsb"), K("psm")), writes=(K("psm"),))
        r.op("vector", lambda h, d=d: h.tensor_tensor(d["zb"][0][:], d["e_st"][:], d["ps_m"][:], ALU.mult),
             reads=(K("est"), K("psm")), writes=(K("zb0"),))
        r.op("tensor", lambda h, i=i, kv=kv, d=d, qsl=qsl: h.matmul(
            d["ps_e"][:], qsl, ka[0:65, kv * 2048 + i * 128: kv * 2048 + i * 128 + 128],
            start=True, stop=True),
             reads=("ka", K("qs")), writes=(K("pse"),))
        r.op("scalar", lambda h, d=d: h.activation(d["e2t"][:], d["ps_e"][:], AF.Exp),
             reads=(K("pse"),), writes=(K("e2t"),))
        r.op("vector", lambda h, d=d: h.scalar_tensor_tensor(
            d["yb"][0][:], d["e2t"][:], d["mbscol"][:], smaskt[:], ALU.mult, ALU.mult),
             reads=(K("e2t"), K("mbscol"), "smaskt"), writes=(K("yb0"),))
        for lev in range(7):
            zc, zn = d["zb"][lev % 2], d["zb"][(lev + 1) % 2]
            yc, yn = d["yb"][lev % 2], d["yb"][(lev + 1) % 2]
            tc_, tn = d["tp"][lev % 2], d["tp"][(lev + 1) % 2]
            kzc, kzn = K(f"zb{lev % 2}"), K(f"zb{(lev + 1) % 2}")
            kyc, kyn = K(f"yb{lev % 2}"), K(f"yb{(lev + 1) % 2}")
            ktc, ktn = K(f"tp{lev % 2}"), K(f"tp{(lev + 1) % 2}")
            r.op("tensor", lambda h, d=d, zc=zc, tc_=tc_: h.matmul(
                d["ps_u"][:], zc[:], tc_[:], start=True, stop=True),
                 reads=(kzc, ktc), writes=(K("psu"),))
            r.op("vector", lambda h, d=d, tn=tn, tc_=tc_: h.tensor_tensor(
                tn[:], d["ps_u"][:], tc_[:], ALU.add),
                 reads=(K("psu"), ktc), writes=(ktn,))
            if lev < 6:
                r.op("tensor", lambda h, d=d, yc=yc, zc=zc: h.matmul(
                    d["ps_m"][:], yc[:], zc[:], start=True, stop=True),
                     reads=(kyc, kzc, K("psm")), writes=(K("psm"),))
                r.op("scalar", lambda h, d=d, zn=zn: h.copy(zn[:], d["ps_m"][:]),
                     reads=(K("psm"),), writes=(kzn,))
                if lev < 5:
                    r.op("tensor", lambda h, d=d, zc=zc, yc=yc: h.matmul(
                        d["ps_m"][:], zc[:], yc[:], start=True, stop=True),
                         reads=(kzc, kyc, K("psm")), writes=(K("psm"),))
                    r.op("scalar", lambda h, d=d, yn=yn: h.copy(yn[:], d["ps_m"][:]),
                         reads=(K("psm"),), writes=(kyn,))
        r.op("vector", lambda h, d=d, i=i: h.tensor_copy(
            d["ubuf"][:, 65 * i:65 * i + 64], d["tp"][1][:]),
             reads=(K("tp1"),), writes=(K("ubuf"),))
        r.op("tensor", lambda h, d=d, i=i: h.matmul(
            d["ps_P"][0:65, 0:128], d["ubuf"][:, 65 * i:65 * i + 65], d["e_ca"][:],
            start=(i == 0), stop=True),
             reads=(K("ubuf"), K("eca")), writes=(K("psP"),))
        r.op("vector", lambda h, d=d: h.tensor_scalar(d["row"][:], d["ps_P"][64:65, 0:128], 1e-20, None, ALU.max),
             reads=(K("psP"),), writes=(K("row"),))
        r.op("vector", lambda h, d=d: h.reciprocal(d["row"][:], d["row"][:]),
             reads=(K("row"),), writes=(K("row"),))
        r.op("vector", lambda h, d=d: h.tensor_copy(d["screcb"][:], d["row"][:]),
             reads=(K("row"),), writes=(K("screcb"),))
        r.op("tensor", lambda h, d=d: h.matmul(
            d["ps_m"][0:64, :], onesr[:, 0:64], d["screcb"][:], start=True, stop=True),
             reads=("onesr", K("screcb"), K("psm")), writes=(K("psm"),))
        r.op("vector", lambda h, d=d: h.tensor_copy(d["rb"][:], d["ps_m"][0:64, :]),
             reads=(K("psm"),), writes=(K("rb"),))
        r.op("vector", lambda h, d=d, i=i: h.tensor_tensor(
            d["o_sb"][:, i * 128:i * 128 + 128], d["ps_P"][0:64, 0:128], d["rb"][:], ALU.mult),
             reads=(K("psP"), K("rb")), writes=(K("osb"),))

    from itertools import zip_longest
    for hp in range(8):
        for s in range(2):
            hh = 2 * hp + s
            d = S[s]
            p.dma("sync", lambda h, hh=hh, d=d: h.dma_start(d["qs"][:], qa_d[hh]),
                  writes=(f"qs_{s}",), tag=f"qld{hh}")
            p.dma("sync", lambda h, hh=hh, d=d: h.dma_start(d["brow"][:], beta_d[hh:hh + 1, :]),
                  writes=(f"brow_{s}",), tag=f"bld{hh}")
            p.op("gpsimd", lambda h, d=d: h.memset(d["ubuf"][:], 1.0), writes=(f"ubuf_{s}",))
        for i in range(16):
            ra, rb_ = Rec(), Rec()
            emit_block(ra, 0, 2 * hp, i)
            emit_block(rb_, 1, 2 * hp + 1, i)
            for a, b in zip_longest(ra.q, rb_.q):
                if a:
                    a()
                if b:
                    b()
        for s in range(2):
            hh = 2 * hp + s
            d = S[s]

            def oo_store(h, hh=hh, d=d):
                dst = oo_d[hh]
                srcap = d["o_sb"][:]
                if len(dst.shape) == 3:
                    srcap = srcap.rearrange("p (j t) -> p j t", j=dst.shape[1])
                return h.dma_start(dst, srcap)
            p.dma("sync", oo_store, reads=(f"osb_{s}",), writes=("oo_out",),
                  tag=f"oo{hh}", final=True)
    return None


def build_attn0():
    p = Prog("attn0")
    with p.scope('att_'):
        emit_attn0(p)
    return p.finalize()


def emit_wo_res(p, nt=TLOC):
    o_d = p.dram_in("ofm", [128, 16 * nt], BF16)
    wo_d = p.dram_in("wo", [2048, 2048], F32)
    bo_d = p.dram_in("bo", [1, 2048], F32)
    x_d = p.dram_in("xin", [nt, 2048], F32)
    xo_d = p.dram_out("xout", [nt, 2048], F32)
    ofm = p.sb("ofm", [128, 16 * nt], BF16)
    wo = p.sb("wo", [128, 16 * 2048], BF16)
    bo = p.sb("bo", [1, 2048], F32)
    bob = p.sb("bob", [1, 2048], BF16)
    onesr = p.sb("onesr", [1, 128], BF16)
    xt = p.sb("xt", [128, 2048], F32)
    ot = p.sb("ot", [128, 2048], F32)
    boB = p.sb("boB", [128, 2048], F32)
    ps = p.ps("ps", [128, 512], F32)
    p.dma("sync", lambda h: h.dma_start(ofm[:], o_d), writes=("ofm",))
    p.dma("gpsimd", lambda h: h.dma_start(
        wo[:].rearrange("p (c n) -> p c n", c=16), wo_d.rearrange("(c p) n -> p c n", p=128)),
        writes=("wo",))
    p.dma("sync", lambda h: h.dma_start(bo[:], bo_d), writes=("bo",))
    p.op("vector", lambda h: h.memset(onesr[:], 1.0), writes=("onesr",))
    p.op("vector", lambda h: h.tensor_copy(bob[:], bo[:]), reads=("bo",), writes=("bob",))
    for f in range(4):
        p.op("tensor", lambda h, f=f: h.matmul(
            ps[:], onesr[:], bob[:, f * 512:f * 512 + 512], start=True, stop=True),
             reads=("onesr", "bob"), writes=("ps",))
        p.op("vector", lambda h, f=f: h.tensor_copy(boB[:, f * 512:f * 512 + 512], ps[:]),
             reads=("ps",), writes=("boB",))
    ntc = nt // 128
    for tc in range(ntc):
        p.dma("sync", lambda h, tc=tc: h.dma_start(xt[:], x_d[tc * 128:tc * 128 + 128, :]),
              writes=("xt",))
        for f in range(4):
            for kc in range(16):
                p.op("tensor", lambda h, tc=tc, f=f, kc=kc: h.matmul(
                    ps[:], ofm[:, kc * nt + tc * 128: kc * nt + tc * 128 + 128],
                    wo[:, kc * 2048 + f * 512: kc * 2048 + f * 512 + 512],
                    start=(kc == 0), stop=(kc == 15)),
                     reads=("ofm", "wo"), writes=("ps",))
            p.op("vector", lambda h, f=f: h.tensor_tensor(
                ot[:, f * 512:f * 512 + 512], ps[:], boB[:, f * 512:f * 512 + 512], ALU.add),
                 reads=("ps", "boB"), writes=("ot",))
        p.op("vector", lambda h: h.tensor_tensor(ot[:], ot[:], xt[:], ALU.add),
             reads=("ot", "xt"), writes=("ot",))
        p.dma("sync", lambda h, tc=tc: h.dma_start(xo_d[tc * 128:tc * 128 + 128, :], ot[:]),
              reads=("ot",), tag=f"st{tc}", final=True)
    return None


def build_wo_res(nt=TLOC):
    p = Prog("wo_res")
    with p.scope('wo__'):
        emit_wo_res(p, nt)
    return p.finalize()


def emit_moe0(p, ntok=2048, nexp=8, nsh=1):
    X = mybir.AxisListType.X
    ntc = ntok // 128
    nft = ntok // 512
    xnf_d = p.dram_in("xnf", [128, 16 * ntok], BF16)
    gwt_d = p.dram_in("gwt", [2048, 16], F32)
    gb_d = p.dram_in("gb", [1, 16], F32)
    w1t_d = p.dram_in("w1t", [nexp, 2048, 1408], F32)
    w3t_d = p.dram_in("w3t", [nexp, 2048, 1408], F32)
    w2t_d = p.dram_in("w2t", [nexp, 1408, 2048], F32)
    b1_d = p.dram_in("b1", [128, nexp * 11], F32)
    b3_d = p.dram_in("b3", [128, nexp * 11], F32)
    b2_d = p.dram_in("b2", [nexp, 2048], F32)
    ws1t_d = p.dram_in("ws1t", [2048, nsh * 1408], F32)
    ws3t_d = p.dram_in("ws3t", [2048, nsh * 1408], F32)
    ws2t_d = p.dram_in("ws2t", [nsh * 1408, 2048], F32)
    bs1_d = p.dram_in("bs1", [128, nsh * 11], F32)
    bs3_d = p.dram_in("bs3", [128, nsh * 11], F32)
    id_d = p.dram_in("ident", [128, 128], BF16)
    y_d = p.dram_out("yy", [ntok, 2048], F32)

    xn = p.sb("xn", [128, 16 * ntok], BF16)
    gwt = p.sb("gwt", [128, 16 * 16], BF16)
    gbr = p.sb("gbr", [1, 16], F32)
    gbrb = p.sb("gbrb", [1, 16], BF16)
    ident = p.sb("ident", [128, 128], BF16)
    onesr = p.sb("onesr", [1, 128], BF16)
    probs = p.sb("probs", [128, 16], F32)
    biased = p.sb("biased", [128, 16], F32)
    scr = p.sb("scr", [128, 16], F32)
    mcol = p.sb("mcol", [128, 1], F32)
    t6 = p.sb("t6", [128, 1], F32)
    ind = p.sb("ind", [128, 16], F32)
    cwtm = p.sb("cwtm", [128, 16], F32)
    cw8 = p.sb("cw8", [128, nexp], BF16)
    cwfm = p.sb("cwfm", [nexp, ntok], F32)
    cwfmb = p.sb("cwfmb", [nexp, ntok], BF16)
    sel = p.sb("sel", [nexp, 128], BF16)
    b2sb = p.sb("b2sb", [nexp, 2048], BF16)
    b1sb = p.sb("b1sb", [128, nexp * 11], F32)
    b3sb = p.sb("b3sb", [128, nexp * 11], F32)
    bs1sb = p.sb("bs1sb", [128, nsh * 11], F32)
    bs3sb = p.sb("bs3sb", [128, nsh * 11], F32)
    w13 = [p.sb(f"w13_{i}", [128, 16 * 128], BF16) for i in range(4)]
    w2t = [p.sb("w2t_0", [128, 11 * 2048], BF16)]
    gt = [p.sb(f"gt{i}", [128, 512], F32) for i in range(2)]
    ut = [p.sb(f"ut{i}", [128, 512], F32) for i in range(2)]
    hbuf = p.sb("hbuf", [128, 11 * ntok], BF16)
    cwB = p.sb("cwB", [128, ntok], BF16)
    yst = [p.sb(f"yst{i}", [128, 512], F32) for i in range(3)]
    ps_g = [p.ps(f"ps_g{i}", [128, 512], F32) for i in range(2)]
    ps_u = [p.ps(f"ps_u{i}", [128, 512], F32) for i in range(2)]
    ps_y = [p.ps(f"ps_y{i}", [128, 512], F32) for i in range(2)]
    ps_r = p.ps("ps_r", [128, 16], F32)
    ps_t = p.ps("ps_t", [128, 128], BF16)

    p.dma("sync", lambda h: h.dma_start(xn[:], xnf_d), writes=("xn",))
    p.dma("gpsimd", lambda h: h.dma_start(
        gwt[:].rearrange("p (c n) -> p c n", c=16), gwt_d.rearrange("(c p) n -> p c n", p=128)),
        writes=("gwt",))
    p.dma("sync", lambda h: h.dma_start(gbr[:], gb_d), writes=("gbr",))
    p.dma("sync", lambda h: h.dma_start(ident[:], id_d), writes=("ident",))
    p.dma("sync", lambda h: h.dma_start(b1sb[:], b1_d), writes=("b1sb",))
    p.dma("sync", lambda h: h.dma_start(b3sb[:], b3_d), writes=("b3sb",))
    p.dma("sync", lambda h: h.dma_start(bs1sb[:], bs1_d), writes=("bs1sb",))
    p.dma("sync", lambda h: h.dma_start(bs3sb[:], bs3_d), writes=("bs3sb",))
    p.dma("gpsimd", lambda h: h.dma_start(b2sb[:], b2_d), writes=("b2sb",))
    p.op("vector", lambda h: h.memset(onesr[:], 1.0), writes=("onesr",))
    p.op("vector", lambda h: h.tensor_copy(gbrb[:], gbr[:]), reads=("gbr",), writes=("gbrb",))

    gbB = p.sb("gbB", [128, 16], F32)
    p.op("tensor", lambda h: h.matmul(ps_r[:], onesr[:], gbrb[:], start=True, stop=True),
         reads=("onesr", "gbrb"), writes=("ps_r",))
    p.op("vector", lambda h: h.tensor_copy(gbB[:], ps_r[:]), reads=("ps_r",), writes=("gbB",))
    for tc in range(ntc):
        for kc in range(16):
            p.op("tensor", lambda h, tc=tc, kc=kc: h.matmul(
                ps_r[:], xn[:, kc * ntok + tc * 128: kc * ntok + tc * 128 + 128],
                gwt[:, kc * 16: kc * 16 + 16], start=(kc == 0), stop=(kc == 15)),
                 reads=("xn", "gwt"), writes=("ps_r",))
        p.op("vector", lambda h: h.reduce_max(mcol[:], ps_r[:], axis=X, negate=True),
             reads=("ps_r",), writes=("mcol",))
        p.op("scalar", lambda h: h.activation(probs[:], ps_r[:], AF.Exp, bias=mcol[:]),
             reads=("ps_r", "mcol"), writes=("probs",))
        p.op("vector", lambda h: h.reduce_sum(mcol[:], probs[:], axis=X),
             reads=("probs",), writes=("mcol",))
        p.op("vector", lambda h: h.reciprocal(mcol[:], mcol[:]), reads=("mcol",), writes=("mcol",))
        p.op("vector", lambda h: h.tensor_scalar(probs[:], probs[:], mcol[:], None, ALU.mult),
             reads=("probs", "mcol"), writes=("probs",))
        p.op("vector", lambda h: h.tensor_tensor(biased[:], probs[:], gbB[:], ALU.add),
             reads=("probs", "gbB"), writes=("biased",))
        p.op("vector", lambda h: h.tensor_copy(scr[:], biased[:]), reads=("biased",), writes=("scr",))
        for it in range(5):
            p.op("vector", lambda h: h.reduce_max(t6[:], scr[:], axis=X),
                 reads=("scr",), writes=("t6",))
            p.op("vector", lambda h: h.tensor_scalar(ind[:], scr[:], t6[:], None, ALU.is_ge),
                 reads=("scr", "t6"), writes=("ind",))
            p.op("vector", lambda h: h.scalar_tensor_tensor(scr[:], ind[:], -1e30, scr[:], ALU.mult, ALU.add),
                 reads=("ind", "scr"), writes=("scr",))
        p.op("vector", lambda h: h.reduce_max(t6[:], scr[:], axis=X),
             reads=("scr",), writes=("t6",))
        p.op("vector", lambda h: h.tensor_scalar(ind[:], biased[:], t6[:], None, ALU.is_ge),
             reads=("biased", "t6"), writes=("ind",))
        p.op("vector", lambda h: h.tensor_tensor(cwtm[:], probs[:], ind[:], ALU.mult),
             reads=("probs", "ind"), writes=("cwtm",))
        p.op("vector", lambda h: h.tensor_copy(cw8[:], cwtm[:, 0:nexp]),
             reads=("cwtm",), writes=("cw8",))
        p.op("tensor", lambda h: h.transpose(ps_t[0:nexp, :], cw8[:], ident[:]),
             reads=("cw8", "ident"), writes=("ps_t",))
        p.op("vector", lambda h, tc=tc: h.tensor_copy(cwfm[:, tc * 128:tc * 128 + 128], ps_t[0:nexp, 0:128]),
             reads=("ps_t",), writes=("cwfm",))
    p.op("vector", lambda h: h.tensor_copy(cwfmb[:], cwfm[:]), reads=("cwfm",), writes=("cwfmb",))

    def expert_pass(e, w1slc, w3slc, w2slc, b1col, b3col, scale_cw, first):
        wkey = "w2buf"
        p.dma("gpsimd", lambda h: h.dma_start(
            w2t[0][:].rearrange("p (c n) -> p c n", c=11), w2slc), writes=(wkey,),
              tag=f"w2ld{e}")
        for hc in range(11):
            bkey = f"w13_{(hc % 2) * 2}", f"w13_{(hc % 2) * 2 + 1}"
            p.dma("gpsimd", lambda h, hc=hc: h.dma_start(
                w13[(hc % 2) * 2][:].rearrange("p (c n) -> p c n", c=16), w1slc(hc)),
                  writes=(bkey[0],), tag=f"w1ld{e}_{hc}")
            p.dma("gpsimd", lambda h, hc=hc: h.dma_start(
                w13[(hc % 2) * 2 + 1][:].rearrange("p (c n) -> p c n", c=16), w3slc(hc)),
                  writes=(bkey[1],), tag=f"w3ld{e}_{hc}")
            for ft in range(nft):
                rr = (hc * nft + ft) % 2
                psg, psu2 = ps_g[rr], ps_u[rr]
                gtt, utt = gt[rr], ut[rr]
                kg, ku, kgt, kut = f"ps_g{rr}", f"ps_u{rr}", f"gt{rr}", f"ut{rr}"
                for kc in range(16):
                    p.op("tensor", lambda h, hc=hc, ft=ft, kc=kc, psg=psg: h.matmul(
                        psg[:], w13[(hc % 2) * 2][:, kc * 128: kc * 128 + 128],
                        xn[:, kc * ntok + ft * 512: kc * ntok + ft * 512 + 512],
                        start=(kc == 0), stop=(kc == 15)),
                         reads=(bkey[0], "xn"), writes=(kg,))
                for kc in range(16):
                    p.op("tensor", lambda h, hc=hc, ft=ft, kc=kc, psu2=psu2: h.matmul(
                        psu2[:], w13[(hc % 2) * 2 + 1][:, kc * 128: kc * 128 + 128],
                        xn[:, kc * ntok + ft * 512: kc * ntok + ft * 512 + 512],
                        start=(kc == 0), stop=(kc == 15)),
                         reads=(bkey[1], "xn"), writes=(ku,))
                p.op("scalar", lambda h, b1col=b1col, hc=hc, psg=psg, gtt=gtt: h.activation(
                    gtt[:], psg[:], AF.Silu, bias=b1col(hc)),
                     reads=(kg, "b1sb"), writes=(kgt,))
                if scale_cw:
                    p.op("vector", lambda h, b3col=b3col, hc=hc, ft=ft, psu2=psu2, utt=utt: h.scalar_tensor_tensor(
                        utt[:], psu2[:], b3col(hc), cwB[:, ft * 512:ft * 512 + 512],
                        ALU.add, ALU.mult),
                         reads=(ku, "b3sb", "cwB"), writes=(kut,))
                else:
                    p.op("vector", lambda h, b3col=b3col, hc=hc, psu2=psu2, utt=utt: h.tensor_scalar(
                        utt[:], psu2[:], b3col(hc), None, ALU.add),
                         reads=(ku, "b3sb"), writes=(kut,))
                p.op("vector", lambda h, hc=hc, ft=ft, gtt=gtt, utt=utt: h.tensor_tensor(
                    hbuf[:, hc * ntok + ft * 512: hc * ntok + ft * 512 + 512],
                    gtt[:], utt[:], ALU.mult),
                     reads=(kgt, kut), writes=("hbuf",))
        for tc in range(ntc):
            for mf in range(4):
                ri = (tc * 4 + mf)
                psy, ky = ps_y[ri % 2], f"ps_y{ri % 2}"
                ys, kys = yst[ri % 3], f"yst{ri % 3}"
                for hc in range(11):
                    p.op("tensor", lambda h, tc=tc, mf=mf, hc=hc, psy=psy: h.matmul(
                        psy[:], hbuf[:, hc * ntok + tc * 128: hc * ntok + tc * 128 + 128],
                        w2t[0][:, hc * 2048 + mf * 512: hc * 2048 + mf * 512 + 512],
                        start=(hc == 0), stop=(hc == 10 and not first)),
                         reads=("hbuf", wkey), writes=(ky,))
                if first:
                    p.op("tensor", lambda h, tc=tc, mf=mf, psy=psy: h.matmul(
                        psy[:], cwfmb[:, tc * 128:tc * 128 + 128],
                        b2sb[:, mf * 512:mf * 512 + 512], start=False, stop=True),
                         reads=("cwfmb", "b2sb"), writes=(ky,))
                p.op("vector", lambda h, psy=psy, ys=ys: h.tensor_copy(ys[:], psy[:]),
                     reads=(ky,), writes=(kys,))
                p.dma("gpsimd", lambda h, tc=tc, mf=mf, first=first, ys=ys: h.dma_start(
                    y_d[tc * 128:tc * 128 + 128, mf * 512:mf * 512 + 512], ys[:],
                    accum_op=(ALU.bypass if first else ALU.add)),
                      reads=(kys,), writes=(f"ydr_{tc}_{mf}",),
                      tag=f"yacc{e}_{tc}_{mf}", final=True)

    for e in range(nexp):
        p.op("vector", lambda h: h.memset(sel[:], 0.0), writes=("sel",))
        p.dma("sync", lambda h, e=e: h.dma_start(sel[e:e + 1, :], onesr[:]),
              reads=("onesr",), writes=("sel",))
        for ft in range(nft):
            p.op("tensor", lambda h, e=e, ft=ft: h.matmul(
                ps_y[0][:], sel[:], cwfmb[:, ft * 512:ft * 512 + 512],
                start=True, stop=True),
                 reads=("sel", "cwfmb", "ps_y0"), writes=("ps_y0",))
            p.op("vector", lambda h, ft=ft: h.tensor_copy(cwB[:, ft * 512:ft * 512 + 512], ps_y[0][:]),
                 reads=("ps_y0",), writes=("cwB",))
        expert_pass(
            e,
            w1slc=lambda hc, e=e: w1t_d[e].rearrange("(c p) n -> p c n", p=128)[:, :, hc * 128:hc * 128 + 128],
            w3slc=lambda hc, e=e: w3t_d[e].rearrange("(c p) n -> p c n", p=128)[:, :, hc * 128:hc * 128 + 128],
            w2slc=w2t_d[e].rearrange("(c p) n -> p c n", p=128),
            b1col=lambda hc, e=e: b1sb[:, e * 11 + hc: e * 11 + hc + 1],
            b3col=lambda hc, e=e: b3sb[:, e * 11 + hc: e * 11 + hc + 1],
            scale_cw=True, first=(e == 0))
    for s in range(nsh):
        expert_pass(
            nexp + s,
            w1slc=lambda hc, s=s: ws1t_d.rearrange("(c p) n -> p c n", p=128)[:, :, s * 1408 + hc * 128:s * 1408 + hc * 128 + 128],
            w3slc=lambda hc, s=s: ws3t_d.rearrange("(c p) n -> p c n", p=128)[:, :, s * 1408 + hc * 128:s * 1408 + hc * 128 + 128],
            w2slc=ws2t_d[s * 1408:(s + 1) * 1408, :].rearrange("(c p) n -> p c n", p=128),
            b1col=lambda hc, s=s: bs1sb[:, s * 11 + hc:s * 11 + hc + 1],
            b3col=lambda hc, s=s: bs3sb[:, s * 11 + hc:s * 11 + hc + 1],
            scale_cw=False, first=False)
    return None


def build_moe0():
    p = Prog("moe0")
    with p.scope('moe_'):
        emit_moe0(p)
    return p.finalize()


def emit_combine_ln(p, nt=TLOC, d=D, two_y=True):
    X = mybir.AxisListType.X
    xa_d = p.dram_in("xattn", [nt, d], F32)
    y0_d = p.dram_in("y0", [nt, d], F32)
    y1_d = p.dram_in("y1", [nt, d], F32) if two_y else None
    bs2_d = p.dram_in("bs2", [1, d], F32)
    x1_d = p.dram_out("x1", [nt, d], F32)
    xn_d = p.dram_out("xn", [nt, d], BF16)
    xt = p.sb("xt", [128, d], F32)
    y0t = p.sb("y0t", [128, d], F32)
    y1t = p.sb("y1t", [128, d], F32)
    sq = p.sb("sq", [128, d], F32)
    xnt = p.sb("xnt", [128, d], BF16)
    bs2 = p.sb("bs2", [1, d], F32)
    bs2b = p.sb("bs2b", [1, d], BF16)
    bs2B = p.sb("bs2B", [128, d], F32)
    onesr = p.sb("onesr", [1, 128], BF16)
    mean = p.sb("mean", [128, 1], F32)
    var = p.sb("var", [128, 1], F32)
    rstd = p.sb("rstd", [128, 1], F32)
    nmean = p.sb("nmean", [128, 1], F32)
    ps = p.ps("ps", [128, 512], F32)
    p.dma("sync", lambda h: h.dma_start(bs2[:], bs2_d), writes=("bs2",))
    p.op("vector", lambda h: h.memset(onesr[:], 1.0), writes=("onesr",))
    p.op("vector", lambda h: h.tensor_copy(bs2b[:], bs2[:]), reads=("bs2",), writes=("bs2b",))
    for f in range(4):
        p.op("tensor", lambda h, f=f: h.matmul(
            ps[:], onesr[:], bs2b[:, f * 512:f * 512 + 512], start=True, stop=True),
             reads=("onesr", "bs2b"), writes=("ps",))
        p.op("vector", lambda h, f=f: h.tensor_copy(bs2B[:, f * 512:f * 512 + 512], ps[:]),
             reads=("ps",), writes=("bs2B",))
    for c in range(nt // 128):
        sl = slice(c * 128, c * 128 + 128)
        p.dma("sync", lambda h, sl=sl: h.dma_start(xt[:], xa_d[sl, :]), writes=("xt",))
        p.dma("sync", lambda h, sl=sl: h.dma_start(y0t[:], y0_d[sl, :]), writes=("y0t",))
        if two_y:
            p.dma("sync", lambda h, sl=sl: h.dma_start(y1t[:], y1_d[sl, :]), writes=("y1t",))
            p.op("vector", lambda h: h.tensor_tensor(y0t[:], y0t[:], y1t[:], ALU.add),
                 reads=("y0t", "y1t"), writes=("y0t",))
        p.op("vector", lambda h: h.tensor_tensor(y0t[:], y0t[:], bs2B[:], ALU.add),
             reads=("y0t", "bs2B"), writes=("y0t",))
        p.op("vector", lambda h: h.tensor_tensor(xt[:], xt[:], y0t[:], ALU.add),
             reads=("xt", "y0t"), writes=("xt",))
        p.dma("sync", lambda h, sl=sl: h.dma_start(x1_d[sl, :], xt[:]),
              reads=("xt",), tag=f"x1st{c}", final=True)
        p.op("vector", lambda h: h.reduce_sum(mean[:], xt[:], axis=X),
             reads=("xt",), writes=("mean",))
        p.op("vector", lambda h: h.tensor_scalar(nmean[:], mean[:], -1.0 / d, None, ALU.mult),
             reads=("mean",), writes=("nmean",))
        p.op("vector", lambda h: h.tensor_scalar(xt[:], xt[:], nmean[:], None, ALU.add),
             reads=("xt", "nmean"), writes=("xt",))
        p.op("scalar", lambda h: h.activation(sq[:], xt[:], AF.Square),
             reads=("xt",), writes=("sq",))
        p.op("vector", lambda h: h.reduce_sum(var[:], sq[:], axis=X),
             reads=("sq",), writes=("var",))
        p.op("vector", lambda h: h.tensor_scalar(var[:], var[:], 1.0 / d, LN_EPS, ALU.mult, ALU.add),
             reads=("var",), writes=("var",))
        p.op("scalar", lambda h: h.activation(rstd[:], var[:], AF.Ln),
             reads=("var",), writes=("rstd",))
        p.op("scalar", lambda h: h.activation(rstd[:], rstd[:], AF.Exp, scale=-0.5),
             reads=("rstd",), writes=("rstd",))
        p.op("vector", lambda h: h.tensor_scalar(xnt[:], xt[:], rstd[:], None, ALU.mult),
             reads=("xt", "rstd"), writes=("xnt",))
        p.dma("sync", lambda h, sl=sl: h.dma_start(xn_d[sl, :], xnt[:]),
              reads=("xnt",), tag=f"xnst{c}", final=True)
    return None


def build_combine_ln(nt=TLOC, d=D):
    p = Prog("combine_ln")
    with p.scope('com_'):
        emit_combine_ln(p, nt, d)
    return p.finalize()


def emit_proj1(p):
    xnf_d = p.dram_in("xnf", [128, 16 * 2048], BF16)
    wq_d = p.dram_in("wq", [2048, 1024], F32)
    wk_d = p.dram_in("wk", [2048, 1024], F32)
    wv_d = p.dram_in("wv", [2048, 1024], F32)
    bq_d = p.dram_in("bq", [128, 8], F32)
    bk_d = p.dram_in("bk", [128, 8], F32)
    bv_d = p.dram_in("bv", [1, 1024], F32)
    id_d = p.dram_in("ident", [128, 128], BF16)
    qf_d = p.dram_out("qf", [8, 128, 2048], BF16)
    kff_d = p.dram_out("kff", [8, 128, 2048], BF16)
    ktm_d = p.dram_out("ktm", [2048, 1024], BF16)
    v_d = p.dram_out("vtm", [2048, 1024], BF16)

    xn = p.sb("xn", [128, 16 * 2048], BF16)
    wq = p.sb("wq", [128, 16 * 1024], BF16)
    wk = p.sb("wk", [128, 16 * 1024], BF16)
    wv = p.sb("wv", [128, 16 * 1024], BF16)
    bqc = p.sb("bqc", [128, 8], F32)
    bkc = p.sb("bkc", [128, 8], F32)
    bvr = p.sb("bvr", [1, 1024], F32)
    bvrb = p.sb("bvrb", [1, 1024], BF16)
    ident = p.sb("ident", [128, 128], BF16)
    onesr = p.sb("onesr", [1, 128], BF16)
    mn = p.sb("mn", [128, 512], F32)
    mx = p.sb("mx", [128, 512], F32)
    ex = p.sb("ex", [128, 512], F32)
    ob = p.sb("ob", [128, 512], BF16)
    vtile = p.sb("vtile", [128, 1024], BF16)
    ps_a = p.ps("ps_a", [128, 512], F32)
    ps_t = p.ps("ps_t", [128, 128], BF16)

    p.dma("sync", lambda h: h.dma_start(xn[:], xnf_d), writes=("xn",))
    for nm, buf, dr in (("qw", wq, wq_d), ("kw", wk, wk_d), ("vw", wv, wv_d)):
        p.dma("gpsimd", lambda h, buf=buf, dr=dr: h.dma_start(
            buf[:].rearrange("p (c n) -> p c n", c=16), dr.rearrange("(c p) n -> p c n", p=128)),
            writes=(nm,))
    p.dma("sync", lambda h: h.dma_start(bqc[:], bq_d), writes=("qb",))
    p.dma("sync", lambda h: h.dma_start(bkc[:], bk_d), writes=("kb",))
    p.dma("sync", lambda h: h.dma_start(bvr[:], bv_d), writes=("bvr",))
    p.dma("sync", lambda h: h.dma_start(ident[:], id_d), writes=("ident",))
    p.op("vector", lambda h: h.memset(onesr[:], 1.0), writes=("onesr",))
    p.op("vector", lambda h: h.tensor_copy(bvrb[:], bvr[:]), reads=("bvr",), writes=("bvrb",))

    for nm, wbuf, bcol, out_d in (("q", wq, bqc, qf_d), ("k", wk, bkc, kff_d)):
        for hh in range(8):
            for ft in range(4):
                for kc in range(16):
                    p.op("tensor", lambda h, hh=hh, ft=ft, kc=kc, wbuf=wbuf: h.matmul(
                        ps_a[:], wbuf[:, kc * 1024 + hh * 128: kc * 1024 + hh * 128 + 128],
                        xn[:, kc * 2048 + ft * 512: kc * 2048 + ft * 512 + 512],
                        start=(kc == 0), stop=(kc == 15)),
                         reads=(nm + "w", "xn"), writes=("ps_a",))
                p.op("vector", lambda h, hh=hh: h.tensor_scalar(
                    mn[:], ps_a[:], bcol[:, hh:hh + 1], 0.0, ALU.add, ALU.min),
                     reads=("ps_a", nm + "b"), writes=("mn",))
                p.op("vector", lambda h, hh=hh: h.tensor_scalar(
                    mx[:], ps_a[:], bcol[:, hh:hh + 1], 0.0, ALU.add, ALU.max),
                     reads=("ps_a", nm + "b"), writes=("mx",))
                p.op("scalar", lambda h: h.activation(ex[:], mn[:], AF.Exp),
                     reads=("mn",), writes=("ex",))
                p.op("vector", lambda h: h.tensor_tensor(ob[:], ex[:], mx[:], ALU.add),
                     reads=("ex", "mx"), writes=("ob",))
                p.dma("sync", lambda h, hh=hh, ft=ft, out_d=out_d: h.dma_start(
                    out_d[hh][:, ft * 512:ft * 512 + 512], ob[:]),
                      reads=("ob",), writes=(f"{nm}outd{hh}",),
                      tag=f"{nm}st{hh}_{ft}", final=True)
    kfl = p.sb("kfl", [128, 2048], BF16)
    ttile = p.sb("ttile", [128, 128], BF16)
    for hh in range(8):
        p.dma("sync", lambda h, hh=hh: h.dma_start(kfl[:], kff_d[hh]),
              reads=(f"koutd{hh}",), writes=("kfl",), tag=f"krl{hh}")
        for tc in range(16):
            p.op("tensor", lambda h, tc=tc: h.transpose(
                ps_t[:], kfl[:, tc * 128:tc * 128 + 128], ident[:]),
                 reads=("kfl", "ident"), writes=("ps_t",))
            p.op("vector", lambda h, tc=tc: h.tensor_copy(ttile[:], ps_t[:]),
                 reads=("ps_t",), writes=("ttile",))
            p.dma("sync", lambda h, hh=hh, tc=tc: h.dma_start(
                ktm_d[tc * 128:tc * 128 + 128, hh * 128:hh * 128 + 128], ttile[:]),
                  reads=("ttile",), tag=f"ktmst{hh}_{tc}", final=True)
    for tc in range(16):
        for ft in range(2):
            for kc in range(16):
                p.op("tensor", lambda h, tc=tc, ft=ft, kc=kc: h.matmul(
                    ps_a[:], xn[:, kc * 2048 + tc * 128: kc * 2048 + tc * 128 + 128],
                    wv[:, kc * 1024 + ft * 512: kc * 1024 + ft * 512 + 512],
                    start=(kc == 0), stop=False),
                     reads=("vw", "xn"), writes=("ps_a",))
            p.op("tensor", lambda h, tc=tc, ft=ft: h.matmul(
                ps_a[:], onesr[:], bvrb[:, ft * 512:ft * 512 + 512], start=False, stop=True),
                 reads=("onesr", "bvrb"), writes=("ps_a",))
            p.op("vector", lambda h, ft=ft: h.tensor_copy(vtile[:, ft * 512:ft * 512 + 512], ps_a[:]),
                 reads=("ps_a",), writes=("vtile",))
        p.dma("sync", lambda h, tc=tc: h.dma_start(v_d[tc * 128:tc * 128 + 128, :], vtile[:]),
              reads=("vtile",), tag=f"vst{tc}", final=True)
    return None


def build_proj1():
    p = Prog("proj1")
    with p.scope('pro_'):
        emit_proj1(p)
    return p.finalize()


def emit_attn1(p):
    qf_d = p.dram_in("qf", [8, 128, 2048], BF16)
    kff_d = p.dram_in("kff", [8, 128, 2048], BF16)
    ktm_d = p.dram_in("ktm", [2048, 1024], BF16)
    v_d = p.dram_in("vtm", [2048, 1024], BF16)
    cm_d = p.dram_in("cmask", [128, 128], BF16)
    oo_d = p.dram_out("oo", [8, 128, 2048], BF16)
    ktm = p.sb("ktm", [128, 16 * 1024], BF16)
    vtm = p.sb("vtm", [128, 16 * 1024], BF16)
    cmask = p.sb("cmask", [128, 128], BF16)
    onesr = p.sb("onesr", [1, 128], BF16)
    onescol = p.sb("onescol", [128, 1], BF16)
    p.dma("sync", lambda h: h.dma_start(cmask[:], cm_d), writes=("cmask",))
    p.dma("sync", lambda h: h.dma_start(
        ktm[:].rearrange("p (c n) -> p c n", c=16), ktm_d.rearrange("(c p) n -> p c n", p=128)),
        writes=("ktm",))
    p.dma("sync", lambda h: h.dma_start(
        vtm[:].rearrange("p (c n) -> p c n", c=16), v_d.rearrange("(c p) n -> p c n", p=128)),
        writes=("vtm",))
    p.op("vector", lambda h: h.memset(onesr[:], 1.0), writes=("onesr",))
    p.op("vector", lambda h: h.memset(onescol[:], 1.0), writes=("onescol",))
    S = []
    for s in range(2):
        d = {}
        d["qf"] = p.sb(f"qf{s}", [128, 2048], BF16)
        d["kff"] = p.sb(f"kff{s}", [128, 2048], BF16)
        d["kvsb"] = p.sb(f"kvsb{s}", [128, 128], BF16)
        d["kssb"] = p.sb(f"kssb{s}", [128, 1], BF16)
        d["ksst"] = p.sb(f"ksst{s}", [128, 1], F32)
        d["kcol"] = p.sb(f"kcol{s}", [128, 1], F32)
        d["e_am"] = p.sb(f"e_am{s}", [128, 128], BF16)
        d["zrow"] = p.sb(f"zrow{s}", [1, 128], F32)
        d["zrb"] = p.sb(f"zrb{s}", [1, 128], BF16)
        d["rbt"] = p.sb(f"rbt{s}", [128, 128], F32)
        d["o1"] = p.sb(f"o1_{s}", [128, 2048], BF16)
        d["ps_kv"] = p.ps(f"ps_kv{s}", [128, 128], F32)
        d["ps_a"] = p.ps(f"ps_a{s}", [128, 128], F32)
        d["ps_o"] = p.ps(f"ps_o{s}", [128, 128], F32)
        d["ps_m"] = p.ps(f"ps_m{s}", [128, 128], F32)
        S.append(d)

    class Rec:
        def __init__(self):
            self.q = []

        def op(self, *a, **k):
            self.q.append(lambda: p.op(*a, **k))

        def dma(self, *a, **k):
            self.q.append(lambda: p.dma(*a, **k))

    def emit_chunk(r, s, hh, n):
        d = S[s]
        K = lambda nm: f"{nm}_{s}"
        cs = slice(n * 128, n * 128 + 128)
        hs = slice(n * 1024 + hh * 128, n * 1024 + hh * 128 + 128)
        if n == 0:
            r.op("vector", lambda h, d=d: h.memset(d["kvsb"][:], 0.0), writes=(K("kvsb"),))
            r.op("vector", lambda h, d=d: h.memset(d["kssb"][:], 0.0), writes=(K("kssb"),))
            r.op("vector", lambda h, d=d: h.memset(d["ksst"][:], 0.0), writes=(K("ksst"),))
        else:
            r.op("vector", lambda h, d=d: h.tensor_copy(d["kvsb"][:], d["ps_kv"][:]),
                 reads=(K("pskv"),), writes=(K("kvsb"),))
            r.op("vector", lambda h, d=d: h.tensor_copy(d["kssb"][:], d["ksst"][:]),
                 reads=(K("ksst"),), writes=(K("kssb"),))
        r.op("tensor", lambda h, d=d, cs=cs: h.matmul(
            d["ps_a"][:], d["kff"][:, cs], d["qf"][:, cs], start=True, stop=True),
             reads=(K("kff"), K("qf")), writes=(K("psa"),))
        r.op("vector", lambda h, d=d: h.tensor_tensor(d["e_am"][:], d["ps_a"][:], cmask[:], ALU.mult),
             reads=(K("psa"), "cmask"), writes=(K("eam"),))
        r.op("tensor", lambda h, d=d, hs=hs, n=n: h.matmul(
            d["ps_kv"][:], ktm[:, hs], vtm[:, hs], start=(n == 0), stop=(n == 15)),
             reads=("ktm", "vtm"), writes=(K("pskv"),))
        r.op("vector", lambda h, d=d, cs=cs: h.reduce_sum(
            d["kcol"][:], d["kff"][:, cs], axis=mybir.AxisListType.X),
             reads=(K("kff"),), writes=(K("kcol"),))
        r.op("vector", lambda h, d=d: h.tensor_tensor(d["ksst"][:], d["ksst"][:], d["kcol"][:], ALU.add),
             reads=(K("ksst"), K("kcol")), writes=(K("ksst"),))
        r.op("tensor", lambda h, d=d, cs=cs: h.matmul(
            d["ps_o"][:], d["kvsb"][:], d["qf"][:, cs], start=True, stop=False),
             reads=(K("kvsb"), K("qf")), writes=(K("pso"),))
        r.op("tensor", lambda h, d=d, hs=hs: h.matmul(
            d["ps_o"][:], vtm[:, hs], d["e_am"][:], start=False, stop=True),
             reads=("vtm", K("eam")), writes=(K("pso"),))
        r.op("tensor", lambda h, d=d, cs=cs: h.matmul(
            d["ps_m"][0:1, :], d["kssb"][:], d["qf"][:, cs], start=True, stop=False),
             reads=(K("kssb"), K("qf")), writes=(K("psm"),))
        r.op("tensor", lambda h, d=d: h.matmul(
            d["ps_m"][0:1, :], onescol[:], d["e_am"][:], start=False, stop=True),
             reads=("onescol", K("eam")), writes=(K("psm"),))
        r.op("vector", lambda h, d=d: h.tensor_scalar(d["zrow"][:], d["ps_m"][0:1, :], 1e-6, None, ALU.add),
             reads=(K("psm"),), writes=(K("zrow"),))
        r.op("vector", lambda h, d=d: h.reciprocal(d["zrow"][:], d["zrow"][:]),
             reads=(K("zrow"),), writes=(K("zrow"),))
        r.op("vector", lambda h, d=d: h.tensor_copy(d["zrb"][:], d["zrow"][:]),
             reads=(K("zrow"),), writes=(K("zrb"),))
        r.op("tensor", lambda h, d=d: h.matmul(d["ps_m"][:], onesr[:], d["zrb"][:], start=True, stop=True),
             reads=("onesr", K("zrb"), K("psm")), writes=(K("psm"),))
        r.op("vector", lambda h, d=d: h.tensor_copy(d["rbt"][:], d["ps_m"][:]),
             reads=(K("psm"),), writes=(K("rbt"),))
        r.op("vector", lambda h, d=d, cs=cs: h.tensor_tensor(d["o1"][:, cs], d["ps_o"][:], d["rbt"][:], ALU.mult),
             reads=(K("pso"), K("rbt")), writes=(K("o1"),))

    from itertools import zip_longest
    for hp in range(4):
        for s in range(2):
            hh = 2 * hp + s
            d = S[s]
            p.dma("sync", lambda h, hh=hh, d=d: h.dma_start(d["qf"][:], qf_d[hh]),
                  writes=(f"qf_{s}",), tag=f"q{hh}")
            p.dma("sync", lambda h, hh=hh, d=d: h.dma_start(d["kff"][:], kff_d[hh]),
                  writes=(f"kff_{s}",), tag=f"k{hh}")
        for n in range(16):
            ra, rb_ = Rec(), Rec()
            emit_chunk(ra, 0, 2 * hp, n)
            emit_chunk(rb_, 1, 2 * hp + 1, n)
            for a, b in zip_longest(ra.q, rb_.q):
                if a:
                    a()
                if b:
                    b()
        for s in range(2):
            hh = 2 * hp + s
            d = S[s]

            def oo_store(h, hh=hh, d=d):
                dst = oo_d[hh]
                srcap = d["o1"][:]
                if len(dst.shape) == 3:
                    srcap = srcap.rearrange("p (j t) -> p j t", j=dst.shape[1])
                return h.dma_start(dst, srcap)
            p.dma("sync", oo_store, reads=(f"o1_{s}",), writes=("oo_out",),
                  tag=f"oo{hh}", final=True)
    return None


def build_attn1():
    p = Prog("attn1")
    with p.scope('att_'):
        emit_attn1(p)
    return p.finalize()


def emit_dequant(p):
    gup_d = p.dram_in("gup", [1408, 256], I32)
    gus_d = p.dram_in("gus", [16, 1408], mybir.dt.float16)
    dnp_d = p.dram_in("dnp", [256, 704], I32)
    dns_d = p.dram_in("dns", [44, 256], mybir.dt.float16)
    idf_d = p.dram_in("identf", [128, 128], F32)
    id_d = p.dram_in("ident", [128, 128], BF16)
    gut_d = p.dram_out("gut", [2048, 1408], BF16)
    dnt_d = p.dram_out("dnt", [5632, 256], BF16)
    plan = []
    for rc in range(11):
        plan.append((gup_d[rc * 128:(rc + 1) * 128, :], 256,
                     gus_d[:, rc * 128:(rc + 1) * 128], 16,
                     gut_d[:, rc * 128:(rc + 1) * 128], f"gu{rc}"))
    for rc in range(2):
        plan.append((dnp_d[rc * 128:(rc + 1) * 128, :], 704,
                     dns_d[:, rc * 128:(rc + 1) * 128], 44,
                     dnt_d[:, rc * 128:(rc + 1) * 128], f"dn{rc}"))
    _emit_dequant_core(p, plan, idf_d, id_d)


def _emit_dequant_core(p, plan, idf_d, id_d):

    identf = p.sb("identf", [128, 128], F32)
    ident = p.sb("ident", [128, 128], BF16)
    pkt = p.sb("pkt", [128, 704], I32)
    n_t = p.sb("n_t", [128, 704], I32)
    nf = p.sb("nf", [128, 5632], F32)
    sgf = p.sb("sgf", [128, 5632], F32)
    low3 = p.sb("low3", [128, 5632], F32)
    aaf = p.sb("aaf", [128, 5632], F32)
    bbf = p.sb("bbf", [128, 5632], F32)
    u16t = p.sb("u16t", [128, 5632], U16)
    vals = p.sb("vals", [128, 5632], BF16)
    scf16 = p.sb("scf16", [44, 128], mybir.dt.float16)
    scf = p.sb("scf", [44, 128], F32)
    scT = p.sb("scT", [128, 44], F32)
    tst = p.sb("tst", [128, 44 * 128], BF16)
    ps_t = p.ps("ps_t", [128, 128], BF16)
    ps_tf = p.ps("ps_tf", [128, 128], F32)

    p.dma("sync", lambda h: h.dma_start(identf[:], idf_d), writes=("identf",))
    p.dma("sync", lambda h: h.dma_start(ident[:], id_d), writes=("ident",))

    def decode_chunk(pk_ap, nwords, scales_ap, ngroups, out_ap, rc_tag):
        nin = nwords * 8
        p.dma("sync", lambda h: h.dma_start(pkt[:, 0:nwords], pk_ap), writes=("pkt",), tag=rc_tag + "pk")
        for j in range(8):
            dst = slice(j * nwords, (j + 1) * nwords)
            p.op("vector", lambda h, j=j, dst=dst: h.tensor_scalar(
                n_t[:, 0:nwords], pkt[:, 0:nwords], 4 * j, 15,
                ALU.logical_shift_right, ALU.bitwise_and),
                 reads=("pkt",), writes=("n_t",))
            p.op("vector", lambda h, dst=dst: h.tensor_copy(nf[:, dst], n_t[:, 0:nwords]),
                 reads=("n_t",), writes=("nf",))
        p.op("vector", lambda h: h.tensor_scalar(sgf[:, 0:nin], nf[:, 0:nin], 7.0, 0.0, ALU.subtract, ALU.max),
             reads=("nf",), writes=("sgf",))
        p.op("vector", lambda h: h.tensor_scalar(sgf[:, 0:nin], sgf[:, 0:nin], 1.0, 32768.0, ALU.min, ALU.mult),
             reads=("sgf",), writes=("sgf",))
        p.op("vector", lambda h: h.scalar_tensor_tensor(
            low3[:, 0:nin], sgf[:, 0:nin], -1.0 / 4096.0, nf[:, 0:nin], ALU.mult, ALU.add),
             reads=("sgf", "nf"), writes=("low3",))
        p.op("vector", lambda h: h.tensor_scalar(aaf[:, 0:nin], low3[:, 0:nin], 1.0, 16128.0, ALU.min, ALU.mult),
             reads=("low3",), writes=("aaf",))
        p.op("vector", lambda h: h.tensor_scalar(bbf[:, 0:nin], low3[:, 0:nin], 1.0, 0.0, ALU.subtract, ALU.max),
             reads=("low3",), writes=("bbf",))
        p.op("vector", lambda h: h.tensor_scalar(bbf[:, 0:nin], bbf[:, 0:nin], 1.0, 64.0, ALU.min, ALU.mult),
             reads=("bbf",), writes=("bbf",))
        p.op("vector", lambda h: h.tensor_tensor(bbf[:, 0:nin], bbf[:, 0:nin], low3[:, 0:nin], ALU.mult),
             reads=("bbf", "low3"), writes=("bbf",))
        p.op("vector", lambda h: h.tensor_tensor(aaf[:, 0:nin], aaf[:, 0:nin], bbf[:, 0:nin], ALU.add),
             reads=("aaf", "bbf"), writes=("aaf",))
        p.op("vector", lambda h: h.tensor_tensor(aaf[:, 0:nin], aaf[:, 0:nin], sgf[:, 0:nin], ALU.add),
             reads=("aaf", "sgf"), writes=("aaf",))
        p.op("vector", lambda h: h.tensor_copy(u16t[:, 0:nin], aaf[:, 0:nin]),
             reads=("aaf",), writes=("u16t",))
        p.dma("sync", lambda h: h.dma_start(scf16[0:ngroups, :], scales_ap), writes=("scf16",), tag=rc_tag + "sc")
        p.op("vector", lambda h, ngroups=ngroups: h.tensor_copy(scf[0:ngroups, :], scf16[0:ngroups, :]),
             reads=("scf16",), writes=("scf",))
        p.op("tensor", lambda h, ngroups=ngroups: h.transpose(
            ps_tf[:, 0:ngroups], scf[0:ngroups, :], identf[0:ngroups, 0:ngroups]),
             reads=("scf", "identf"), writes=("ps_tf",))
        p.op("vector", lambda h, ngroups=ngroups: h.tensor_copy(scT[:, 0:ngroups], ps_tf[:, 0:ngroups]),
             reads=("ps_tf",), writes=("scT",))
        for g in range(ngroups):
            gview_v = bass.AP(vals, 16 * g, [[5632, 128], [nwords, 8], [1, 16]])
            gview_a = bass.AP(aaf, 16 * g, [[5632, 128], [nwords, 8], [1, 16]])
            gview_u = bass.AP(u16t, 16 * g, [[5632, 128], [nwords, 8], [1, 16]])
            p.op("vector", lambda h, g=g, gv=gview_v, gu=gview_u: h.tensor_scalar(
                gv, gu.bitcast(BF16), scT[:, g:g + 1], None, ALU.mult),
                 reads=("u16t", "scT"), writes=("vals",))
        nt_ = nin // 128
        for ic in range(nt_):
            p.op("tensor", lambda h, ic=ic: h.transpose(
                ps_t[:], vals[:, ic * 128:ic * 128 + 128], ident[:]),
                 reads=("vals", "ident"), writes=("ps_t",))
            p.op("vector", lambda h, ic=ic: h.tensor_copy(tst[:, ic * 128:ic * 128 + 128], ps_t[:]),
                 reads=("ps_t",), writes=("tst",))
        p.dma("sync", lambda h, nt_=nt_: h.dma_start(
            out_ap.rearrange("(c p) n -> p c n", p=128),
            tst[:].rearrange("p (c n) -> p c n", c=44)[:, 0:nt_, :] if nt_ != 44 else
            tst[:].rearrange("p (c n) -> p c n", c=44)),
              reads=("tst",), tag=rc_tag + "out", final=True)

    for args in plan:
        decode_chunk(*args)
    return None


def build_dequant():
    p = Prog("dequant")
    with p.scope('deq_'):
        emit_dequant(p)
    return p.finalize()


def emit_ffn1(p, ntok=2048, nsub=2):
    ntc = ntok // 128
    nft = ntok // 512
    xnf_d = p.dram_in("xnf", [128, 16 * ntok], BF16)
    wg_d = p.dram_in("wgT", [2048, nsub * 1408], BF16)
    wu_d = p.dram_in("wuT", [2048, nsub * 1408], BF16)
    wd_d = p.dram_in("wdT", [nsub * 1408, 2048], BF16)
    y_d = p.dram_out("yy", [ntok, 2048], F32)
    xn = p.sb("xn", [128, 16 * ntok], BF16)
    w13 = [p.sb(f"w13_{i}", [128, 16 * 128], BF16) for i in range(4)]
    w2t = p.sb("w2t", [128, 11 * 2048], BF16)
    gt = [p.sb(f"gt{i}", [128, 512], F32) for i in range(2)]
    ut = [p.sb(f"ut{i}", [128, 512], F32) for i in range(2)]
    hbuf = p.sb("hbuf", [128, 11 * ntok], BF16)
    yst = [p.sb(f"yst{i}", [128, 512], F32) for i in range(3)]
    ps_g = [p.ps(f"ps_g{i}", [128, 512], F32) for i in range(2)]
    ps_u = [p.ps(f"ps_u{i}", [128, 512], F32) for i in range(2)]
    ps_y = [p.ps(f"ps_y{i}", [128, 512], F32) for i in range(2)]
    p.dma("sync", lambda h: h.dma_start(xn[:], xnf_d), writes=("xn",))
    for sub in range(nsub):
        ss = slice(sub * 1408, (sub + 1) * 1408)
        p.dma("gpsimd", lambda h, sub=sub: h.dma_start(
            w2t[:].rearrange("p (c n) -> p c n", c=11),
            wd_d[sub * 1408:(sub + 1) * 1408, :].rearrange("(c p) n -> p c n", p=128)),
              writes=("w2buf",), tag=f"w2ld{sub}")
        for hc in range(11):
            hco = sub * 1408 + hc * 128
            p.dma("gpsimd", lambda h, hco=hco, hc=hc: h.dma_start(
                w13[(hc % 2) * 2][:].rearrange("p (c n) -> p c n", c=16),
                wg_d.rearrange("(c p) n -> p c n", p=128)[:, :, hco:hco + 128]),
                  writes=(f"w13_{(hc % 2) * 2}",), tag=f"wg{sub}_{hc}")
            p.dma("gpsimd", lambda h, hco=hco, hc=hc: h.dma_start(
                w13[(hc % 2) * 2 + 1][:].rearrange("p (c n) -> p c n", c=16),
                wu_d.rearrange("(c p) n -> p c n", p=128)[:, :, hco:hco + 128]),
                  writes=(f"w13_{(hc % 2) * 2 + 1}",), tag=f"wu{sub}_{hc}")
            for ft in range(nft):
                rr = (hc * nft + ft) % 2
                psg, psu2, gtt, utt = ps_g[rr], ps_u[rr], gt[rr], ut[rr]
                kg, ku, kgt, kut = f"ps_g{rr}", f"ps_u{rr}", f"gt{rr}", f"ut{rr}"
                for kc in range(16):
                    p.op("tensor", lambda h, hc=hc, ft=ft, kc=kc, psg=psg: h.matmul(
                        psg[:], w13[(hc % 2) * 2][:, kc * 128: kc * 128 + 128],
                        xn[:, kc * ntok + ft * 512: kc * ntok + ft * 512 + 512],
                        start=(kc == 0), stop=(kc == 15)),
                         reads=(f"w13_{(hc % 2) * 2}", "xn"), writes=(kg,))
                for kc in range(16):
                    p.op("tensor", lambda h, hc=hc, ft=ft, kc=kc, psu2=psu2: h.matmul(
                        psu2[:], w13[(hc % 2) * 2 + 1][:, kc * 128: kc * 128 + 128],
                        xn[:, kc * ntok + ft * 512: kc * ntok + ft * 512 + 512],
                        start=(kc == 0), stop=(kc == 15)),
                         reads=(f"w13_{(hc % 2) * 2 + 1}", "xn"), writes=(ku,))
                p.op("scalar", lambda h, psg=psg, gtt=gtt: h.activation(gtt[:], psg[:], AF.Silu),
                     reads=(kg,), writes=(kgt,))
                p.op("vector", lambda h, psu2=psu2, utt=utt: h.tensor_copy(utt[:], psu2[:]),
                     reads=(ku,), writes=(kut,))
                p.op("vector", lambda h, hc=hc, ft=ft, gtt=gtt, utt=utt: h.tensor_tensor(
                    hbuf[:, hc * ntok + ft * 512: hc * ntok + ft * 512 + 512],
                    gtt[:], utt[:], ALU.mult),
                     reads=(kgt, kut), writes=("hbuf",))
        for tc in range(ntc):
            for mf in range(4):
                ri = tc * 4 + mf
                psy, ky = ps_y[ri % 2], f"ps_y{ri % 2}"
                ys, kys = yst[ri % 3], f"yst{ri % 3}"
                for hc in range(11):
                    p.op("tensor", lambda h, tc=tc, mf=mf, hc=hc, psy=psy: h.matmul(
                        psy[:], hbuf[:, hc * ntok + tc * 128: hc * ntok + tc * 128 + 128],
                        w2t[:, hc * 2048 + mf * 512: hc * 2048 + mf * 512 + 512],
                        start=(hc == 0), stop=(hc == 10)),
                         reads=("hbuf", "w2buf"), writes=(ky,))
                p.op("vector", lambda h, psy=psy, ys=ys: h.tensor_copy(ys[:], psy[:]),
                     reads=(ky,), writes=(kys,))
                p.dma("gpsimd", lambda h, tc=tc, mf=mf, sub=sub, ys=ys: h.dma_start(
                    y_d[tc * 128:tc * 128 + 128, mf * 512:mf * 512 + 512], ys[:],
                    accum_op=(ALU.bypass if sub == 0 else ALU.add)),
                      reads=(kys,), writes=(f"ydr_{tc}_{mf}",),
                      tag=f"yacc{sub}_{tc}_{mf}", final=True)
    return None


def build_ffn1():
    p = Prog("ffn1")
    with p.scope('ffn_'):
        emit_ffn1(p)
    return p.finalize()


def emit_final(p, nt=TLOC, d=D, two_y=True):
    X = mybir.AxisListType.X
    xa_d = p.dram_in("x1a", [nt, d], F32)
    y0_d = p.dram_in("y0", [nt, d], F32)
    y1_d = p.dram_in("y1", [nt, d], F32) if two_y else None
    x2_d = p.dram_out("x2", [nt, d], F32)
    xt = p.sb("xt", [128, d], F32)
    y0t = p.sb("y0t", [128, d], F32)
    y1t = p.sb("y1t", [128, d], F32)
    sq = p.sb("sq", [128, d], F32)
    ms = p.sb("ms", [128, 1], F32)
    rstd = p.sb("rstd", [128, 1], F32)
    for c in range(nt // 128):
        sl = slice(c * 128, c * 128 + 128)
        p.dma("sync", lambda h, sl=sl: h.dma_start(xt[:], xa_d[sl, :]), writes=("xt",))
        p.dma("sync", lambda h, sl=sl: h.dma_start(y0t[:], y0_d[sl, :]), writes=("y0t",))
        if two_y:
            p.dma("sync", lambda h, sl=sl: h.dma_start(y1t[:], y1_d[sl, :]), writes=("y1t",))
            p.op("vector", lambda h: h.tensor_tensor(y0t[:], y0t[:], y1t[:], ALU.add),
                 reads=("y0t", "y1t"), writes=("y0t",))
        p.op("scalar", lambda h: h.activation(sq[:], y0t[:], AF.Square),
             reads=("y0t",), writes=("sq",))
        p.op("vector", lambda h: h.reduce_sum(ms[:], sq[:], axis=X),
             reads=("sq",), writes=("ms",))
        p.op("vector", lambda h: h.tensor_scalar(ms[:], ms[:], 1.0 / d, 1e-6, ALU.mult, ALU.add),
             reads=("ms",), writes=("ms",))
        p.op("scalar", lambda h: h.activation(rstd[:], ms[:], AF.Ln),
             reads=("ms",), writes=("rstd",))
        p.op("scalar", lambda h: h.activation(rstd[:], rstd[:], AF.Exp, scale=-0.5),
             reads=("rstd",), writes=("rstd",))
        p.op("vector", lambda h: h.tensor_scalar(y0t[:], y0t[:], rstd[:], None, ALU.mult),
             reads=("y0t", "rstd"), writes=("y0t",))
        p.op("vector", lambda h: h.tensor_tensor(xt[:], xt[:], y0t[:], ALU.add),
             reads=("xt", "y0t"), writes=("xt",))
        p.dma("sync", lambda h, sl=sl: h.dma_start(x2_d[sl, :], xt[:]),
              reads=("xt",), tag=f"st{c}", final=True)
    return None


def build_final(nt=TLOC, d=D):
    p = Prog("final")
    with p.scope('fin_'):
        emit_final(p, nt, d)
    return p.finalize()


def emit_tm2fm(p):
    tm = p.dram_in("tm", [TLOC, 2048], BF16)
    fm = p.dram_in("fm", [2048, TLOC], BF16)
    idd = p.dram_in("ident", [128, 128], BF16)
    ident = p.sb("ident", [128, 128], BF16)
    tmt = p.sb("tmt", [128, 2048], BF16)
    stg = p.sb("stg", [128, 16 * 128], BF16)
    ps_t = p.ps("ps_t", [128, 128], BF16)
    p.dma("sync", lambda h: h.dma_start(ident[:], idd), writes=("ident",))
    for tc in range(TLOC // 128):
        p.dma("sync", lambda h, tc=tc: h.dma_start(tmt[:], tm[tc * 128:tc * 128 + 128, :]),
              writes=("tmt",), tag=f"ld{tc}")
        for dc in range(16):
            p.op("tensor", lambda h, dc=dc: h.transpose(
                ps_t[:], tmt[:, dc * 128:dc * 128 + 128], ident[:]),
                 reads=("tmt", "ident"), writes=("ps_t",))
            p.op("vector", lambda h, dc=dc: h.tensor_copy(stg[:, dc * 128:dc * 128 + 128], ps_t[:]),
                 reads=("ps_t",), writes=("stg",))
        p.dma("sync", lambda h, tc=tc: h.dma_start(
            fm[:, tc * 128:tc * 128 + 128].rearrange("(c p) t -> p c t", p=128),
            stg[:].rearrange("p (c t) -> p c t", c=16)),
              reads=("stg",), writes=("fm_out",), tag=f"st{tc}", final=True)


def emit_fm2own(p):
    fm = p.dram_in("fm", [2048, TLOC], BF16)
    xno = p.dram_in("xnfown", [128, 16 * TLOC], BF16)
    p.dma("sync", lambda h: h.dma_start(
        xno.rearrange("p (c t) -> p c t", c=16),
        fm.rearrange("(c p) t -> p c t", p=128)),
          reads=("fm_in",), writes=("xno_out",), tag="fmo", final=True)


def emit_gather(p, ag_flat_ap, nch, width, dst3, idxname, tag):
    from concourse.bass import IndirectOffsetOnAxis
    idx_d = p.dram_in(idxname, [128, nch], I32)
    idxt = p.sb("idxt", [128, nch], I32)
    ga = p.sb("ga", [128, width], BF16)
    gb2 = p.sb("gb2", [128, width], BF16)
    tiles = [ga, gb2]
    p.dma("sync", lambda h: h.dma_start(idxt[:], idx_d), writes=("idxt",))
    for c in range(nch):
        t = tiles[c % 2]
        tk = f"gt{c % 2}"
        p.dma("gpsimd", lambda h, c=c, t=t: h.indirect_dma_start(
            t[:], None, ag_flat_ap, IndirectOffsetOnAxis(idxt[:, c:c + 1], 0)),
              reads=("idxt", "ag_in"), writes=(tk,), tag=f"{tag}g{c}")
        p.dma("sync", lambda h, c=c, t=t: h.dma_start(dst3[:, c, :], t[:]),
              reads=(tk,), writes=("gout",), tag=f"{tag}s{c}", final=True)


def emit_wasm(p):
    gu_ag = p.dram_in("gu_ag", [8 * 2048, 1408], BF16)
    dn_ag = p.dram_in("dn_ag", [8 * 5632, 256], BF16)
    wg = p.dram_in("wg_asm", [2048, 5632], BF16)
    wu = p.dram_in("wu_asm", [2048, 5632], BF16)
    wd = p.dram_in("wd_asm", [5632, 2048], BF16)

    def unperm(src2d, dst2d, tag):
        nk = dst2d.shape[0] // 128
        for a in range(8):
            sr = src2d.rearrange("(a k b) n -> a k b n", a=8, b=16)[a]
            dr = bass.AP(dst2d.tensor, dst2d.offset + a * _row_stride(dst2d),
                         [[128 * _row_stride(dst2d), nk], [8 * _row_stride(dst2d), 16],
                          [1, dst2d.shape[1]]])
            p.dma("sync", lambda h, sr=sr, dr=dr: h.dma_start(dr, sr),
                  reads=("ag_in",), writes=("w_out",), tag=f"{tag}a{a}", final=True)

    for r in range(4):
        unperm(gu_ag[r * 2048:(r + 1) * 2048, :], wg[:, r * 1408:(r + 1) * 1408], f"wg{r}")
        unperm(gu_ag[(4 + r) * 2048:(5 + r) * 2048, :], wu[:, r * 1408:(r + 1) * 1408], f"wu{r}")
    for r in range(8):
        unperm(dn_ag[r * 5632:(r + 1) * 5632, :], wd[:, r * 256:(r + 1) * 256], f"wd{r}")


def _row_stride(ap2d):
    return ap2d.ap[0][0]


PAIRS = [[0, 1], [2, 3], [4, 5], [6, 7]]
ALL8 = [[0, 1, 2, 3, 4, 5, 6, 7]]


def build_fused():
    p = Prog("fused")
    din, dtmp = p.dram_in, p.dram_tmp
    ext = {}
    ext["x"] = din("x", [TLOC, 2048], F32)
    for nm, sh in [("ident", [128, 128]), ("smask", [128, 128]), ("smaskt", [128, 128]),
                   ("cmask", [128, 128])]:
        ext[nm] = din(nm, sh, BF16)
    ext["identf"] = din("identf", [128, 128], F32)
    ext["cosr"] = din("cosr", [2048, 256], F32)
    ext["sinr"] = din("sinr", [2048, 256], F32)
    for nm, sh, dt in [
        ("a0_wq", [2048, 1024], F32), ("a0_wk", [2048, 256], F32), ("a0_wv", [2048, 256], F32),
        ("a0_wb", [2048, 16], F32), ("a0_bb", [16, 1], F32),
        ("a0_qnw", [1, 64], F32), ("a0_knw", [1, 64], F32),
        ("wo0", [2048, 2048], F32), ("bo0", [1, 2048], F32),
        ("m_gwt", [2048, 16], F32), ("m_gb", [1, 16], F32),
        ("m_w1t", [16, 2048, 1408], F32), ("m_w3t", [16, 2048, 1408], F32),
        ("m_w2t", [16, 1408, 2048], F32),
        ("m_b1", [128, 176], F32), ("m_b3", [128, 176], F32), ("m_b2", [16, 2048], F32),
        ("m_ws1t", [2048, 2816], F32), ("m_ws3t", [2048, 2816], F32),
        ("m_ws2t", [2816, 2048], F32), ("m_bs1", [128, 22], F32), ("m_bs3", [128, 22], F32),
        ("m_bs2", [1, 2048], F32),
        ("a1_wq", [2048, 1024], F32), ("a1_wk", [2048, 1024], F32), ("a1_wv", [2048, 1024], F32),
        ("a1_bq", [128, 8], F32), ("a1_bk", [128, 8], F32), ("a1_bv", [1, 1024], F32),
        ("wo1", [2048, 2048], F32), ("bo1", [1, 2048], F32),
        ("dq_gup", [1408, 256], I32), ("dq_gus", [16, 1408], mybir.dt.float16),
        ("dq_dnp", [256, 704], I32), ("dq_dns", [44, 256], mybir.dt.float16),
    ]:
        ext[nm] = din(nm, sh, dt)
    x2_out = p.dram_out("x2", [TLOC, 2048], F32)

    T = dtmp
    xn_tm = T("i_xntm", [TLOC, 2048], BF16)
    fm_own = T("i_fmown", [2048, TLOC], BF16)
    xnf_own = T("i_xnfown", [128, 16 * TLOC], BF16)
    fmag_t = p.nc.dram_tensor("i_fmag", [8 * 2048, TLOC], BF16, kind="Internal")
    xnf_full = T("i_xnffull", [128, 16 * 2048], BF16)
    qa_t = T("i_qa", [16, 65, 2048], BF16)
    ka_t = T("i_ka", [4, 66, 2048], BF16)
    vv_t = T("i_vv", [2048, 256], BF16)
    beta_t = T("i_beta", [16, 2048], F32)
    oo_loc = T("i_ooloc", [TLOC, 2048], BF16)
    oag_t = p.nc.dram_tensor("i_oag", [8 * TLOC, 2048], BF16, kind="Internal")
    ofm_asm = T("i_ofmasm", [128, 16 * TLOC], BF16)
    x_attn = T("i_xattn", [TLOC, 2048], F32)
    yy0 = T("i_yy0", [TLOC, 2048], F32)
    x1_t = T("i_x1", [TLOC, 2048], F32)
    xn1_tm = T("i_xn1tm", [TLOC, 2048], BF16)
    qf_t = T("i_qf", [8, 128, 2048], BF16)
    kff_t = T("i_kff", [8, 128, 2048], BF16)
    ktm_t = T("i_ktm", [2048, 1024], BF16)
    vtm_t = T("i_vtm", [2048, 1024], BF16)
    x1a_t = T("i_x1a", [TLOC, 2048], F32)
    gut_t = T("i_gut", [2048, 1408], BF16)
    dnt_t = T("i_dnt", [5632, 256], BF16)
    gu_ag = T("i_guag", [8 * 2048, 1408], BF16)
    dn_ag = T("i_dnag", [8 * 5632, 256], BF16)
    wg_asm = T("i_wgasm", [2048, 5632], BF16)
    wu_asm = T("i_wuasm", [2048, 5632], BF16)
    wd_asm = T("i_wdasm", [5632, 2048], BF16)
    yy1 = T("i_yy1", [TLOC, 2048], F32)

    fmag_flat = bass.AP(fmag_t, 0, [[TLOC, 8 * 2048], [1, TLOC]])
    oag_flat = bass.AP(oag_t, 0, [[1024, 8 * TLOC * 2], [1, 1024]])
    oo0_view = oo_loc.rearrange("(h r) t -> h r t", h=16)
    oo1_view = oo_loc.rearrange("(h r) t -> h r t", h=8)
    xnf_full3 = xnf_full.rearrange("p (c j t) -> p (c j) t", j=2, t=1024)
    ofm3 = ofm_asm.rearrange("p (c t) -> p c t", c=16)

    def phase(pfx, fn, io, **kw):
        p.set_io(io)
        with p.scope(pfx):
            fn(p, **kw)
        p.set_io(None)

    import os
    maxcoll = int(os.environ.get("FUSED_MAXCOLL", "99"))
    ncoll = [0]

    def coll(kind, op, groups, in_ap, out_ap, tag):
        p.barrier()
        if ncoll[0] < maxcoll:
            p.op("gpsimd", lambda h: h.collective_compute(
                kind, op, replica_groups=groups, ins=[in_ap.opt()], outs=[out_ap.opt()]),
                 reads=(), writes=(), tag=tag)
        ncoll[0] += 1
        p.barrier()

    def gphase(pfx, ag_flat, nch, width, dst3, idxname):
        p.set_io({idxname: ext[idxname]} if idxname in ext else None)
        ext.setdefault(idxname, din(idxname, [128, nch], I32))
        p.set_io({idxname: ext[idxname]})
        with p.scope(pfx):
            emit_gather(p, ag_flat, nch, width, dst3, idxname, pfx)
        p.set_io(None)

    phase("ln0_", emit_ln, {"x": ext["x"], "xn": xn_tm})
    phase("tf0_", emit_tm2fm, {"tm": xn_tm, "fm": fm_own, "ident": ext["ident"]})
    phase("fo0_", emit_fm2own, {"fm": fm_own, "xnfown": xnf_own})
    coll("AllGather", ALU.bypass, ALL8, fm_own, fmag_t.ap(), "agxn0")
    gphase("gx0_", fmag_flat, 32, 1024, xnf_full3, "gxn_idx")
    phase("p0_", emit_proj0, {
        "xnf": xnf_full, "wq": ext["a0_wq"], "wk": ext["a0_wk"], "wv": ext["a0_wv"],
        "wb": ext["a0_wb"], "bb": ext["a0_bb"], "qnw": ext["a0_qnw"], "knw": ext["a0_knw"],
        "cosr": ext["cosr"], "sinr": ext["sinr"], "ident": ext["ident"],
        "qa": qa_t, "ka": ka_t, "vv": vv_t, "beta": beta_t})
    phase("at0_", emit_attn0, {
        "qa": qa_t, "ka": ka_t, "vv": vv_t, "beta": beta_t, "ident": ext["ident"],
        "identf": ext["identf"], "smask": ext["smask"], "smaskt": ext["smaskt"],
        "cmask": ext["cmask"], "oo": oo0_view})
    coll("AllGather", ALU.bypass, ALL8, oo_loc, oag_t.ap(), "ago0")
    gphase("go0_", oag_flat, 16, 1024, ofm3, "go_idx")
    phase("wo0_", emit_wo_res, {
        "ofm": ofm_asm, "wo": ext["wo0"], "bo": ext["bo0"], "xin": ext["x"],
        "xout": x_attn})
    phase("mo0_", emit_moe0, {
        "xnf": xnf_own, "gwt": ext["m_gwt"], "gb": ext["m_gb"],
        "w1t": ext["m_w1t"], "w3t": ext["m_w3t"], "w2t": ext["m_w2t"],
        "b1": ext["m_b1"], "b3": ext["m_b3"], "b2": ext["m_b2"],
        "ws1t": ext["m_ws1t"], "ws3t": ext["m_ws3t"], "ws2t": ext["m_ws2t"],
        "bs1": ext["m_bs1"], "bs3": ext["m_bs3"], "ident": ext["ident"], "yy": yy0},
        ntok=TLOC, nexp=16, nsh=2)
    phase("cb1_", emit_combine_ln, {
        "xattn": x_attn, "y0": yy0, "bs2": ext["m_bs2"], "x1": x1_t, "xn": xn1_tm},
        two_y=False)
    phase("tf1_", emit_tm2fm, {"tm": xn1_tm, "fm": fm_own, "ident": ext["ident"]})
    phase("fo1_", emit_fm2own, {"fm": fm_own, "xnfown": xnf_own})
    coll("AllGather", ALU.bypass, ALL8, fm_own, fmag_t.ap(), "agxn1")
    gphase("gx1_", fmag_flat, 32, 1024, xnf_full3, "gxn_idx")
    phase("p1_", emit_proj1, {
        "xnf": xnf_full, "wq": ext["a1_wq"], "wk": ext["a1_wk"], "wv": ext["a1_wv"],
        "bq": ext["a1_bq"], "bk": ext["a1_bk"], "bv": ext["a1_bv"], "ident": ext["ident"],
        "qf": qf_t, "kff": kff_t, "ktm": ktm_t, "vtm": vtm_t})
    phase("at1_", emit_attn1, {
        "qf": qf_t, "kff": kff_t, "ktm": ktm_t, "vtm": vtm_t, "cmask": ext["cmask"],
        "oo": oo1_view})
    coll("AllGather", ALU.bypass, ALL8, oo_loc, oag_t.ap(), "ago1")
    gphase("go1_", oag_flat, 16, 1024, ofm3, "go_idx")
    phase("wo1_", emit_wo_res, {
        "ofm": ofm_asm, "wo": ext["wo1"], "bo": ext["bo1"], "xin": x1_t,
        "xout": x1a_t})
    phase("dq_", emit_dequant, {
        "gup": ext["dq_gup"], "gus": ext["dq_gus"], "dnp": ext["dq_dnp"],
        "dns": ext["dq_dns"], "identf": ext["identf"], "ident": ext["ident"],
        "gut": gut_t, "dnt": dnt_t})
    coll("AllGather", ALU.bypass, ALL8, gut_t, gu_ag, "agdq1")
    coll("AllGather", ALU.bypass, ALL8, dnt_t, dn_ag, "agdq2")
    phase("wa_", emit_wasm, {"gu_ag": gu_ag, "dn_ag": dn_ag,
                             "wg_asm": wg_asm, "wu_asm": wu_asm, "wd_asm": wd_asm})
    phase("f1_", emit_ffn1, {
        "xnf": xnf_own, "wgT": wg_asm, "wuT": wu_asm, "wdT": wd_asm, "yy": yy1},
        ntok=TLOC, nsub=4)
    phase("fin_", emit_final, {"x1a": x1a_t, "y0": yy1, "x2": x2_out}, two_y=False)
    return p.finalize()


def _fused_in_map(ins, c, consts, shared):
    b, pp = c // 2, c % 2
    identb, identf, smask, smaskt, cmask, cosr, sinr = consts
    ds1 = slice(pp * 1024, (pp + 1) * 1024)
    pidx = np.arange(128, dtype=np.int32)
    gxn = np.empty((128, 32), np.int32)
    for ch in range(16):
        for j in range(2):
            gxn[:, ch * 2 + j] = (2 * b + j) * 2048 + ch * 128 + pidx
    go = np.empty((128, 16), np.int32)
    for ch in range(16):
        go[:, ch] = ((2 * b + ch // 8) * 1024 + (ch % 8) * 128 + pidx) * 2 + pp
    m = {
        "x": np.ascontiguousarray(ins["L0_attn_x"].reshape(4, 2, TLOC, D)[b, pp]),
        "ident": identb, "identf": identf, "smask": smask, "smaskt": smaskt,
        "cmask": cmask, "cosr": cosr, "sinr": sinr,
        "a0_wq": np.ascontiguousarray(ins["L0_attn_Wq"][:, pp * 1024:(pp + 1) * 1024]),
        "a0_wk": np.ascontiguousarray(ins["L0_attn_Wk"][:, pp * 256:(pp + 1) * 256]),
        "a0_wv": np.ascontiguousarray(ins["L0_attn_Wv"][:, pp * 256:(pp + 1) * 256]),
        "a0_wb": np.ascontiguousarray(ins["L0_attn_Wb"][:, pp * 16:(pp + 1) * 16]),
        "a0_bb": np.ascontiguousarray(ins["L0_attn_bb"][pp * 16:(pp + 1) * 16].reshape(16, 1)),
        "a0_qnw": np.ascontiguousarray(ins["L0_attn_q_norm_w"].reshape(1, 64)),
        "a0_knw": np.ascontiguousarray(ins["L0_attn_k_norm_w"].reshape(1, 64)),
        "wo0": ins["L0_attn_Wo"], "bo0": shared["zeros2048"],
        "m_gwt": shared["gwt"], "m_gb": shared["gb"],
        "m_w1t": shared["w1t"], "m_w3t": shared["w3t"], "m_w2t": shared["w2t"],
        "m_b1": shared["b1"], "m_b3": shared["b3"], "m_b2": ins["L0_ffn_b2"],
        "m_ws1t": shared["ws1t"], "m_ws3t": shared["ws3t"], "m_ws2t": shared["ws2t"],
        "m_bs1": shared["bs1"], "m_bs3": shared["bs3"],
        "m_bs2": np.ascontiguousarray(ins["L0_ffn_bs2"].reshape(1, 2048)),
        "a1_wq": np.ascontiguousarray(ins["L1_attn_Wq"][:, ds1]),
        "a1_wk": np.ascontiguousarray(ins["L1_attn_Wk"][:, ds1]),
        "a1_wv": np.ascontiguousarray(ins["L1_attn_Wv"][:, ds1]),
        "a1_bq": np.ascontiguousarray(ins["L1_attn_bq"][ds1].reshape(8, 128).T),
        "a1_bk": np.ascontiguousarray(ins["L1_attn_bk"][ds1].reshape(8, 128).T),
        "a1_bv": np.ascontiguousarray(ins["L1_attn_bv"][ds1].reshape(1, 1024)),
        "wo1": ins["L1_attn_Wo"],
        "bo1": np.ascontiguousarray(ins["L1_attn_bo"].reshape(1, 2048)),
        "dq_gup": np.ascontiguousarray(ins["L1_ffn_gate_up_packed"][c * 1408:(c + 1) * 1408]),
        "dq_gus": np.ascontiguousarray(ins["L1_ffn_gate_up_scales"][:, c * 1408:(c + 1) * 1408]),
        "dq_dnp": np.ascontiguousarray(ins["L1_ffn_down_packed"][c * 256:(c + 1) * 256]),
        "dq_dns": np.ascontiguousarray(ins["L1_ffn_down_scales"][:, c * 256:(c + 1) * 256]),
        "gxn_idx": gxn, "go_idx": go,
    }
    return m


def kernel_fused(**ins):
    identb = np.eye(128, dtype=BF16NP)
    identf = np.eye(128, dtype=np.float32)
    smask, smaskt, cmask = _masks()
    cosr, sinr = rope_tables()
    consts = (identb, identf, smask, smaskt, cmask, cosr, sinr)
    w1, w2, w3 = ins["L0_ffn_w1"], ins["L0_ffn_w2"], ins["L0_ffn_w3"]
    b1, b3 = ins["L0_ffn_b1"], ins["L0_ffn_b3"]
    shared = {
        "zeros2048": np.zeros((1, 2048), np.float32),
        "gwt": np.ascontiguousarray(ins["L0_ffn_gate_w"].T),
        "gb": np.ascontiguousarray(ins["L0_ffn_gate_b"].reshape(1, 16)),
        "w1t": np.ascontiguousarray(w1.transpose(0, 2, 1)),
        "w3t": np.ascontiguousarray(w3.transpose(0, 2, 1)),
        "w2t": np.ascontiguousarray(w2.transpose(0, 2, 1)),
        "b1": np.ascontiguousarray(b1.reshape(16, 11, 128).transpose(2, 0, 1).reshape(128, 176)),
        "b3": np.ascontiguousarray(b3.reshape(16, 11, 128).transpose(2, 0, 1).reshape(128, 176)),
        "ws1t": np.ascontiguousarray(ins["L0_ffn_ws1"].T),
        "ws3t": np.ascontiguousarray(ins["L0_ffn_ws3"].T),
        "ws2t": np.ascontiguousarray(ins["L0_ffn_ws2"].T),
        "bs1": np.ascontiguousarray(ins["L0_ffn_bs1"].reshape(22, 128).T),
        "bs3": np.ascontiguousarray(ins["L0_ffn_bs3"].reshape(22, 128).T),
    }
    C = list(range(8))
    nc = _prog("fused", build_fused)
    in_maps = [_fused_in_map(ins, c, consts, shared) for c in C]
    res = run_bass_kernel_spmd(nc, in_maps, core_ids=C)
    out = np.empty((4, 2048, 2048), np.float32)
    for c in C:
        b, th = c // 2, c % 2
        out[b, th * TLOC:(th + 1) * TLOC] = res.results[c]["x2"]
    return out


_progcache = {}


def _prog(name, fn):
    if name not in _progcache:
        _progcache[name] = fn()
    return _progcache[name]


def _fm_chunks(a2d):
    ntok = a2d.shape[1]
    return np.ascontiguousarray(a2d.reshape(16, 128, ntok).transpose(1, 0, 2).reshape(128, 16 * ntok))


def _masks():
    ii, jj = np.arange(128)[:, None], np.arange(128)[None, :]
    smask = (ii < jj).astype(BF16NP)
    smaskt = (jj < ii).astype(BF16NP)
    cmask = (ii <= jj).astype(BF16NP)
    return smask, smaskt, cmask


def kernel_unfused(**ins):
    identb = np.eye(128, dtype=BF16NP)
    identf = np.eye(128, dtype=np.float32)
    smask, smaskt, cmask = _masks()
    cosr, sinr = rope_tables()
    C = list(range(8))

    x0 = ins["L0_attn_x"]
    xr = x0.reshape(4, 2, TLOC, D)

    nc = _prog("ln", build_ln)
    r_ln = run_bass_kernel_spmd(nc, [
        {"x": np.ascontiguousarray(xr[c // 2, c % 2])} for c in C], core_ids=C)
    xn0 = [r_ln.results[c]["xn"] for c in C]

    def xnf_maps(xn_list):
        out = []
        for b in range(4):
            full = np.concatenate([xn_list[2 * b], xn_list[2 * b + 1]], axis=0)
            out.append(_fm_chunks(full.T))
        return out

    xnf0 = xnf_maps(xn0)

    Wq, Wk, Wv, Wb = ins["L0_attn_Wq"], ins["L0_attn_Wk"], ins["L0_attn_Wv"], ins["L0_attn_Wb"]
    nc = _prog("proj0", build_proj0)
    im = []
    for c in C:
        b, hg = c // 2, c % 2
        im.append({
            "xnf": xnf0[b],
            "wq": np.ascontiguousarray(Wq[:, hg * 1024:(hg + 1) * 1024]),
            "wk": np.ascontiguousarray(Wk[:, hg * 256:(hg + 1) * 256]),
            "wv": np.ascontiguousarray(Wv[:, hg * 256:(hg + 1) * 256]),
            "wb": np.ascontiguousarray(Wb[:, hg * 16:(hg + 1) * 16]),
            "bb": np.ascontiguousarray(ins["L0_attn_bb"][hg * 16:(hg + 1) * 16].reshape(16, 1)),
            "qnw": np.ascontiguousarray(ins["L0_attn_q_norm_w"].reshape(1, 64)),
            "knw": np.ascontiguousarray(ins["L0_attn_k_norm_w"].reshape(1, 64)),
            "cosr": cosr, "sinr": sinr, "ident": identb,
        })
    r_p0 = run_bass_kernel_spmd(nc, im, core_ids=C)

    nc = _prog("attn0", build_attn0)
    im = []
    for c in C:
        r = r_p0.results[c]
        im.append({"qa": r["qa"], "ka": r["ka"], "vv": r["vv"], "beta": r["beta"],
                   "ident": identb, "identf": identf, "smask": smask, "cmask": cmask})
    r_a0 = run_bass_kernel_spmd(nc, im, core_ids=C)

    nc = _prog("wo_res", build_wo_res)
    zeros_bo = np.zeros((1, 2048), np.float32)
    im = []
    for c in C:
        b, th = c // 2, c % 2
        ofull = np.concatenate([
            r_a0.results[2 * b]["oo"].reshape(1024, 2048),
            r_a0.results[2 * b + 1]["oo"].reshape(1024, 2048)], axis=0)
        im.append({
            "ofm": _fm_chunks(ofull[:, th * TLOC:(th + 1) * TLOC]),
            "wo": ins["L0_attn_Wo"],
            "bo": zeros_bo,
            "xin": np.ascontiguousarray(xr[b, th]),
        })
    r_w0 = run_bass_kernel_spmd(nc, im, core_ids=C)

    w1, w2, w3 = ins["L0_ffn_w1"], ins["L0_ffn_w2"], ins["L0_ffn_w3"]
    b1, b2, b3 = ins["L0_ffn_b1"], ins["L0_ffn_b2"], ins["L0_ffn_b3"]
    ws1, ws2, ws3 = ins["L0_ffn_ws1"], ins["L0_ffn_ws2"], ins["L0_ffn_ws3"]
    bs1, bs3 = ins["L0_ffn_bs1"], ins["L0_ffn_bs3"]
    gw, gb = ins["L0_ffn_gate_w"], ins["L0_ffn_gate_b"]
    nc = _prog("moe0", build_moe0)
    im = []
    for c in C:
        b, eg = c // 2, c % 2
        es = slice(eg * 8, eg * 8 + 8)
        hs = slice(eg * 1408, (eg + 1) * 1408)
        order = list(range(eg * 8, eg * 8 + 8)) + list(range((1 - eg) * 8, (1 - eg) * 8 + 8))
        im.append({
            "xnf": xnf0[b],
            "gwt": np.ascontiguousarray(gw[order].T),
            "gb": np.ascontiguousarray(gb[order].reshape(1, 16)),
            "w1t": np.ascontiguousarray(w1[es].transpose(0, 2, 1)),
            "w3t": np.ascontiguousarray(w3[es].transpose(0, 2, 1)),
            "w2t": np.ascontiguousarray(w2[es].transpose(0, 2, 1)),
            "b1": np.ascontiguousarray(b1[es].reshape(8, 11, 128).transpose(2, 0, 1).reshape(128, 88)),
            "b3": np.ascontiguousarray(b3[es].reshape(8, 11, 128).transpose(2, 0, 1).reshape(128, 88)),
            "b2": np.ascontiguousarray(b2[es]),
            "ws1t": np.ascontiguousarray(ws1[hs].T),
            "ws3t": np.ascontiguousarray(ws3[hs].T),
            "ws2t": np.ascontiguousarray(ws2[:, hs].T),
            "bs1": np.ascontiguousarray(bs1[hs].reshape(11, 128).T),
            "bs3": np.ascontiguousarray(bs3[hs].reshape(11, 128).T),
            "ident": identb,
        })
    r_m0 = run_bass_kernel_spmd(nc, im, core_ids=C)

    nc = _prog("combine_ln", build_combine_ln)
    im = []
    for c in C:
        b, th = c // 2, c % 2
        sl = slice(th * TLOC, (th + 1) * TLOC)
        im.append({
            "xattn": r_w0.results[c]["xout"],
            "y0": np.ascontiguousarray(r_m0.results[2 * b]["yy"][sl]),
            "y1": np.ascontiguousarray(r_m0.results[2 * b + 1]["yy"][sl]),
            "bs2": np.ascontiguousarray(ins["L0_ffn_bs2"].reshape(1, 2048)),
        })
    r_c1 = run_bass_kernel_spmd(nc, im, core_ids=C)
    xn1 = [r_c1.results[c]["xn"] for c in C]
    x1 = [r_c1.results[c]["x1"] for c in C]
    xnf1 = xnf_maps(xn1)

    nc = _prog("proj1", build_proj1)
    im = []
    for c in C:
        b, hg = c // 2, c % 2
        ds = slice(hg * 1024, (hg + 1) * 1024)
        im.append({
            "xnf": xnf1[b],
            "wq": np.ascontiguousarray(ins["L1_attn_Wq"][:, ds]),
            "wk": np.ascontiguousarray(ins["L1_attn_Wk"][:, ds]),
            "wv": np.ascontiguousarray(ins["L1_attn_Wv"][:, ds]),
            "bq": np.ascontiguousarray(ins["L1_attn_bq"][ds].reshape(8, 128).T),
            "bk": np.ascontiguousarray(ins["L1_attn_bk"][ds].reshape(8, 128).T),
            "bv": np.ascontiguousarray(ins["L1_attn_bv"][ds].reshape(1, 1024)),
            "ident": identb,
        })
    r_p1 = run_bass_kernel_spmd(nc, im, core_ids=C)

    nc = _prog("attn1", build_attn1)
    im = []
    for c in C:
        r = r_p1.results[c]
        im.append({"qf": r["qf"], "kff": r["kff"], "ktm": r["ktm"], "vtm": r["vtm"],
                   "cmask": cmask})
    r_a1 = run_bass_kernel_spmd(nc, im, core_ids=C)

    nc = _prog("wo_res", build_wo_res)
    im = []
    for c in C:
        b, th = c // 2, c % 2
        ofull = np.concatenate([
            r_a1.results[2 * b]["oo"].reshape(1024, 2048),
            r_a1.results[2 * b + 1]["oo"].reshape(1024, 2048)], axis=0)
        im.append({
            "ofm": _fm_chunks(ofull[:, th * TLOC:(th + 1) * TLOC]),
            "wo": ins["L1_attn_Wo"],
            "bo": np.ascontiguousarray(ins["L1_attn_bo"].reshape(1, 2048)),
            "xin": x1[c],
        })
    r_w1 = run_bass_kernel_spmd(nc, im, core_ids=C)

    gup, gus = ins["L1_ffn_gate_up_packed"], ins["L1_ffn_gate_up_scales"]
    dnp, dns = ins["L1_ffn_down_packed"], ins["L1_ffn_down_scales"]
    nc = _prog("dequant", build_dequant)
    im = []
    for c in C:
        im.append({
            "gup": np.ascontiguousarray(gup[c * 1408:(c + 1) * 1408]),
            "gus": np.ascontiguousarray(gus[:, c * 1408:(c + 1) * 1408]),
            "dnp": np.ascontiguousarray(dnp[c * 256:(c + 1) * 256]),
            "dns": np.ascontiguousarray(dns[:, c * 256:(c + 1) * 256]),
            "identf": identf, "ident": identb,
        })
    r_dq = run_bass_kernel_spmd(nc, im, core_ids=C)
    perm_gu = np.array([8 * (i % 256) + i // 256 for i in range(2048)])
    perm_dn = np.array([8 * (i % 704) + i // 704 for i in range(5632)])
    inv_gu = np.argsort(perm_gu)
    inv_dn = np.argsort(perm_dn)
    w_guT = np.concatenate([r_dq.results[c]["gut"][inv_gu] for c in C], axis=1)
    w_dnT = np.concatenate([r_dq.results[c]["dnt"][inv_dn] for c in C], axis=1)

    nc = _prog("ffn1", build_ffn1)
    im = []
    for c in C:
        b, ih = c // 2, c % 2
        gsl = slice(ih * 2816, (ih + 1) * 2816)
        usl = slice(5632 + ih * 2816, 5632 + (ih + 1) * 2816)
        im.append({
            "xnf": xnf1[b],
            "wgT": np.ascontiguousarray(w_guT[:, gsl]),
            "wuT": np.ascontiguousarray(w_guT[:, usl]),
            "wdT": np.ascontiguousarray(w_dnT[gsl]),
        })
    r_f1 = run_bass_kernel_spmd(nc, im, core_ids=C)

    nc = _prog("final", build_final)
    im = []
    for c in C:
        b, th = c // 2, c % 2
        sl = slice(th * TLOC, (th + 1) * TLOC)
        im.append({
            "x1a": r_w1.results[c]["xout"],
            "y0": np.ascontiguousarray(r_f1.results[2 * b]["yy"][sl]),
            "y1": np.ascontiguousarray(r_f1.results[2 * b + 1]["yy"][sl]),
        })
    r_fin = run_bass_kernel_spmd(nc, im, core_ids=C)

    out = np.empty((4, 2048, 2048), np.float32)
    for c in C:
        b, th = c // 2, c % 2
        out[b, th * TLOC:(th + 1) * TLOC] = r_fin.results[c]["x2"]
    return out




kernel = kernel_fused
